# version 03
# baseline (speedup 1.0000x reference)
import numpy as np
import ml_dtypes
import concourse.bass as bass
import concourse.mybir as mybir
from concourse.alu_op_type import AluOpType as OP
from concourse.bass_utils import run_bass_kernel_spmd

F32 = mybir.dt.float32
F16 = mybir.dt.float16
I32 = mybir.dt.int32
BF16 = mybir.dt.bfloat16
AF = mybir.ActivationFunctionType
AX = mybir.AxisListType.X

NCORES = 8
D = 2048
B = 4
T = 2048
TL = 1024
NT = D // 128
H0 = 5461
H0P = 5504
NH = 16
HD = 128
I1 = 5632
GUP = 2 * I1
NEG = -1e30
BIGNEG = -3.0e38
INV2PI = float(1.0 / (2 * np.pi))
TWOPI = float(2 * np.pi)

PI_P = np.arange(D).reshape(256, 8).T.reshape(-1).copy()
PI_INV = np.argsort(PI_P)
XVP = np.concatenate([np.arange(0, D, 2), np.arange(1, D, 2)])
PII = np.arange(I1).reshape(704, 8).T.reshape(-1).copy()


def ceil_div(a, b):
    return (a + b - 1) // b


class Prog:
    def __init__(self, nc):
        self.nc = nc
        self.ops = []
        self.sems = {}
        self.counts = {}
        self.ns = ""
        self.barrier_evts = []
        self._active = {}
        self._pool = []
        self._dma_only = {}
        self.base = {}

    def sem(self, name, kind=None):
        if name in self.sems:
            if kind is not None and self._dma_only.get(name) is None:
                self._dma_only[name] = kind
            return name
        name2 = self.ns + name
        if name2 not in self.sems:
            h = None
            base = 0
            want = kind or "c"
            for i, (ph, pc, pk) in enumerate(self._pool):
                if pk == want and (want == "c" or pc % 16 == 0):
                    h, base, _ = self._pool.pop(i)
                    break
            if h is None:
                h = self.nc.semaphore(name2).__enter__()
                base = 0
            self.sems[name2] = h
            self.base[name2] = base
            self.counts[name2] = 0
            self._active[name2] = h
            self._dma_only[name2] = kind
        else:
            if kind is not None and self._dma_only.get(name2) is None:
                self._dma_only[name2] = kind
        return name2

    def retire_all(self):
        for name, h in self._active.items():
            self._pool.append((h, self.base[name] + self.counts[name],
                               self._dma_only.get(name) or "c"))
        self._active = {}

    def retire_names(self, names):
        for name in list(names):
            if name in self._active:
                h = self._active.pop(name)
                self._pool.append((h, self.base[name] + self.counts[name],
                                   self._dma_only.get(name) or "c"))

    def active_names(self):
        return set(self._active.keys())

    def events_since(self, snap):
        return [(s, v) for s, v in self.counts.items() if v > snap.get(s, 0)]

    def cnt(self, name):
        return self.counts.get(self.ns + name, self.counts.get(name, 0))

    def has(self, name):
        return (self.ns + name) in self.counts

    def op(self, engine, fn, waits=(), inc=None, inc_by=1):
        wl = [(self.sem(w[0]), w[1]) for w in waits
              if w is not None and w[1] > 0]
        wl = wl + self.barrier_evts
        incs = []
        if inc is not None:
            if inc_by == 16:
                kind = "swdge" if engine == "gpsimd" else "hwdge"
            else:
                kind = "c"
            inc = self.sem(inc, kind=kind)
            self.counts[inc] += inc_by
            incs = [(inc, inc_by)]
        self.ops.append((engine, fn, wl, incs))
        return (inc, self.counts[inc]) if inc else None

    def now(self, semname):
        semname = self.sem(semname)
        return (semname, self.counts[semname])

    def snapshot(self):
        return dict(self.counts)

    def set_barrier_from(self, snap):
        evs = []
        for s, v in self.counts.items():
            if v > snap.get(s, 0):
                evs.append((s, v))
        self.barrier_evts = evs
        self.retire_all()

    def emit(self):
        nc = self.nc
        order = ("sync", "vector", "scalar", "tensor", "gpsimd")
        with nc.Block() as block:
            for en in order:
                def body(eng, en=en):
                    last = {}
                    for (e, fn, waits, incs) in self.ops:
                        if e != en:
                            continue
                        for (s, v) in waits:
                            va = v + self.base.get(s, 0)
                            if last.get(s, 0) >= va:
                                continue
                            eng.wait_ge(self.sems[s], va)
                            last[s] = va
                        instr = fn(eng)
                        for (s, by) in incs:
                            instr.then_inc(self.sems[s], by)
                getattr(block, en)(body)


class Launch:
    def __init__(self, name):
        self.nc = bass.Bass("TRN2", target_bir_lowering=False, debug=False)
        self.p = Prog(self.nc)
        self.ins = {}
        self.outs = {}
        self._sb = {}
        self._ps = {}
        self._ctx = {}
        self._sb_stack = []
        self._ps_stack = []

    @property
    def ns(self):
        return self.p.ns

    @ns.setter
    def ns(self, v):
        self.p.ns = v

    def mark(self):
        return (len(self._sb_stack), len(self._ps_stack))

    def free_to_mark(self, mark):
        sbn, psn = mark
        while len(self._ps_stack) > psn:
            self.psum_free(self._ps_stack[-1], prefixed=True)
        while len(self._sb_stack) > sbn:
            self.sbuf_free(self._sb_stack[-1], prefixed=True)

    def din(self, name, shape, dtype):
        t = self.nc.dram_tensor(name, list(shape), dtype, kind="ExternalInput")
        self.ins[name] = t
        return t

    def dout(self, name, shape, dtype):
        t = self.nc.dram_tensor(name, list(shape), dtype, kind="ExternalOutput")
        self.outs[name] = t
        return t

    def io(self, iod, name, shape, dtype, out=False):
        if iod and name in iod:
            return iod[name]
        return (self.dout if out else self.din)(name, shape, dtype)

    def dint(self, name, shape, dtype):
        return self.nc.dram_tensor(self.ns + name, list(shape), dtype)

    def sbuf(self, name, shape, dtype):
        name = self.ns + name
        cm = self.nc.sbuf_tensor(name, list(shape), dtype)
        t = cm.__enter__()
        self._ctx[name] = cm
        self._sb[name] = t
        self._sb_stack.append(name)
        return t

    def sbuf_free_abs(self, name):
        self.sbuf_free(name, prefixed=True)

    def psum_free_abs(self, name):
        self.psum_free(name, prefixed=True)

    def sbuf_free(self, name, prefixed=False):
        if not prefixed:
            name = self.ns + name
        self._ctx.pop(name).__exit__(None, None, None)
        self._sb.pop(name)
        self._sb_stack.remove(name)

    def psum_free(self, name, prefixed=False):
        if not prefixed:
            name = self.ns + name
        self._ctx.pop("psum_" + name).__exit__(None, None, None)
        self._ps.pop(name)
        self._ps_stack.remove(name)

    def psum(self, name, shape, dtype=F32):
        name = self.ns + name
        cm = self.nc.psum_tensor(name, list(shape), dtype)
        t = cm.__enter__()
        self._ctx["psum_" + name] = cm
        self._ps[name] = t
        self._ps_stack.append(name)
        return t

    def run(self, in_maps):
        self.p.emit()
        res = run_bass_kernel_spmd(self.nc, in_maps, core_ids=list(range(NCORES)))
        return res.results


def stage_all(inputs):
    f32 = np.float32
    x = np.asarray(inputs["L0_attn_x"], f32)

    W_act = np.asarray(inputs["L0_attn_W_act"], f32)
    W_rec = np.asarray(inputs["L0_attn_W_rec"], f32)
    W_val = np.asarray(inputs["L0_attn_W_val"], f32)
    in_perm = np.concatenate([PI_P, D + PI_P])
    WactT = np.ascontiguousarray(W_act[XVP][:, in_perm].T)
    WrecT = np.ascontiguousarray(W_rec[PI_P][:, in_perm].T)
    kvmap = np.concatenate([np.arange(0, D, 2), np.arange(1, D, 2)])
    WvalT = np.ascontiguousarray(W_val[PI_P][:, kvmap].T)

    w1T = np.ascontiguousarray(np.asarray(inputs["L0_ffn_w1"], f32)[:, PI_P].T)
    w3T = np.ascontiguousarray(np.asarray(inputs["L0_ffn_w3"], f32)[:, PI_P].T)
    w2T = np.zeros((H0P, D), f32)
    w2T[:H0] = np.asarray(inputs["L0_ffn_w2"], f32)[PI_P].T

    WqT = np.ascontiguousarray(np.asarray(inputs["L1_attn_Wq"], f32)[:, PI_P].T)
    WkT = np.ascontiguousarray(np.asarray(inputs["L1_attn_Wk"], f32)[:, PI_P].T)
    WvT = np.ascontiguousarray(np.asarray(inputs["L1_attn_Wv"], f32)[:, PI_P].T)
    WoT = np.ascontiguousarray(np.asarray(inputs["L1_attn_Wo"], f32)[PI_P].T)

    gup = np.asarray(inputs["L1_ffn_gate_up_packed"], np.int32)
    gus = np.asarray(inputs["L1_ffn_gate_up_scales"])
    dnp = np.asarray(inputs["L1_ffn_down_packed"], np.int32)
    dns = np.asarray(inputs["L1_ffn_down_scales"])
    row_perm_gu = np.concatenate([PII, I1 + PII])
    gupT = np.ascontiguousarray(gup[row_perm_gu].T)
    gusE = np.ascontiguousarray(np.repeat(gus[:, row_perm_gu], 16, axis=0))
    dnpT = np.ascontiguousarray(dnp[PI_P].T)
    dnsE = np.ascontiguousarray(np.repeat(dns[:, PI_P], 16, axis=0))

    common = dict(WactT=WactT, WrecT=WrecT, WvalT=WvalT,
                  w1T=w1T, w3T=w3T, w2T=w2T,
                  WqT=WqT, WkT=WkT, WvT=WvT, WoT=WoT,
                  gupT=gupT, gusE=gusE, dnpT=dnpT, dnsE=dnsE,
                  tri01=np.triu(np.ones((128, 128), np.float32)).astype(ml_dtypes.bfloat16))

    per_core = []
    for c in range(NCORES):
        b, h = c // 2, c % 2
        t0 = h * TL
        xc = np.empty((D, TL + 1), f32)
        xc[:, 1:] = x[b, t0:t0 + TL, :][:, PI_P].T
        xc[:, 0] = x[b, t0 - 1, PI_P] if t0 > 0 else 0.0
        second = (h == 1)
        per_core.append(dict(
            xcat=xc,
            pm=np.full((128, 1), 1.0 if second else 0.0, f32),
            mbias=np.full((128, 1), 0.0 if second else BIGNEG, f32),
            negsib=np.full((128, 1), 0.0 if second else NEG, f32),
            nibmask=np.full((128, 1), np.int32(15) << np.int32(4 * c), np.int32),
            nibinv=np.full((128, 1), float(2.0 ** (-4 * c)), np.float32),
        ))
    return common, per_core


def emit_ln(L, tag, src_ap, src_evt, dst_ap, ntok=TL, rms=False, rms_eps=None,
            post_tile=None, alloc_tag=None, prior_evt=None, dst_lag=2):
    p = L.p
    at = alloc_tag or tag
    ns = L.ns
    nch = ntok // 512
    sq = L._sb.get(ns + at + "_sqbf") or L.sbuf(at + "_sqbf", [128, ntok], BF16)
    if ("psum_" + ns + at + "_pst0") in L._ctx:
        ps_st = [L._ps[ns + at + f"_pst{j}"] for j in range(nch)]
        bc_ps = [L._ps[ns + at + "_bcp0"]] * nch
    else:
        ps_st = [L.psum(at + f"_pst{j}", [64, 512]) for j in range(nch)]
        bc_ps = [L.psum(at + "_bcp0", [128, 512])] * nch
    ps_y = [t[0:1, :] for t in ps_st]
    ps_q = [t[32:33, :] for t in ps_st]
    st = L._sb.get(ns + at + "_st") or L.sbuf(at + "_st", [1, 2 * ntok], F32)
    ab = L._sb.get(ns + at + "_ab") or L.sbuf(at + "_ab", [1, 2 * ntok], F32)
    bca = L._sb.get(ns + at + "_bca") or L.sbuf(at + "_bca", [128, ntok], F32)
    bcb = None
    if not rms:
        bcb = L._sb.get(ns + at + "_bcb") or L.sbuf(at + "_bcb", [128, ntok], F32)
    ones = L._sb.get(ns + "ones_bf") or L.sbuf("ones_bf", [128, 2], BF16)
    onesr = L._sb.get(ns + "onesr_f32") or L.sbuf("onesr_f32", [2, 128], F32)
    epsc = L._sb.get(ns + tag + "_eps") or L.sbuf(tag + "_eps", [1, 1], F32)
    eps0 = rms_eps if rms else 1e-5
    if not L.p.has("ones_init"):
        p.op("vector", lambda e: e.memset(ones[:], 1.0), inc="ones_init")
        p.op("vector", lambda e: e.memset(onesr[:], 1.0), inc="ones_init")
    p.op("vector", lambda e: e.memset(epsc[:], eps0), inc="ones_init")
    ev_ones = p.now("ones_init")

    sv, ss, stm = tag + "_v", tag + "_s", tag + "_t"
    stm_base = p.cnt(stm)
    per_tile_mm = (1 if rms else 2) * nch
    for i in range(NT):
        buf = 0
        war = (stm, stm_base + i * per_tile_mm) if i >= 1 else None
        p.op("scalar", lambda e, i=i, buf=buf: e.activation(
            sq[:, buf:buf + ntok], src_ap(i), AF.Square),
            waits=[src_evt(i), war, prior_evt], inc=ss)
        ev_sq = p.now(ss)
        for j in range(nch):
            p.op("tensor", lambda e, i=i, j=j, buf=buf: e.matmul(
                ps_q[j], ones[:, 0:1], sq[:, buf + j * 512:buf + (j + 1) * 512],
                start=(i == 0), stop=(i == NT - 1)),
                waits=[ev_sq, ev_ones, prior_evt], inc=stm)
        if not rms:
            p.op("scalar", lambda e, i=i, buf=buf: e.activation(
                sq[:, buf:buf + ntok], src_ap(i), AF.Identity),
                waits=[p.now(stm), src_evt(i)], inc=ss)
            ev_y = p.now(ss)
            for j in range(nch):
                p.op("tensor", lambda e, i=i, j=j, buf=buf: e.matmul(
                    ps_y[j], ones[:, 0:1],
                    sq[:, buf + j * 512:buf + (j + 1) * 512],
                    start=(i == 0), stop=(i == NT - 1)), waits=[ev_y],
                    inc=stm)
    ev_stats = p.now(stm)
    for j in range(nch):
        p.op("vector", lambda e, j=j: e.tensor_copy(
            st[0:1, ntok + j * 512:ntok + (j + 1) * 512], ps_q[j]),
            waits=[ev_stats], inc=sv)
        if not rms:
            p.op("vector", lambda e, j=j: e.tensor_copy(
                st[0:1, j * 512:(j + 1) * 512], ps_y[j]),
                waits=[ev_stats], inc=sv)
    ev_st = p.now(sv)
    n = ntok
    if rms:
        p.op("scalar", lambda e: e.activation(ab[0:1, 0:n], st[0:1, n:2 * n],
                                              AF.Sqrt, scale=1.0 / D,
                                              bias=epsc[0:1, 0:1]),
             waits=[ev_st, ev_ones], inc=ss)
        p.op("vector", lambda e: e.reciprocal(ab[0:1, 0:n], ab[0:1, 0:n]),
             waits=[p.now(ss)], inc=sv)
    else:
        mu = bca[0:1, 0:n]
        tmp = bcb[0:1, 0:n]
        p.op("vector", lambda e: e.tensor_scalar(mu, st[0:1, 0:n], 1.0 / D,
                                                 None, OP.mult),
             waits=[ev_st], inc=sv)
        p.op("scalar", lambda e: e.activation(tmp, mu, AF.Square),
             waits=[p.now(sv)], inc=ss)
        p.op("vector", lambda e: e.scalar_tensor_tensor(
            tmp, st[0:1, n:2 * n], 1.0 / D, tmp, OP.mult, OP.subtract),
            waits=[p.now(ss), ev_st], inc=sv)
        p.op("scalar", lambda e: e.activation(tmp, tmp, AF.Sqrt,
                                              bias=epsc[0:1, 0:1]),
             waits=[p.now(sv), ev_ones], inc=ss)
        p.op("vector", lambda e: e.reciprocal(ab[0:1, 0:n], tmp),
             waits=[p.now(ss)], inc=sv)
        p.op("vector", lambda e: e.scalar_tensor_tensor(
            ab[0:1, n:2 * n], mu, -1.0, ab[0:1, 0:n],
            OP.mult, OP.mult),
            waits=[p.now(sv)], inc=sv)
    ev_ab = [p.now(sv), p.now(ss)]
    for j in range(nch):
        p.op("tensor", lambda e, j=j: e.matmul(
            bc_ps[j][:, :], onesr[0:1, :], ab[0:1, j * 512:(j + 1) * 512],
            start=True, stop=True),
            waits=ev_ab + [ev_stats, prior_evt, p.now(sv)], inc=stm)
        p.op("vector", lambda e, j=j: e.tensor_copy(bca[:, j * 512:(j + 1) * 512],
                                                    bc_ps[j][:, :]),
             waits=[p.now(stm)], inc=sv)
        if not rms:
            p.op("tensor", lambda e, j=j: e.matmul(
                bc_ps[j][:, :], onesr[0:1, :], ab[0:1, n + j * 512:n + (j + 1) * 512],
                start=True, stop=True), waits=ev_ab + [p.now(sv)], inc=stm)
            p.op("vector", lambda e, j=j: e.tensor_copy(bcb[:, j * 512:(j + 1) * 512],
                                                        bc_ps[j][:, :]),
                 waits=[p.now(stm)], inc=sv)
    ev_bc = p.now(sv)
    done = []
    post_evts = {}
    for i in range(NT):
        war = post_evts.get(i - dst_lag)
        p.op("vector", lambda e, i=i: e.tensor_mul(dst_ap(i), src_ap(i), bca[:]),
             waits=[ev_bc, src_evt(i), war], inc=sv)
        if not rms:
            p.op("vector", lambda e, i=i: e.tensor_add(dst_ap(i), dst_ap(i), bcb[:]),
                 waits=[p.now(sv)], inc=sv)
        done.append(p.now(sv))
        if post_tile is not None:
            post_evts[i] = post_tile(i, done[-1])
    return done


class WStream:

    def __init__(self, L, tag, max_k, strip_cols=128, need_f32=True,
                 f32_bufs=2):
        self.L, self.p, self.tag = L, L.p, tag
        self.maxw = max_k * strip_cols
        self.f32_bufs = f32_bufs
        self.wf = (L.sbuf(tag + "_wf", [128, f32_bufs * self.maxw], F32)
                   if need_f32 else None)
        self.wb = L.sbuf(tag + "_wb", [128, 2 * self.maxw], BF16)
        self.pss = [L.psum(tag + f"_ps{i}", [128, 512]) for i in range(4)]
        self.sd = [tag + "_d0", tag + "_d1"]
        self.sv, self.stm = tag + "_v", tag + "_m"
        self.mi = 0
        self.dmas = [0, 0]
        self.casts = 0
        self.mms = {}
        self.evac_evts = {}

    def job(self, wdram, Ktiles, m_list, rhs_ap, rhs_evt, evac_fn,
            wdtype=F32, strip_cols=128, queue="sync", extra_strip_wait=None,
            k_off=0, k_count=None):
        p, L = self.p, self.L
        nK = Ktiles if k_count is None else k_count
        needs_cast = (wdtype == F32)
        nch = TL // 512
        for m in m_list:
            mi = self.mi
            self.mi += 1
            sl = (mi % 2) * self.maxw
            mw = min(strip_cols, wdram.shape[1] - m * strip_cols)
            src = wdram.ap()[k_off * 128:(k_off + nK) * 128,
                             m * strip_cols:m * strip_cols + mw]
            src = src.rearrange("(k p) f -> p k f", p=128)
            par = mi % 2
            if needs_cast:
                fpar = mi % self.f32_bufs
                slf = fpar * self.maxw
                nb = self.f32_bufs
                war = [(self.sv, self.casts - (nb - 1))] if mi >= nb else []
                p.op(queue, lambda e, src=src, dst=self.wf[:, slf:slf + nK * mw]
                     .rearrange("p (k f) -> p k f", f=mw):
                     e.dma_start(out=dst, in_=src),
                     waits=war + ([extra_strip_wait] if extra_strip_wait else []),
                     inc=self.sd[par], inc_by=16)
                self.dmas[par] += 1
                p.op("vector", lambda e, sl=sl, slf=slf, mw=mw, nK=nK:
                     e.tensor_copy(self.wb[:, sl:sl + nK * mw],
                                   self.wf[:, slf:slf + nK * mw]),
                     waits=[(self.sd[par], 16 * self.dmas[par]),
                            (self.stm, self.mms.get(mi - 2, 0))],
                     inc=self.sv)
                self.casts += 1
                ev_w = (self.sv, self.casts)
            else:
                war = [(self.stm, self.mms.get(mi - 2, 0))]
                p.op(queue, lambda e, src=src, dst=self.wb[:, sl:sl + nK * mw]
                     .rearrange("p (k f) -> p k f", f=mw):
                     e.dma_start(out=dst, in_=src),
                     waits=war + ([extra_strip_wait] if extra_strip_wait else []),
                     inc=self.sd[par], inc_by=16)
                self.dmas[par] += 1
                ev_w = (self.sd[par], 16 * self.dmas[par])
            ps_pair = (self.pss[0], self.pss[1]) if mi % 2 == 0 else \
                      (self.pss[2], self.pss[3])
            ev_prev_evac = self.evac_evts.get(mi - 2)
            for j in range(nch):
                for k in range(nK):
                    p.op("tensor", lambda e, sl=sl, mw=mw, j=j, k=k, ps=ps_pair[j]:
                         e.matmul(ps[0:mw, :],
                                  self.wb[:, sl + k * mw:sl + (k + 1) * mw],
                                  rhs_ap(k, j * 512, (j + 1) * 512),
                                  start=(k == 0), stop=(k == nK - 1)),
                         waits=[ev_w, rhs_evt, ev_prev_evac] if k == 0
                         else [ev_w, rhs_evt],
                         inc=self.stm)
            self.mms[mi] = p.cnt(self.stm)
            ev_mm = (self.stm, self.mms[mi])
            self.evac_evts[mi] = evac_fn(m, mi, ps_pair, ev_mm, mw)
        return self.evac_evts

    def free(self):
        for i in (3, 2, 1, 0):
            self.L.psum_free(self.tag + f"_ps{i}")
        self.L.sbuf_free(self.tag + "_wb")
        if self.wf is not None:
            self.L.sbuf_free(self.tag + "_wf")


def build_l0a(L=None, io=None, part2=None):
    L = L or Launch("l0a")
    p = L.p
    xcat = L.io(io, "xcat", [D, TL + 1], F32)
    WactT = L.io(io, "WactT", [2 * D, D], F32)
    WrecT = L.io(io, "WrecT", [2 * D, D], F32)
    xv_o = L.io(io, "xv", [D, TL], F32, out=True)
    r4_o = L.io(io, "r4", [D, TL], BF16, out=True)

    xs = L.sbuf("xs", [128, 2 * (TL + 1)], F32)
    xb = L.sbuf("xb", [128, NT * (TL + 1)], BF16)
    ev1 = L.sbuf("ev1", [128, 1024], F32)
    ev2 = L.sbuf("ev2", [128, 1024], F32)
    dst = L.sbuf("dst", [128, 2 * TL], F32)
    dstr = L.sbuf("dstr", [128, 2 * TL], BF16)

    for i in range(NT):
        par = i % 2
        p.op("sync", lambda e, i=i, par=par: e.dma_start(
            out=xs[:, par * (TL + 1):(par + 1) * (TL + 1)],
            in_=xcat.ap()[i * 128:(i + 1) * 128, :]),
            waits=([("xc", i - 1)] if i >= 2 else []),
            inc=f"xd{par}", inc_by=16)
        p.op("vector", lambda e, i=i, par=par: e.tensor_copy(
            xb[:, i * (TL + 1):(i + 1) * (TL + 1)],
            xs[:, par * (TL + 1):(par + 1) * (TL + 1)]),
            waits=[p.now(f"xd{par}")], inc="xc")
    ev_x = p.now("xc")

    def rhs_ap(k, c0, c1):
        base = (k % NT) * (TL + 1) + (0 if k < NT else 1)
        return xb[:, base + c0:base + c1]

    def mk_evac(which):
        def evac(m, mi, ps_pair, ev_mm, mw):
            evs = []
            for j in range(2):
                sl = (m % 2) * TL + j * 512
                buf = ev1 if j == 0 else ev2
                par = m % 2
                if which == "xv":
                    p.op("vector", lambda e, ps=ps_pair[j], buf=buf:
                         e.tensor_scalar(buf[:, 0:512], ps[:], 0.0, None, OP.max),
                         waits=[ev_mm, p.now("xvs")], inc="xvev")
                    p.op("scalar", lambda e, sl=sl, buf=buf: e.activation(
                        dst[:, sl:sl + 512], buf[:, 0:512], AF.Sigmoid),
                        waits=[p.now("xvev"),
                               (f"xvod{par}", 16 * (m // 2)) if m >= 2 else None],
                        inc="xvs")
                    evs.append(p.now("xvs"))
                else:
                    p.op("vector", lambda e, ps=ps_pair[j], buf=buf:
                         e.tensor_scalar(buf[:, 0:512], ps[:], 0.0, None, OP.max),
                         waits=[ev_mm, p.now("r4ev")], inc="r4ev")
                    p.op("scalar", lambda e, buf=buf: e.activation(
                        buf[:, 512:1024], buf[:, 0:512], AF.Sigmoid),
                        waits=[p.now("r4ev")], inc="r4s")
                    p.op("scalar", lambda e, buf=buf: e.activation(
                        buf[:, 0:512], buf[:, 512:1024], AF.Square),
                        waits=[p.now("r4s")], inc="r4s")
                    p.op("vector", lambda e, sl=sl, buf=buf: e.tensor_mul(
                        dstr[:, sl:sl + 512], buf[:, 0:512], buf[:, 0:512]),
                        waits=[p.now("r4s"),
                               (f"r4od{par}", 16 * (m // 2)) if m >= 2 else None],
                        inc="r4ev")
                    evs.append(p.now("r4ev"))
            if which == "xv":
                p.op("gpsimd", lambda e, m=m: e.dma_start(
                    out=xv_o.ap()[m * 128:(m + 1) * 128, :],
                    in_=dst[:, (m % 2) * TL:(m % 2) * TL + TL]),
                    waits=evs, inc=f"xvod{par}", inc_by=16)
                return p.now(f"xvod{par}")
            else:
                p.op("gpsimd", lambda e, m=m: e.dma_start(
                    out=r4_o.ap()[m * 128:(m + 1) * 128, :],
                    in_=dstr[:, (m % 2) * TL:(m % 2) * TL + TL]),
                    waits=evs, inc=f"r4od{par}", inc_by=16)
                return p.now(f"r4od{par}")
        return evac

    ws = WStream(L, "w", 2 * NT, f32_bufs=1)
    ws.job(WactT, 2 * NT, list(range(NT)), rhs_ap, ev_x, mk_evac("xv"))
    if part2 is None:
        ws.job(WrecT, 2 * NT, list(range(NT)), rhs_ap, ev_x, mk_evac("r4"))
        p.op("gpsimd", lambda e: e.engine_nop(),
             waits=[p.now("xvod0"), p.now("xvod1"), p.now("r4od0"),
                    p.now("r4od1")], inc="fin")
        return L
    def do_part2():
        ws.job(WrecT, 2 * NT, list(range(NT)), rhs_ap, ev_x, mk_evac("r4"))
        p.op("gpsimd", lambda e: e.engine_nop(),
             waits=[p.now("xvod0"), p.now("xvod1"), p.now("r4od0"),
                    p.now("r4od1")], inc="fin")
    part2.append(do_part2)
    return L


def build_l0s(L=None, io=None):
    L = L or Launch("l0s")
    p = L.p
    xv_i = L.io(io, "xv", [D, TL], F32)
    pm_i = L.io(io, "pm", [128, 1], F32)
    mb_i = L.io(io, "mbias", [128, 1], F32)
    kv_o = L.io(io, "kv", [D, TL], BF16, out=True)

    snd = L.dint("snd", [128, 24], F32)
    agg = L.dint("agg", [256, 24], F32)

    sb = L.sbuf("sb", [128, NT * TL], F32)
    A = L.sbuf("A", [128, TL], F32)
    Bu = L.sbuf("Bu", [128, TL], F32)
    C = L.sbuf("C", [128, TL], F32)
    Dd = L.sbuf("Dd", [128, TL], F32)
    Ii = L.sbuf("Ii", [128, TL], I32)
    kv = L.sbuf("kv_sb", [128, 2 * TL], BF16)
    sendb = L.sbuf("sendb", [128, 24], F32)
    sibb = L.sbuf("sibb", [128, 24], F32)
    inib = L.sbuf("inib", [128, 24], F32)
    pm = L.sbuf("pm_sb", [128, 1], F32)
    mb = L.sbuf("mb_sb", [128, 1], F32)

    p.op("sync", lambda e: e.dma_start(
        out=sb[:].rearrange("p (i f) -> p i f", f=TL),
        in_=xv_i.ap().rearrange("(i p) f -> p i f", p=128)), inc="xd", inc_by=16)
    p.op("sync", lambda e: e.dma_start(out=pm[:], in_=pm_i.ap()), inc="xd", inc_by=16)
    p.op("sync", lambda e: e.dma_start(out=mb[:], in_=mb_i.ap()), inc="xd", inc_by=16)
    ev_x = p.now("xd")

    def sl(i):
        return sb[:, i * TL:(i + 1) * TL]

    sv, ss = "sv", "ss"
    Ee = L.sbuf("Ee", [128, TL], F32)
    for t in range(8):
        re, im = sl(t), sl(8 + t)
        p.op("scalar", lambda e, re=re: e.activation(A[:], re, AF.Square),
             waits=[ev_x, p.now(sv)], inc=ss)
        p.op("scalar", lambda e, im=im: e.activation(Bu[:], im, AF.Square),
             waits=[p.now(ss)], inc=ss)
        p.op("vector", lambda e: e.tensor_add(A[:], A[:], Bu[:]),
             waits=[p.now(ss)], inc=sv)
        p.op("scalar", lambda e: e.activation(Ee[:], A[:], AF.Sqrt),
             waits=[p.now(sv)], inc=ss)
        p.op("vector", lambda e, re=re: e.tensor_add(Ee[:], Ee[:], re),
             waits=[p.now(ss)], inc=sv)
        p.op("scalar", lambda e: e.activation(C[:], Ee[:], AF.Ln),
             waits=[p.now(sv)], inc=ss)
        p.op("scalar", lambda e: e.activation(C[:], C[:], AF.Exp, scale=-1.0),
             waits=[p.now(ss)], inc=ss)
        p.op("scalar", lambda e, re=re: e.activation(re, A[:], AF.Ln),
             waits=[p.now(ss), p.now(sv)], inc=ss)
        ev_u2 = p.now(ss)
        p.op("vector", lambda e, im=im: e.tensor_mul(C[:], im, C[:]),
             waits=[ev_u2], inc=sv)
        p.op("scalar", lambda e: e.activation(C[:], C[:], AF.Arctan),
             waits=[p.now(sv)], inc=ss)
        p.op("vector", lambda e, im=im: e.tensor_tensor_scan(
            im, C[:], C[:], 0.0, OP.add, OP.bypass),
            waits=[p.now(ss)], inc=sv)
        p.op("vector", lambda e, re=re: e.tensor_tensor_scan(
            Bu[:], re, re, BIGNEG, OP.max, OP.max),
            waits=[p.now(sv), ev_u2], inc=sv)
        p.op("vector", lambda e, t=t: e.tensor_copy(sendb[:, t:t + 1],
                                                    Bu[:, TL - 1:TL]),
             waits=[p.now(sv)], inc=sv)
        p.op("vector", lambda e, re=re: e.tensor_sub(A[:], re, Bu[:]),
             waits=[p.now(sv)], inc=sv)
        p.op("vector", lambda e, t=t: e.reduce_sum(sendb[:, 8 + t:9 + t], A[:], axis=AX),
             waits=[p.now(sv)], inc=sv)
        p.op("vector", lambda e, t=t, im=im: e.tensor_copy(sendb[:, 16 + t:17 + t],
                                                           im[:, TL - 1:TL]),
             waits=[p.now(sv)], inc=sv)
    ev_p1 = p.now(sv)
    p.op("gpsimd", lambda e: e.dma_start(out=snd.ap(), in_=sendb[:]),
         waits=[ev_p1], inc="cd", inc_by=16)
    p.op("gpsimd", lambda e: e.collective_compute(
        "AllGather", OP.bypass,
        replica_groups=[[2 * i, 2 * i + 1] for i in range(4)],
        ins=[snd.ap().opt()], outs=[agg.ap().opt()]),
        waits=[p.now("cd")], inc="cc")
    p.op("gpsimd", lambda e: e.dma_start(out=sibb[:], in_=agg.ap()[0:128, :]),
         waits=[p.now("cc")], inc="cd", inc_by=16)
    ev_sib = p.now("cd")
    p.op("vector", lambda e: e.tensor_scalar(inib[:, 0:8], sibb[:, 0:8],
                                             pm[:, 0:1], None, OP.mult),
         waits=[ev_sib, ev_x], inc=sv)
    p.op("vector", lambda e: e.tensor_scalar(inib[:, 0:8], inib[:, 0:8],
                                             mb[:, 0:1], None, OP.add),
         waits=[p.now(sv)], inc=sv)
    p.op("vector", lambda e: e.tensor_scalar(inib[:, 8:24], sibb[:, 8:24],
                                             pm[:, 0:1], None, OP.mult),
         waits=[p.now(sv), ev_sib], inc=sv)
    ev_ini = p.now(sv)
    for t in range(8):
        u2, Th = sl(t), sl(8 + t)
        p.op("vector", lambda e, u2=u2, t=t: e.tensor_tensor_scan(
            A[:], u2, u2, inib[:, t:t + 1], OP.max, OP.max),
            waits=[ev_ini, p.now(ss)], inc=sv)
        p.op("vector", lambda e, u2=u2: e.tensor_sub(Bu[:], u2, A[:]),
             waits=[p.now(sv)], inc=sv)
        p.op("vector", lambda e, t=t: e.tensor_tensor_scan(
            A[:], Bu[:], Bu[:], inib[:, 8 + t:9 + t], OP.add, OP.bypass),
            waits=[p.now(sv)], inc=sv)
        p.op("scalar", lambda e: e.activation(Bu[:], A[:], AF.Exp, scale=0.5),
             waits=[p.now(sv)], inc=ss)
        p.op("vector", lambda e, Th=Th, t=t: e.tensor_scalar(
            C[:], Th, inib[:, 16 + t:17 + t], 2.0, OP.add, OP.mult),
            waits=[p.now(ss), ev_ini], inc=sv)
        def rr_sin(theta_ap, out_tile_off):
            p.op("vector", lambda e: e.tensor_scalar(A[:], theta_ap, INV2PI, None,
                                                     OP.mult),
                 waits=[p.now(sv)], inc=sv)
            p.op("vector", lambda e: e.tensor_copy(Ii[:], A[:]),
                 waits=[p.now(sv)], inc=sv)
            p.op("vector", lambda e: e.tensor_copy(A[:], Ii[:]),
                 waits=[p.now(sv)], inc=sv)
            p.op("vector", lambda e: e.scalar_tensor_tensor(
                A[:], A[:], -TWOPI, theta_ap, OP.mult, OP.add),
                waits=[p.now(sv)], inc=sv)
            p.op("vector", lambda e: e.tensor_scalar(Dd[:], A[:], float(np.pi),
                                                     None, OP.is_gt),
                 waits=[p.now(sv)], inc=sv)
            p.op("vector", lambda e: e.tensor_scalar(Ee[:], A[:], float(-np.pi),
                                                     None, OP.is_lt),
                 waits=[p.now(sv)], inc=sv)
            p.op("vector", lambda e: e.tensor_sub(Dd[:], Dd[:], Ee[:]),
                 waits=[p.now(sv)], inc=sv)
            p.op("vector", lambda e: e.scalar_tensor_tensor(
                A[:], Dd[:], -TWOPI, A[:], OP.mult, OP.add),
                waits=[p.now(sv)], inc=sv)
            p.op("scalar", lambda e: e.activation(Dd[:], A[:], AF.Sin),
                 waits=[p.now(sv)], inc=ss)
            p.op("vector", lambda e, o=out_tile_off: e.tensor_mul(
                kv[:, o:o + TL], Bu[:], Dd[:]), waits=[p.now(ss)], inc=sv)
        kwar = ("kod", 16 * 2 * t) if t >= 1 else None
        p.op("vector", lambda e: e.engine_nop(), waits=[kwar], inc=sv)
        rr_sin(C[:], TL)
        p.op("vector", lambda e: e.tensor_scalar(C[:], C[:], float(np.pi / 2), None,
                                                 OP.add),
             waits=[p.now(sv)], inc=sv)
        rr_sin(C[:], 0)
        ev_t = p.now(sv)
        p.op("gpsimd", lambda e, t=t: e.dma_start(
            out=kv_o.ap()[t * 128:(t + 1) * 128, :],
            in_=kv[:, 0:TL]),
            waits=[ev_t], inc="kod", inc_by=16)
        p.op("gpsimd", lambda e, t=t: e.dma_start(
            out=kv_o.ap()[(8 + t) * 128:(9 + t) * 128, :],
            in_=kv[:, TL:2 * TL]),
            waits=[ev_t], inc="kod", inc_by=16)
    p.op("gpsimd", lambda e: e.engine_nop(), waits=[p.now("kod")], inc="fin")
    return L


def build_l0v(L=None, io=None):
    L = L or Launch("l0v")
    p = L.p
    kv_i = L.io(io, "kv", [D, TL], BF16)
    r4_i = L.io(io, "r4", [D, TL], BF16)
    x_i = L.io(io, "xcat", [D, TL + 1], F32)
    WvalT = L.io(io, "WvalT", [D, D], F32)
    x1_o = L.io(io, "x1", [D, TL], F32, out=True)

    kv = L.sbuf("kv_sb", [128, NT * TL], BF16)
    r4 = L.sbuf("r4_sb", [128, 2 * TL], BF16)
    xstr = L.sbuf("xstr", [128, 2 * TL], F32)
    ysb = L.sbuf("ysb", [128, NT * TL], F32)
    dst = L.sbuf("dst", [128, 2 * TL], F32)

    p.op("sync", lambda e: e.dma_start(
        out=kv[:].rearrange("p (i f) -> p i f", f=TL),
        in_=kv_i.ap().rearrange("(i p) f -> p i f", p=128)), inc="xd", inc_by=16)
    ev_x = p.now("xd")
    xld = [0, 0]
    ev_xt = {}

    def load_x(m, war_evt):
        par = m % 2
        p.op("scalar", lambda e, m=m: e.dma_start(
            out=xstr[:, par * TL:(par + 1) * TL],
            in_=x_i.ap()[m * 128:(m + 1) * 128, 1:]),
            waits=[war_evt], inc=f"xld{par}", inc_by=16)
        p.op("scalar", lambda e, m=m: e.dma_start(
            out=r4[:, par * TL:(par + 1) * TL],
            in_=r4_i.ap()[m * 128:(m + 1) * 128, :]),
            waits=[war_evt], inc=f"xld{par}", inc_by=16)
        xld[par] += 2
        ev_xt[m] = (f"xld{par}", 16 * xld[par])

    def rhs_ap(k, c0, c1):
        return kv[:, k * TL + c0:k * TL + c1]

    ev_y = {}

    def evac(m, mi, ps_pair, ev_mm, mw):
        if m not in ev_xt:
            load_x(m, ev_y.get(m - 2))
        evs = []
        par = m % 2
        for j in range(2):
            p.op("vector", lambda e, m=m, j=j, par=par, ps=ps_pair[j]:
                 e.scalar_tensor_tensor(
                     ysb[:, m * TL + j * 512:m * TL + (j + 1) * 512],
                     ps[:], 1.0, r4[:, par * TL + j * 512:par * TL + (j + 1) * 512],
                     OP.mult, OP.mult),
                 waits=[ev_mm, ev_x, ev_xt[m]], inc="yev")
            p.op("vector", lambda e, m=m, j=j, par=par: e.tensor_add(
                ysb[:, m * TL + j * 512:m * TL + (j + 1) * 512],
                ysb[:, m * TL + j * 512:m * TL + (j + 1) * 512],
                xstr[:, par * TL + j * 512:par * TL + (j + 1) * 512]),
                waits=[p.now("yev"), ev_xt[m]], inc="yev")
            evs.append(p.now("yev"))
        ev_y[m] = evs[-1]
        return evs[-1]

    ws = WStream(L, "wv", NT)
    ws.job(WvalT, NT, list(range(NT)), rhs_ap, ev_x, evac)

    def post_tile(i, done_evt):
        par = i % 2
        p.op("gpsimd", lambda e, i=i: e.dma_start(
            out=x1_o.ap()[i * 128:(i + 1) * 128, :],
            in_=dst[:, (i % 2) * TL:(i % 2) * TL + TL]),
            waits=[done_evt], inc=f"od{par}", inc_by=16)
        return p.now(f"od{par}")

    emit_ln(L, "ln", lambda i: ysb[:, i * TL:(i + 1) * TL],
            lambda i: ev_y[i],
            lambda i: dst[:, (i % 2) * TL:(i % 2) * TL + TL],
            post_tile=post_tile)
    p.op("gpsimd", lambda e: e.engine_nop(),
         waits=[p.now("od0"), p.now("od1")], inc="fin")
    return L


def build_ffn0(L=None, io=None, pre_out=None, pre=None):
    L = L or Launch("ffn0")
    p = L.p
    x1_i = L.io(io, "x1", [D, TL], F32)
    w1sh = L.io(io, "w1sh", [256, H0], F32)
    w3sh = L.io(io, "w3sh", [256, H0], F32)
    w2sh = L.io(io, "w2sh", [688, D], F32)
    x2_o = L.io(io, "x2", [D, TL], F32, out=True)

    q1sh_d = L.dint("q1sh_d", [256, H0], BF16)
    q3sh_d = L.dint("q3sh_d", [256, H0], BF16)
    q2sh_d = L.dint("q2sh_d", [688, D], BF16)
    q1_d = L.dint("q1_d", [D, H0], BF16)
    q3_d = L.dint("q3_d", [D, H0], BF16)
    q2_d = L.dint("q2_d", [H0P, D], BF16)
    pp_d = L.dint("pp_d", [128, 4], F32)
    ssum_d = L.dint("ssum_d", [1, 4], F32)
    sred_d = L.dint("sred_d", [1, 4], F32)

    if pre is not None:
        xbf = L.sbuf("xbf", [128, NT * TL], BF16)
        ev_ag = pre["ev_ag"]
        q1_d, q3_d, q2_d = pre["q1_d"], pre["q3_d"], pre["q2_d"]
        ev_q_all = list(pre["ev_q"])
        pf = L.sbuf("pfm", [128, 2 * TL], F32)
        for i in range(NT):
            par = i % 2
            p.op("scalar", lambda e, i=i, par=par: e.dma_start(
                out=pf[:, par * TL:(par + 1) * TL],
                in_=x1_i.ap()[i * 128:(i + 1) * 128, :]),
                waits=([("xc", i - 1)] if i >= 2 else []),
                inc=f"xd{par}", inc_by=16)
            p.op("vector", lambda e, i=i, par=par: e.tensor_copy(
                xbf[:, i * TL:(i + 1) * TL], pf[:, par * TL:(par + 1) * TL]),
                waits=[p.now(f"xd{par}")], inc="xc")
        ev_xbf = p.now("xc")
        L.sbuf_free("pfm")
        return _ffn0_main(L, io, x1_i, x2_o, xbf, ev_xbf, ev_ag,
                          q1_d, q3_d, q2_d)
    if pre_out is None:
        xbf = L.sbuf("xbf", [128, NT * TL], BF16)
    acc = L.sbuf("acc", [128, 8], F32)
    red1 = L.sbuf("red1", [128, 64], F32)
    srow = L.sbuf("srow", [1, 1024], F32)
    scb = L.sbuf("scb", [128, 16], F32)
    onesq = L.sbuf("onesq", [2, 128], F32)
    PFW = (H0 // 4) + 3
    pf = L.sbuf("pf", [128, 2 * PFW], F32)
    qb = L.sbuf("qb", [128, 2 * PFW], BF16)
    ps_b = L.psum("ps_b", [128, 16])
    pf_name, qb_name, psb_name = L.ns + "pf", L.ns + "qb", L.ns + "ps_b"

    p.op("vector", lambda e: e.memset(onesq[:], 1.0), inc="ini")
    p.op("vector", lambda e: e.memset(acc[:], 0.0), inc="ini")
    p.op("vector", lambda e: e.memset(srow[0:1, 528:544], 0.0), inc="ini")
    ev_ini = p.now("ini")


    jobs = []
    for (wsh_, qsh_, mat_, rows_, colw_) in ((w1sh, q1sh_d, 0, 256, H0),
                                             (w3sh, q3sh_d, 1, 256, H0),
                                             (w2sh, q2sh_d, 2, 688, D)):
        q4 = colw_ // 4
        csp = [(0, q4), (q4, q4), (2 * q4, q4), (3 * q4, colw_ - 3 * q4)]
        for r0_ in range(0, rows_, 128):
            nr_ = min(128, rows_ - r0_)
            for (c0_, cw_) in csp:
                jobs.append((wsh_, qsh_, r0_, nr_, c0_, cw_, mat_))
    pa = 0
    for (wsh_, qsh_, r0, nr, c0, cw, mat) in jobs:
        par = pa % 2
        sl = par * PFW
        p.op("sync", lambda e, wsh=wsh_, r0=r0, nr=nr, c0=c0, cw=cw, sl=sl:
             e.dma_start(out=pf[0:nr, sl:sl + cw],
                         in_=wsh.ap()[r0:r0 + nr, c0:c0 + cw]),
             waits=[("pav", 2 * (pa - 1)) if pa >= 2 else None],
             inc=f"pad{par}", inc_by=16)
        nd = p.cnt(f"pad{par}")
        p.op("vector", lambda e, nr=nr, sl=sl, cw=cw:
             e.reduce_sum(red1[0:nr, 63:64], pf[0:nr, sl:sl + cw],
                          axis=AX, apply_absolute_value=True),
             waits=[(f"pad{par}", nd), p.now("pav"), ev_ini], inc="pav")
        p.op("vector", lambda e, nr=nr, mat=mat: e.tensor_add(
            acc[0:nr, mat:mat + 1], acc[0:nr, mat:mat + 1], red1[0:nr, 63:64]),
            waits=[p.now("pav")], inc="pav")
        pa += 1
    ev_acc = p.now("pav")
    p.op("gpsimd", lambda e: e.dma_start(out=pp_d.ap()[:, 0:4], in_=acc[:, 0:4]),
         waits=[ev_acc], inc="ppd", inc_by=16)
    p.op("gpsimd", lambda e: e.dma_start(
        out=srow[0:1, 0:512].rearrange("q (c p) -> q c p", p=128),
        in_=pp_d.ap().rearrange("p c -> () c p"),
        allow_slow_non_contiguous=True),
        waits=[p.now("ppd")], inc="ppe", inc_by=16)
    p.op("vector", lambda e: e.tensor_reduce(
        srow[0:1, 512:516], srow[0:1, 0:512].rearrange("q (c p) -> q c p", p=128),
        axis=AX, op=OP.add),
        waits=[p.now("ppe")], inc="pav")
    p.op("gpsimd", lambda e: e.dma_start(out=ssum_d.ap(), in_=srow[0:1, 512:516]),
         waits=[p.now("pav")], inc="ppf", inc_by=16)
    p.op("gpsimd", lambda e: e.collective_compute(
        "AllReduce", OP.add, replica_groups=[list(range(NCORES))],
        ins=[ssum_d.ap().opt()], outs=[sred_d.ap().opt()]),
        waits=[p.now("ppf")], inc="cc")
    p.op("gpsimd", lambda e: e.dma_start(out=srow[0:1, 520:524], in_=sred_d.ap()),
         waits=[p.now("cc")], inc="ppg", inc_by=16)
    ev_tot = p.now("ppg")
    CN1 = float(H0 * D)
    for mat, cnt in ((0, CN1), (1, CN1), (2, CN1)):
        p.op("vector", lambda e, mat=mat, cnt=cnt: e.tensor_scalar(
            srow[0:1, 528 + mat:529 + mat], srow[0:1, 520 + mat:521 + mat],
            0.5 / cnt, None, OP.mult), waits=[ev_tot, ev_ini, p.now("pav")], inc="pav")
        p.op("vector", lambda e, mat=mat, cnt=cnt: e.tensor_scalar(
            srow[0:1, 531 + mat:532 + mat], srow[0:1, 520 + mat:521 + mat],
            1.0 / cnt, None, OP.mult), waits=[ev_tot, ev_ini, p.now("pav")], inc="pav")
        p.op("vector", lambda e, mat=mat, cnt=cnt: e.tensor_scalar(
            srow[0:1, 534 + mat:535 + mat], srow[0:1, 520 + mat:521 + mat],
            -0.5 / cnt, None, OP.mult), waits=[ev_tot, ev_ini, p.now("pav")], inc="pav")
    ev_svals = p.now("pav")
    p.op("tensor", lambda e: e.matmul(ps_b[:, 0:16], onesq[0:1, :],
                                      srow[0:1, 528:544], start=True, stop=True),
         waits=[ev_svals, ev_ini], inc="pbm")
    p.op("vector", lambda e: e.tensor_copy(scb[:], ps_b[:, 0:16]),
         waits=[p.now("pbm")], inc="pav")
    ev_scb = p.now("pav")

    pb = 0
    qd_evts = {}
    for (wsh_, qsh_, r0, nr, c0, cw, mat) in jobs:
        par = pb % 2
        sl = par * PFW
        p.op("scalar", lambda e, wsh=wsh_, r0=r0, nr=nr, c0=c0, cw=cw, sl=sl:
             e.dma_start(out=pf[0:nr, sl:sl + cw],
                         in_=wsh.ap()[r0:r0 + nr, c0:c0 + cw]),
             waits=[("qav", 3 * (pb - 1)) if pb >= 2 else None,
                    ev_acc],
             inc=f"qbd{par}", inc_by=16)
        nd = p.cnt(f"qbd{par}")
        p.op("vector", lambda e, nr=nr, sl=sl, cw=cw, mat=mat:
             e.tensor_scalar(qb[0:nr, sl:sl + cw], pf[0:nr, sl:sl + cw],
                             scb[0:nr, 0 + mat:1 + mat],
                             scb[0:nr, 3 + mat:4 + mat],
                             OP.is_gt, OP.mult),
             waits=[(f"qbd{par}", nd), ev_scb, p.now("qav"),
                    qd_evts.get(pb - 2)], inc="qav")
        p.op("vector", lambda e, nr=nr, sl=sl, cw=cw, mat=mat:
             e.tensor_scalar(pf[0:nr, sl:sl + cw], pf[0:nr, sl:sl + cw],
                             scb[0:nr, 6 + mat:7 + mat],
                             scb[0:nr, 3 + mat:4 + mat],
                             OP.is_lt, OP.mult),
             waits=[p.now("qav")], inc="qav")
        p.op("vector", lambda e, nr=nr, sl=sl, cw=cw:
             e.tensor_sub(qb[0:nr, sl:sl + cw], qb[0:nr, sl:sl + cw],
                          pf[0:nr, sl:sl + cw]),
             waits=[p.now("qav")], inc="qav")
        p.op("gpsimd", lambda e, qsh=qsh_, r0=r0, nr=nr, c0=c0, cw=cw, sl=sl:
             e.dma_start(out=qsh.ap()[r0:r0 + nr, c0:c0 + cw],
                         in_=qb[0:nr, sl:sl + cw]),
             waits=[p.now("qav")], inc=f"qod{par}", inc_by=16)
        qd_evts[pb] = p.now(f"qod{par}")
        pb += 1
    ev_qout = [p.now("qod0"), p.now("qod1")]
    p.op("gpsimd", lambda e: e.collective_compute(
        "AllGather", OP.bypass, replica_groups=[list(range(NCORES))],
        ins=[q1sh_d.ap().opt()], outs=[q1_d.ap().opt()]),
        waits=ev_qout, inc="cc")
    p.op("gpsimd", lambda e: e.collective_compute(
        "AllGather", OP.bypass, replica_groups=[list(range(NCORES))],
        ins=[q3sh_d.ap().opt()], outs=[q3_d.ap().opt()]),
        waits=[p.now("cc")], inc="cc")
    p.op("gpsimd", lambda e: e.collective_compute(
        "AllGather", OP.bypass, replica_groups=[list(range(NCORES))],
        ins=[q2sh_d.ap().opt()], outs=[q2_d.ap().opt()]),
        waits=[p.now("cc")], inc="cc")
    ev_ag = p.now("cc")
    if pre_out is not None:
        pre_out.update(dict(ev_ag=ev_ag, q1_d=q1_d, q3_d=q3_d, q2_d=q2_d,
                            ev_q=ev_qout + [p.now("qav")],
                            free=lambda: (L.sbuf_free_abs(qb_name),
                                          L.sbuf_free_abs(pf_name),
                                          L.psum_free_abs(psb_name))))
        return L

    ev_q_all = ev_qout + [p.now("qav")]
    for i in range(NT):
        par = i % 2
        p.op("scalar", lambda e, i=i, par=par: e.dma_start(
            out=pf[:, par * TL:(par + 1) * TL],
            in_=x1_i.ap()[i * 128:(i + 1) * 128, :]),
            waits=ev_q_all + ([("xc", i - 1)] if i >= 2 else []),
            inc=f"xd{par}", inc_by=16)
        p.op("vector", lambda e, i=i, par=par: e.tensor_copy(
            xbf[:, i * TL:(i + 1) * TL], pf[:, par * TL:(par + 1) * TL]),
            waits=[p.now(f"xd{par}")], inc="xc")
    ev_xbf = p.now("xc")

    L.sbuf_free("qb")
    L.sbuf_free("pf")
    L.psum_free("ps_b")
    return _ffn0_main(L, io, x1_i, x2_o, xbf, ev_xbf, ev_ag, q1_d, q3_d, q2_d)


def _ffn0_main(L, io, x1_i, x2_o, xbf, ev_xbf, ev_ag, q1_d, q3_d, q2_d):
    p = L.p
    hsb = L.sbuf("hsb", [128, 16 * TL], BF16)
    ysb = L.sbuf("ysb", [128, NT * TL], F32)
    hA = L.sbuf("hA", [128, TL], F32)
    x1str = L.sbuf("x1str", [128, TL], F32)
    dst = L.sbuf("dst", [128, TL], F32)

    p.op("vector", lambda e: e.memset(hsb[:, 10 * TL:11 * TL], 0.0),
         waits=[ev_xbf], inc="hi")
    ev_hz = [p.now("hi")]

    ws = WStream(L, "f0", 16, need_f32=False)

    def rhs_x(k, c0, c1):
        return xbf[:, k * TL + c0:k * TL + c1]

    rounds = [(0, 16), (16, 32), (32, 43)]
    ev_y = {}
    xld = [0, 0]
    ev_xt = {}

    def load_x1(m, war_evt):
        p.op("scalar", lambda e, m=m: e.dma_start(
            out=x1str[:, 0:TL],
            in_=x1_i.ap()[m * 128:(m + 1) * 128, :]),
            waits=[war_evt], inc="x1ld0", inc_by=16)
        xld[0] += 1
        ev_xt[m] = ("x1ld0", 16 * xld[0])

    for ri, (j0, j1) in enumerate(rounds):
        hevts = {}
        for j in range(j0, j1):
            slot = j - j0
            def evac_gate(m, mi, ps_pair, ev_mm, mw, slot=slot):
                evs = []
                for jj in range(2):
                    p.op("scalar", lambda e, jj=jj, ps=ps_pair[jj], mw=mw:
                         e.activation(hA[0:mw, jj * 512:(jj + 1) * 512],
                                      ps[0:mw, :], AF.Silu),
                         waits=[ev_mm, ("hi", p.cnt("hi"))], inc="hs")
                    evs.append(p.now("hs"))
                return evs[-1]

            ws.job(q1_d, NT, [j], rhs_x, ev_xbf, evac_gate, wdtype=BF16,
                   extra_strip_wait=ev_ag)
            ev_g = p.now("hs")

            def evac_up(m, mi, ps_pair, ev_mm, mw, slot=slot, ev_g=ev_g):
                evs = []
                for jj in range(2):
                    p.op("vector", lambda e, jj=jj, ps=ps_pair[jj], slot=slot, mw=mw:
                         e.tensor_mul(hsb[0:mw, slot * TL + jj * 512:
                                          slot * TL + (jj + 1) * 512],
                                      hA[0:mw, jj * 512:(jj + 1) * 512],
                                      ps[0:mw, :]),
                         waits=[ev_mm, ev_g, ev_hz[0]], inc="hi")
                    evs.append(p.now("hi"))
                return evs[-1]

            ws.job(q3_d, NT, [j], rhs_x, ev_xbf, evac_up, wdtype=BF16,
                   extra_strip_wait=ev_ag)
            hevts[j] = p.now("hi")
        ev_h = p.now("hi")

        def rhs_h(k, c0, c1):
            return hsb[:, k * TL + c0:k * TL + c1]

        def evac_down(m, mi, ps_pair, ev_mm, mw, ri=ri):
            if ri == 0 and m not in ev_xt:
                load_x1(m, ev_y.get(m - 1))
            evs = []
            for jj in range(2):
                if ri == 0:
                    p.op("vector", lambda e, m=m, jj=jj, ps=ps_pair[jj]:
                         e.tensor_add(ysb[:, m * TL + jj * 512:m * TL + (jj + 1) * 512],
                                      ps[0:128, :],
                                      x1str[:, jj * 512:(jj + 1) * 512]),
                         waits=[ev_mm, ev_xt[m]], inc="yev")
                else:
                    p.op("vector", lambda e, m=m, jj=jj, ps=ps_pair[jj]:
                         e.tensor_add(ysb[:, m * TL + jj * 512:m * TL + (jj + 1) * 512],
                                      ysb[:, m * TL + jj * 512:m * TL + (jj + 1) * 512],
                                      ps[0:128, :]),
                         waits=[ev_mm, ev_y[m]], inc="yev")
                evs.append(p.now("yev"))
            ev_y[m] = evs[-1]
            return evs[-1]

        ws.job(q2_d, 43, list(range(NT)), rhs_h, ev_h, evac_down, wdtype=BF16,
               k_off=j0, k_count=j1 - j0, extra_strip_wait=ev_ag)
        if ri < len(rounds) - 1:
            p.op("vector", lambda e: e.memset(hsb[:, 10 * TL:11 * TL], 0.0),
                 waits=[p.now(ws.stm)], inc="hi")
            ev_hz[0] = p.now("hi")

    def post_tile(i, done_evt):
        p.op("gpsimd", lambda e, i=i: e.dma_start(
            out=x2_o.ap()[i * 128:(i + 1) * 128, :],
            in_=dst[:, 0:TL]),
            waits=[done_evt], inc="od0", inc_by=16)
        return p.now("od0")

    emit_ln(L, "ln", lambda i: ysb[:, i * TL:(i + 1) * TL],
            lambda i: ev_y[i],
            lambda i: dst[:, 0:TL],
            post_tile=post_tile, dst_lag=1)
    p.op("gpsimd", lambda e: e.engine_nop(),
         waits=[p.now("od0")], inc="fin")
    return L


def build_l1attn(L=None, io=None):
    L = L or Launch("l1attn")
    p = L.p
    x2_i = L.io(io, "x2", [D, TL], F32)
    WqT_i = L.io(io, "WqT", [D, D], F32)
    WkT_i = L.io(io, "WkT", [D, D], F32)
    WvT_i = L.io(io, "WvT", [D, D], F32)
    WoT_i = L.io(io, "WoT", [D, D], F32)
    tri_i = L.io(io, "tri01", [128, 128], BF16)
    negsib_i = L.io(io, "negsib", [128, 1], F32)
    x3_o = L.io(io, "x3", [D, TL], F32, out=True)

    q_d = L.dint("q_d", [D, TL], BF16)
    ksnd_d = L.dint("ksnd_d", [D, TL], BF16)
    vsnd_d = L.dint("vsnd_d", [TL, D], BF16)
    kag_d = L.dint("kag_d", [2 * D, TL], BF16)
    vag_d = L.dint("vag_d", [2 * TL, D], BF16)

    RSC = float(1.0 / np.sqrt(128.0))
    ISQ = float(1.0 / np.sqrt(128.0))

    tri = L.sbuf("tri", [128, 128], BF16)
    negsib = L.sbuf("negsib_sb", [128, 1], F32)
    ones = L.sbuf("ones_bf", [128, 2], BF16)
    onesr = L.sbuf("onesr_f32", [2, 128], F32)
    zsb = L.sbuf("zsb", [128, NT * TL], BF16)
    xbf = L.sbuf("xbf", [128, NT * TL], BF16)
    xf = L.sbuf("xf", [128, 2 * TL], F32)
    stg = L.sbuf("stg", [128, 2 * TL], BF16)

    p.op("vector", lambda e: e.memset(ones[:], 1.0), inc="ones_init")
    p.op("vector", lambda e: e.memset(onesr[:], 1.0), inc="ones_init")
    ev_ones = p.now("ones_init")
    p.op("sync", lambda e: e.dma_start(out=tri[:], in_=tri_i.ap()), inc="cd0",
         inc_by=16)
    p.op("sync", lambda e: e.dma_start(out=negsib[:], in_=negsib_i.ap()),
         inc="cd0", inc_by=16)
    ev_cst = p.now("cd0")

    for i in range(NT):
        par = i % 2
        p.op("scalar", lambda e, i=i, par=par: e.dma_start(
            out=xf[:, par * TL:(par + 1) * TL],
            in_=x2_i.ap()[i * 128:(i + 1) * 128, :]),
            waits=[("xc", i - 1)] if i >= 2 else [],
            inc=f"xd{par}", inc_by=16)
        p.op("vector", lambda e, i=i, par=par: e.tensor_copy(
            xbf[:, i * TL:(i + 1) * TL], xf[:, par * TL:(par + 1) * TL]),
            waits=[(f"xd{par}", 16 * ((i // 2) + 1))], inc="xc")
    ev_xbf = p.now("xc")

    def rhs_x(k, c0, c1):
        return xbf[:, k * TL + c0:k * TL + c1]

    ws = WStream(L, "qk", NT)
    sd_cnt = [0, 0]

    def mk_evac(dram):
        def evac(m, mi, ps_pair, ev_mm, mw):
            par = m % 2
            evs = []
            for j in range(2):
                p.op("vector", lambda e, ps=ps_pair[j], par=par, j=j:
                     e.tensor_copy(stg[:, par * TL + j * 512:par * TL + (j + 1) * 512],
                                   ps[:]),
                     waits=[ev_mm,
                            (f"qko{par}", 16 * sd_cnt[par]) if sd_cnt[par] else None],
                     inc="qkev")
                evs.append(p.now("qkev"))
            p.op("gpsimd", lambda e, m=m, par=par: e.dma_start(
                out=dram.ap()[m * 128:(m + 1) * 128, :],
                in_=stg[:, par * TL:(par + 1) * TL]),
                waits=evs, inc=f"qko{par}", inc_by=16)
            sd_cnt[par] += 1
            return p.now(f"qko{par}")
        return evac

    ws.job(WqT_i, NT, list(range(NT)), rhs_x, ev_xbf, mk_evac(q_d))
    ws.job(WkT_i, NT, list(range(NT)), rhs_x, ev_xbf, mk_evac(ksnd_d))
    ev_k_done = [p.now("qko0"), p.now("qko1")]

    p.op("gpsimd", lambda e: e.collective_compute(
        "AllGather", OP.bypass, replica_groups=[[2 * i, 2 * i + 1] for i in range(4)],
        ins=[ksnd_d.ap().opt()], outs=[kag_d.ap().opt()]),
        waits=ev_k_done, inc="cc")

    ev_qk_ev = p.now("qkev")
    ev_qk_mm = p.now("qk_m")
    ws.free()
    L.sbuf_free("stg")
    L.sbuf_free("xf")
    free_xbf = [False]
    wvf = L.sbuf("wvf", [128, 2 * NT * 512], F32)
    wvb = L.sbuf("wvb", [128, 2 * NT * 512], BF16)
    ps_v = [L.psum("ps_v0", [128, 512]), L.psum("ps_v1", [128, 512])]
    vstg = L.sbuf("vstg", [128, 2 * 512], BF16)
    vmm = 0
    vod = [0, 0]
    for ch in range(4):
        par = ch % 2
        sl = par * NT * 512
        p.op("sync", lambda e, ch=ch, sl=sl: e.dma_start(
            out=wvf[:, sl:sl + NT * 512].rearrange("p (k f) -> p k f", f=512),
            in_=WvT_i.ap()[:, ch * 512:(ch + 1) * 512]
            .rearrange("(k p) f -> p k f", p=128)),
            waits=([("wvc", ch - 1)] if ch >= 2 else []) + [ev_qk_mm] + ev_k_done,
            inc=f"wvd{par}", inc_by=16)
        p.op("vector", lambda e, sl=sl: e.tensor_copy(wvb[:, sl:sl + NT * 512],
                                                      wvf[:, sl:sl + NT * 512]),
             waits=[(f"wvd{par}", 16 * (ch // 2 + 1)),
                    ("vmm", max(0, vmm - 8 * NT))], inc="wvc")
        ev_w = p.now("wvc")
        for tt in range(8):
            pp = ps_v[tt % 2]
            for kc in range(NT):
                p.op("tensor", lambda e, sl=sl, tt=tt, kc=kc, pp=pp: e.matmul(
                    pp[:], xbf[:, kc * TL + tt * 128:kc * TL + (tt + 1) * 128],
                    wvb[:, sl + kc * 512:sl + (kc + 1) * 512],
                    start=(kc == 0), stop=(kc == NT - 1)),
                    waits=[ev_w, ev_xbf, ev_qk_ev,
                           ("vev", max(0, (ch * 8 + tt - 2) + 1)) if (ch * 8 + tt) >= 2 else None]
                    if kc == 0 else [ev_w],
                    inc="vmm")
                vmm += 1
            ev_vmm = p.now("vmm")
            vpar = tt % 2
            p.op("vector", lambda e, pp=pp, vpar=vpar: e.tensor_copy(
                vstg[:, vpar * 512:(vpar + 1) * 512], pp[:]),
                waits=[ev_vmm,
                       (f"vod{vpar}", 16 * vod[vpar]) if vod[vpar] else None],
                inc="vev")
            p.op("gpsimd", lambda e, ch=ch, tt=tt, vpar=vpar: e.dma_start(
                out=vsnd_d.ap()[tt * 128:(tt + 1) * 128, ch * 512:(ch + 1) * 512],
                in_=vstg[:, vpar * 512:(vpar + 1) * 512]),
                waits=[p.now("vev")], inc=f"vod{vpar}", inc_by=16)
            vod[vpar] += 1
    ev_v_done = [p.now("vod0"), p.now("vod1")]
    p.op("gpsimd", lambda e: e.collective_compute(
        "AllGather", OP.bypass, replica_groups=[[2 * i, 2 * i + 1] for i in range(4)],
        ins=[vsnd_d.ap().opt()], outs=[vag_d.ap().opt()]),
        waits=ev_v_done + [p.now("cc")], inc="cc")
    ev_ag = p.now("cc")

    L.psum_free("ps_v1")
    L.psum_free("ps_v0")
    L.sbuf_free("vstg")
    L.sbuf_free("wvb")
    L.sbuf_free("wvf")
    L.sbuf_free("xbf")

    ksb = L.sbuf("ksb", [128, 2 * 2048], BF16)
    vsb = L.sbuf("vsb", [128, 2 * 2048], BF16)
    qh = L.sbuf("qh", [128, 2 * TL], BF16)
    q2b = L.sbuf("q2b", [128, TL], BF16)
    k2b = L.sbuf("k2b", [128, 2048], BF16)
    mrow = L.sbuf("mrow", [1, 4 * TL], F32)
    mrbf = L.sbuf("mrbf", [1, 2 * TL], BF16)
    mhb = L.sbuf("mhb", [128, 2 * TL], F32)
    scf = L.sbuf("scf", [128, TL], F32)
    pb = L.sbuf("pb", [128, 3 * TL], BF16)
    linvb = L.sbuf("linvb", [128, TL], F32)
    ps_s = [L.psum("ps_s0", [128, 512]), L.psum("ps_s1", [128, 512])]
    ps_z = L.psum("ps_z", [128, TL])
    ps_l = L.psum("ps_l", [64, 512])
    ps_l2 = L.psum("ps_l2", [64, 512])
    ps_bh = L.psum("ps_bh", [128, TL])

    kld = [0, 0]
    z_evt = {}
    for h in range(NH):
        hp = h % 2
        ko, vo, qo = hp * 2048, hp * 2048, hp * TL
        war = z_evt.get(h - 2)
        p.op("sync", lambda e, h=h, ko=ko: e.dma_start(
            out=ksb[:, ko:ko + 1024], in_=kag_d.ap()[h * 128:(h + 1) * 128, :]),
            waits=[ev_ag, war], inc=f"kld{hp}", inc_by=16)
        p.op("sync", lambda e, h=h, ko=ko: e.dma_start(
            out=ksb[:, ko + 1024:ko + 2048],
            in_=ksnd_d.ap()[h * 128:(h + 1) * 128, :]),
            waits=[ev_ag, war], inc=f"kld{hp}", inc_by=16)
        p.op("sync", lambda e, h=h, vo=vo: e.dma_start(
            out=vsb[:, vo:vo + 1024].rearrange("p (s f) -> p s f", f=128),
            in_=vag_d.ap()[0:1024, h * 128:(h + 1) * 128]
            .rearrange("(s p) f -> p s f", p=128)),
            waits=[ev_ag, war], inc=f"kld{hp}", inc_by=16)
        p.op("sync", lambda e, h=h, vo=vo: e.dma_start(
            out=vsb[:, vo + 1024:vo + 2048].rearrange("p (s f) -> p s f", f=128),
            in_=vsnd_d.ap()[:, h * 128:(h + 1) * 128]
            .rearrange("(s p) f -> p s f", p=128)),
            waits=[ev_ag, war], inc=f"kld{hp}", inc_by=16)
        p.op("sync", lambda e, h=h, qo=qo: e.dma_start(
            out=qh[:, qo:qo + TL], in_=q_d.ap()[h * 128:(h + 1) * 128, :]),
            waits=[ev_ag, war], inc=f"kld{hp}", inc_by=16)
        kld[hp] += 5
        ev_ld = (f"kld{hp}", 16 * kld[hp])

        p.op("scalar", lambda e, qo=qo: e.activation(q2b[:], qh[:, qo:qo + TL],
                                                     AF.Square),
             waits=[ev_ld, ("mh", p.cnt("mh"))], inc="mh")
        p.op("scalar", lambda e, ko=ko: e.activation(k2b[:], ksb[:, ko:ko + 2048],
                                                     AF.Square),
             waits=[ev_ld, p.now("mh")], inc="mh")
        ev_sq = p.now("mh")
        for j in range(2):
            p.op("tensor", lambda e, j=j: e.matmul(
                ps_l[0:1, :], ones[:, 0:1], q2b[:, j * 512:(j + 1) * 512],
                start=True, stop=True), waits=[ev_sq, ev_ones, p.now("mhv")],
                inc="mhm")
            p.op("vector", lambda e, j=j: e.tensor_copy(
                mrow[0:1, j * 512:(j + 1) * 512], ps_l[0:1, :]),
                waits=[p.now("mhm")], inc="mhv")
        for j in range(4):
            p.op("tensor", lambda e, j=j: e.matmul(
                ps_l2[0:1, :], ones[:, 0:1], k2b[:, j * 512:(j + 1) * 512],
                start=True, stop=True), waits=[ev_sq, ev_ones, p.now("mhv")],
                inc="mhm")
            p.op("vector", lambda e, j=j: e.tensor_copy(
                mrow[0:1, TL + j * 512:TL + (j + 1) * 512], ps_l2[0:1, :]),
                waits=[p.now("mhm")], inc="mhv")
        p.op("vector", lambda e: e.reduce_max(mrow[0:1, 2 * TL:2 * TL + 1],
                                              mrow[0:1, TL:2 * TL], axis=AX),
             waits=[p.now("mhv")], inc="mhv")
        p.op("scalar", lambda e: e.activation(mrow[0:1, 2 * TL + 1:2 * TL + 2],
                                              mrow[0:1, 2 * TL:2 * TL + 1],
                                              AF.Sqrt),
             waits=[p.now("mhv")], inc="mh")
        p.op("scalar", lambda e: e.activation(mrow[0:1, 0:TL], mrow[0:1, 0:TL],
                                              AF.Sqrt),
             waits=[p.now("mh"), p.now("mhv")], inc="mh")
        p.op("vector", lambda e: e.tensor_scalar(
            mrow[0:1, 0:TL], mrow[0:1, 0:TL], mrow[0:1, 2 * TL + 1:2 * TL + 2],
            ISQ, OP.mult, OP.mult),
            waits=[p.now("mh")], inc="mhv")
        p.op("vector", lambda e: e.tensor_copy(mrbf[0:1, 0:TL], mrow[0:1, 0:TL]),
             waits=[p.now("mhv")], inc="mhv")
        ev_mh = p.now("mhv")
        for j in range(2):
            p.op("tensor", lambda e, j=j: e.matmul(
                ps_bh[:, j * 512:(j + 1) * 512], ones[0:1, 0:1] if False else
                onesr[0:1, :].bitcast(F32) if False else onesr[0:1, :],
                mrow[0:1, j * 512:(j + 1) * 512], start=True, stop=True),
                waits=[ev_mh, ("sev", p.cnt("sev"))], inc="mhm")
        p.op("vector", lambda e: e.tensor_copy(mhb[:, 0:TL], ps_bh[:, 0:TL]),
             waits=[p.now("mhm")], inc="mhv")
        p.op("vector", lambda e: e.tensor_scalar(
            mhb[:, TL:2 * TL], mhb[:, 0:TL], negsib[:, 0:1], None, OP.subtract),
            waits=[p.now("mhv"), ev_cst], inc="mhv")
        ev_mhb = p.now("mhv")

        first = True
        pb_evt = {}
        for st in range(16):
            sib = (st < 8)
            q0 = 0 if sib else (st - 8) * 128
            nq = TL - q0
            pslot = (st % 3) * TL
            nchq = ceil_div(nq, 512)
            ev_sc = []
            for j in range(nchq):
                c0 = q0 + j * 512
                c1 = min(TL, c0 + 512)
                p.op("tensor", lambda e, st=st, ko=ko, qo=qo, c0=c0, c1=c1, j=j:
                     e.matmul(ps_s[j][0:128, 0:c1 - c0],
                              ksb[:, ko + st * 128:ko + (st + 1) * 128],
                              qh[:, qo + c0:qo + c1], start=True, stop=True),
                     waits=[ev_ld, ("sev", p.cnt("sev") - 2)
                            if p.cnt("sev") >= 2 else None],
                     inc="smm")
                ev_mm = p.now("smm")
                p.op("vector", lambda e, st=st, c0=c0, c1=c1, j=j, sib=sib:
                     e.scalar_tensor_tensor(
                         scf[:, c0:c1], ps_s[j][0:128, 0:c1 - c0], RSC,
                         mhb[:, (TL if sib else 0) + c0:(TL if sib else 0) + c1],
                         OP.mult, OP.subtract),
                     waits=[ev_mm, ev_mhb], inc="sev")
                ev_sc.append(p.now("sev"))
            if q0 > 0:
                p.op("vector", lambda e, q0=q0, pslot=pslot: e.memset(
                    pb[:, pslot:pslot + q0], 0.0),
                    waits=[pb_evt.get(st - 3)], inc="pex")
            p.op("scalar", lambda e, q0=q0, nq=nq, pslot=pslot: e.activation(
                pb[:, pslot + q0:pslot + TL], scf[:, q0:TL], AF.Exp),
                waits=ev_sc + [pb_evt.get(st - 3), p.now("pex")], inc="pex")
            ev_p = p.now("pex")
            if not sib:
                p.op("vector", lambda e, pslot=pslot, q0=q0: e.tensor_mul(
                    pb[:, pslot + q0:pslot + q0 + 128],
                    pb[:, pslot + q0:pslot + q0 + 128], tri[:]),
                    waits=[ev_p, ev_cst], inc="pex")
                ev_p = p.now("pex")
            segs = [(0, 512), (512, TL)]
            for (c0, c1) in segs:
                psl = ps_l if c0 < 512 else ps_l2
                cb = c0 % 512
                p.op("tensor", lambda e, c0=c0, c1=c1, pslot=pslot, psl=psl, cb=cb,
                     s0=first, s1=(st == 15):
                     e.matmul(psl[0:1, cb:cb + c1 - c0], ones[:, 0:1],
                              pb[:, pslot + c0:pslot + c1],
                              start=s0, stop=s1),
                     waits=[ev_p], inc="smm")
            for (c0, c1) in segs:
                p.op("tensor", lambda e, st=st, vo=vo, c0=c0, c1=c1, pslot=pslot,
                     s0=first, s1=(st == 15):
                     e.matmul(ps_z[:, c0:c1],
                              vsb[:, vo + st * 128:vo + (st + 1) * 128],
                              pb[:, pslot + c0:pslot + c1],
                              start=s0, stop=s1),
                     waits=[ev_p], inc="smm")
            pb_evt[st] = p.now("smm")
            first = False
        ev_att = p.now("smm")
        for j in range(2):
            psl = ps_l if j == 0 else ps_l2
            p.op("vector", lambda e, j=j, psl=psl: e.tensor_copy(
                mrow[0:1, TL + j * 512:TL + (j + 1) * 512], psl[0:1, :]),
                waits=[ev_att], inc="mhv")
        p.op("vector", lambda e: e.reciprocal(mrow[0:1, 3 * TL:4 * TL],
                                              mrow[0:1, TL:2 * TL]),
             waits=[p.now("mhv")], inc="mhv")
        ev_li = p.now("mhv")
        for j in range(2):
            p.op("tensor", lambda e, j=j: e.matmul(
                ps_bh[:, j * 512:(j + 1) * 512], onesr[0:1, :],
                mrow[0:1, 3 * TL + j * 512:3 * TL + (j + 1) * 512],
                start=True, stop=True),
                waits=[ev_li], inc="mhm")
        p.op("vector", lambda e: e.tensor_copy(linvb[:, 0:TL], ps_bh[:, 0:TL]),
             waits=[p.now("mhm"), z_evt.get(h - 1)], inc="mhv")
        ev_lb = p.now("mhv")
        for j in range(2):
            p.op("vector", lambda e, h=h, j=j: e.scalar_tensor_tensor(
                zsb[:, h * TL + j * 512:h * TL + (j + 1) * 512],
                ps_z[:, j * 512:(j + 1) * 512], 1.0,
                linvb[:, j * 512:(j + 1) * 512],
                OP.mult, OP.mult),
                waits=[ev_att, ev_lb], inc="zev")
        z_evt[h] = p.now("zev")
    ev_z = p.now("zev")

    for nm in ["ps_bh", "ps_l2", "ps_l", "ps_z", "ps_s1", "ps_s0"]:
        L.psum_free(nm)
    for nm in ["linvb", "pb", "scf", "mhb", "mrbf", "mrow", "k2b", "q2b",
               "qh", "vsb", "ksb"]:
        L.sbuf_free(nm)

    ysb = L.sbuf("ysb", [128, NT * TL], F32)
    x2str = L.sbuf("x2str", [128, TL], F32)
    dst = L.sbuf("dst", [128, TL], F32)

    def rhs_z(k, c0, c1):
        return zsb[:, k * TL + c0:k * TL + c1]

    ev_y = {}
    xld2 = [0, 0]
    ev_xt = {}

    def load_x2(m, war_evt):
        p.op("scalar", lambda e, m=m: e.dma_start(
            out=x2str[:, 0:TL],
            in_=x2_i.ap()[m * 128:(m + 1) * 128, :]),
            waits=[war_evt or ev_z], inc="x2l0", inc_by=16)
        xld2[0] += 1
        ev_xt[m] = ("x2l0", 16 * xld2[0])

    def evac_o(m, mi, ps_pair, ev_mm, mw):
        if m not in ev_xt:
            load_x2(m, ev_y.get(m - 1))
        evs = []
        for j in range(2):
            p.op("vector", lambda e, m=m, j=j, ps=ps_pair[j]: e.tensor_add(
                ysb[:, m * TL + j * 512:m * TL + (j + 1) * 512], ps[:],
                x2str[:, j * 512:(j + 1) * 512]),
                waits=[ev_mm, ev_xt[m]], inc="yev")
            evs.append(p.now("yev"))
        ev_y[m] = evs[-1]
        return evs[-1]

    ws2 = WStream(L, "wo", NT)
    ws2.job(WoT_i, NT, list(range(NT)), rhs_z, ev_z, evac_o)

    def post_tile(i, done_evt):
        p.op("gpsimd", lambda e, i=i: e.dma_start(
            out=x3_o.ap()[i * 128:(i + 1) * 128, :],
            in_=dst[:, 0:TL]),
            waits=[done_evt], inc="od0", inc_by=16)
        return p.now("od0")

    emit_ln(L, "ln", lambda i: ysb[:, i * TL:(i + 1) * TL],
            lambda i: ev_y[i],
            lambda i: dst[:, 0:TL],
            post_tile=post_tile, dst_lag=1)
    p.op("gpsimd", lambda e: e.engine_nop(),
         waits=[p.now("od0")], inc="fin")
    return L


def build_ffn1(L=None, io=None, pre_out=None, pre=None):
    L = L or Launch("ffn1")
    p = L.p
    x3_i = L.io(io, "x3", [D, TL], F32)
    gup_i = L.io(io, "gupT", [256, GUP], I32)
    gus_i = L.io(io, "gusE", [256, GUP], F16)
    dnp_i = L.io(io, "dnpT", [704, D], I32)
    dns_i = L.io(io, "dnsE", [704, D], F16)
    nmk_i = L.io(io, "nibmask", [128, 1], I32)
    niv_i = L.io(io, "nibinv", [128, 1], F32)
    y_o = L.io(io, "y", [D, TL], F32, out=True)

    qgush_d = L.dint("qgush_d", [256, GUP], BF16)
    qdnsh_d = L.dint("qdnsh_d", [704, D], BF16)
    qgu_d = L.dint("qgu_d", [D, GUP], BF16)
    qdn_d = L.dint("qdn_d", [I1, D], BF16)

    if pre is not None:
        pre["free"]()
        xbf = L.sbuf("xbf", [128, NT * TL], BF16)
        pf = L.sbuf("pfm", [128, 2 * TL], F32)
        for i in range(NT):
            par = i % 2
            p.op("scalar", lambda e, i=i, par=par: e.dma_start(
                out=pf[:, par * TL:(par + 1) * TL],
                in_=x3_i.ap()[i * 128:(i + 1) * 128, :]),
                waits=([("xc", i - 1)] if i >= 2 else []),
                inc=f"xd{par}", inc_by=16)
            p.op("vector", lambda e, i=i, par=par: e.tensor_copy(
                xbf[:, i * TL:(i + 1) * TL], pf[:, par * TL:(par + 1) * TL]),
                waits=[p.now(f"xd{par}")], inc="xc")
        ev_xbf = p.now("xc")
        L.sbuf_free("pfm")
        return _ffn1_main(L, io, x3_i, y_o, xbf, ev_xbf, pre["ev_ag"],
                          pre["qgu_d"], pre["qdn_d"])
    if pre_out is None:
        xbf = L.sbuf("xbf", [128, NT * TL], BF16)
    nmk = L.sbuf("nmk", [128, 1], I32)
    niv = L.sbuf("niv", [128, 1], F32)
    p.op("sync", lambda e: e.dma_start(out=nmk[:], in_=nmk_i.ap()), inc="nd",
         inc_by=16)
    p.op("sync", lambda e: e.dma_start(out=niv[:], in_=niv_i.ap()), inc="nd",
         inc_by=16)
    ev_nsh = p.now("nd")

    PW = 704
    pw = L.sbuf("pw", [128, 2 * PW], I32)
    psc = L.sbuf("psc", [128, 2 * PW], F16)
    ib1 = L.sbuf("ib1", [128, PW], I32)
    ib2 = L.sbuf("ib2", [128, PW], I32)
    fA = L.sbuf("fA", [128, PW], F32)
    fB = L.sbuf("fB", [128, PW], F32)
    fC = L.sbuf("fC", [128, PW], F32)
    qo = L.sbuf("qo", [128, 2 * PW], BF16)

    pieces = []
    for t in range(2):
        for cp in range(16):
            pieces.append((gup_i, gus_i, qgush_d, t * 128, 128, cp * PW, PW))
    for t in range(6):
        nr = min(128, 704 - t * 128)
        for cp in range(4):
            pieces.append((dnp_i, dns_i, qdnsh_d, t * 128, nr, cp * 512, 512))
    qdo = {}
    for pi, (wd, sd, qd, r0, nr, c0, cols) in enumerate(pieces):
        par = pi % 2
        sl = par * PW
        war = [("dqv", 14 * (pi - 1)) if pi >= 2 else None, qdo.get(pi - 2)]
        p.op("sync", lambda e, wd=wd, r0=r0, nr=nr, c0=c0, cols=cols, sl=sl:
             e.dma_start(out=pw[0:nr, sl:sl + cols],
                         in_=wd.ap()[r0:r0 + nr, c0:c0 + cols]),
             waits=war, inc=f"dqd{par}", inc_by=16)
        p.op("sync", lambda e, sd=sd, r0=r0, nr=nr, c0=c0, cols=cols, sl=sl:
             e.dma_start(out=psc[0:nr, sl:sl + cols],
                         in_=sd.ap()[r0:r0 + nr, c0:c0 + cols]),
             waits=war, inc=f"dqd{par}", inc_by=16)
        nd = p.cnt(f"dqd{par}")
        ev_in = (f"dqd{par}", nd)
        W = [ev_in, ev_nsh]
        p.op("vector", lambda e, nr=nr, sl=sl, cols=cols: e.tensor_scalar(
            ib1[0:nr, 0:cols], pw[0:nr, sl:sl + cols], nmk[0:nr, 0:1], None,
            OP.bitwise_and),
            waits=W + [("dqv", 14 * pi)], inc="dqv")
        p.op("vector", lambda e, nr=nr, cols=cols: e.tensor_copy(
            fA[0:nr, 0:cols], ib1[0:nr, 0:cols]),
            waits=[p.now("dqv")], inc="dqv")
        p.op("vector", lambda e, nr=nr, cols=cols: e.tensor_scalar(
            fA[0:nr, 0:cols], fA[0:nr, 0:cols], niv[0:nr, 0:1], None, OP.mult),
            waits=[p.now("dqv")], inc="dqv")
        p.op("vector", lambda e, nr=nr, cols=cols: e.tensor_scalar(
            fB[0:nr, 0:cols], fA[0:nr, 0:cols], 8.0, None, OP.is_ge),
            waits=[p.now("dqv")], inc="dqv")
        p.op("vector", lambda e, nr=nr, cols=cols: e.scalar_tensor_tensor(
            fA[0:nr, 0:cols], fB[0:nr, 0:cols], -8.0, fA[0:nr, 0:cols],
            OP.mult, OP.add),
            waits=[p.now("dqv")], inc="dqv")
        p.op("vector", lambda e, nr=nr, cols=cols: e.tensor_scalar(
            fA[0:nr, 0:cols], fA[0:nr, 0:cols], 4194304.0, 1056964608.0,
            OP.mult, OP.add),
            waits=[p.now("dqv")], inc="dqv")
        p.op("vector", lambda e, nr=nr, cols=cols: e.tensor_copy(
            ib2[0:nr, 0:cols], fA[0:nr, 0:cols]),
            waits=[p.now("dqv")], inc="dqv")
        p.op("vector", lambda e, nr=nr, cols=cols: e.tensor_scalar(
            fC[0:nr, 0:cols], ib2[0:nr, 0:cols].bitcast(F32), 1.0, None, OP.is_lt),
            waits=[p.now("dqv")], inc="dqv")
        p.op("vector", lambda e, nr=nr, cols=cols: e.scalar_tensor_tensor(
            fA[0:nr, 0:cols], fC[0:nr, 0:cols], 1.0, ib2[0:nr, 0:cols].bitcast(F32),
            OP.add, OP.mult),
            waits=[p.now("dqv")], inc="dqv")
        p.op("vector", lambda e, nr=nr, cols=cols: e.tensor_sub(
            fA[0:nr, 0:cols], fA[0:nr, 0:cols], fC[0:nr, 0:cols]),
            waits=[p.now("dqv")], inc="dqv")
        p.op("vector", lambda e, nr=nr, cols=cols: e.tensor_scalar(
            fB[0:nr, 0:cols], fB[0:nr, 0:cols], -2.0, 1.0, OP.mult, OP.add),
            waits=[p.now("dqv")], inc="dqv")
        p.op("vector", lambda e, nr=nr, cols=cols: e.tensor_mul(
            fA[0:nr, 0:cols], fA[0:nr, 0:cols], fB[0:nr, 0:cols]),
            waits=[p.now("dqv")], inc="dqv")
        p.op("vector", lambda e, nr=nr, cols=cols, sl=sl: e.tensor_copy(
            fC[0:nr, 0:cols], psc[0:nr, sl:sl + cols]),
            waits=[p.now("dqv")], inc="dqv")
        p.op("vector", lambda e, nr=nr, cols=cols, sl=sl: e.tensor_mul(
            qo[0:nr, sl:sl + cols], fA[0:nr, 0:cols], fC[0:nr, 0:cols]),
            waits=[p.now("dqv")], inc="dqv")
        p.op("gpsimd", lambda e, qd=qd, r0=r0, nr=nr, c0=c0, cols=cols, sl=sl:
             e.dma_start(out=qd.ap()[r0:r0 + nr, c0:c0 + cols],
                         in_=qo[0:nr, sl:sl + cols]),
             waits=[p.now("dqv")], inc=f"dqo{par}", inc_by=16)
        qdo[pi] = p.now(f"dqo{par}")
    ev_dq = [p.now("dqo0"), p.now("dqo1")]
    p.op("gpsimd", lambda e: e.collective_compute(
        "AllGather", OP.bypass, replica_groups=[list(range(NCORES))],
        ins=[qgush_d.ap().opt()], outs=[qgu_d.ap().opt()]),
        waits=ev_dq, inc="cc")
    p.op("gpsimd", lambda e: e.collective_compute(
        "AllGather", OP.bypass, replica_groups=[list(range(NCORES))],
        ins=[qdnsh_d.ap().opt()], outs=[qdn_d.ap().opt()]),
        waits=[p.now("cc")], inc="cc")
    ev_ag = p.now("cc")
    if pre_out is not None:
        pnames = [L.ns + nm for nm in
                  ["qo", "fC", "fB", "fA", "ib2", "ib1", "psc", "pw"]]
        pre_out.update(dict(ev_ag=ev_ag, qgu_d=qgu_d, qdn_d=qdn_d,
                            ev_q=ev_dq + [p.now("dqv")],
                            free=lambda: [L.sbuf_free_abs(nm) for nm in pnames]))
        return L

    ev_q_all = ev_dq + [p.now("dqv")]
    for i in range(NT):
        par = i % 2
        p.op("scalar", lambda e, i=i, par=par: e.dma_start(
            out=fA[:, 0:TL] if par == 0 else fC[:, 0:TL],
            in_=x3_i.ap()[i * 128:(i + 1) * 128, :]),
            waits=ev_q_all + ([("xc", i - 1)] if i >= 2 else []),
            inc=f"xd{par}", inc_by=16)
        p.op("vector", lambda e, i=i, par=par: e.tensor_copy(
            xbf[:, i * TL:(i + 1) * TL], fA[:, 0:TL] if par == 0 else fC[:, 0:TL]),
            waits=[(f"xd{par}", 16 * (i // 2 + 1))], inc="xc")
    ev_xbf = p.now("xc")

    for nm in ["qo", "fC", "fB", "fA", "ib2", "ib1", "psc", "pw"]:
        L.sbuf_free(nm)

    return _ffn1_main(L, io, x3_i, y_o, xbf, ev_xbf, ev_ag, qgu_d, qdn_d)


def _ffn1_main(L, io, x3_i, y_o, xbf, ev_xbf, ev_ag, qgu_d, qdn_d):
    p = L.p
    hsb = L.sbuf("hsb", [128, 22 * TL], BF16)
    ysb = L.sbuf("ysb", [128, NT * TL], F32)
    hA = L.sbuf("hA", [128, TL], F32)
    x3str = L.sbuf("x3str", [128, 2 * TL], F32)
    dst = L.sbuf("dst", [128, 2 * TL], F32)

    ws = WStream(L, "f1", 22, need_f32=False)

    def rhs_x(k, c0, c1):
        return xbf[:, k * TL + c0:k * TL + c1]

    rounds = [(0, 22), (22, 44)]
    ev_y = {}

    for ri, (j0, j1) in enumerate(rounds):
        for j in range(j0, j1):
            slot = j - j0

            def evac_gate(m, mi, ps_pair, ev_mm, mw):
                evs = []
                for jj in range(2):
                    p.op("scalar", lambda e, jj=jj, ps=ps_pair[jj]:
                         e.activation(hA[:, jj * 512:(jj + 1) * 512],
                                      ps[:], AF.Silu),
                         waits=[ev_mm, p.now("hi")], inc="hs")
                    evs.append(p.now("hs"))
                return evs[-1]

            ws.job(qgu_d, NT, [j], rhs_x, ev_xbf, evac_gate, wdtype=BF16,
                   extra_strip_wait=ev_ag)
            ev_g = p.now("hs")

            def evac_up(m, mi, ps_pair, ev_mm, mw, slot=slot, ev_g=ev_g):
                evs = []
                for jj in range(2):
                    p.op("vector", lambda e, jj=jj, ps=ps_pair[jj], slot=slot:
                         e.tensor_mul(hsb[:, slot * TL + jj * 512:
                                          slot * TL + (jj + 1) * 512],
                                      hA[:, jj * 512:(jj + 1) * 512],
                                      ps[:]),
                         waits=[ev_mm, ev_g], inc="hi")
                    evs.append(p.now("hi"))
                return evs[-1]

            ws.job(qgu_d, NT, [44 + j], rhs_x, ev_xbf, evac_up, wdtype=BF16,
                   extra_strip_wait=ev_ag)
        ev_h = p.now("hi")

        def rhs_h(k, c0, c1):
            return hsb[:, k * TL + c0:k * TL + c1]

        def evac_down(m, mi, ps_pair, ev_mm, mw, ri=ri):
            evs = []
            for jj in range(2):
                if ri == 0:
                    p.op("vector", lambda e, m=m, jj=jj, ps=ps_pair[jj]:
                         e.tensor_copy(ysb[:, m * TL + jj * 512:m * TL + (jj + 1) * 512],
                                       ps[:]),
                         waits=[ev_mm], inc="yev")
                else:
                    p.op("vector", lambda e, m=m, jj=jj, ps=ps_pair[jj]:
                         e.tensor_add(ysb[:, m * TL + jj * 512:m * TL + (jj + 1) * 512],
                                      ysb[:, m * TL + jj * 512:m * TL + (jj + 1) * 512],
                                      ps[:]),
                         waits=[ev_mm, ev_y[m]], inc="yev")
                evs.append(p.now("yev"))
            ev_y[m] = evs[-1]
            return evs[-1]

        ws.job(qdn_d, 44, list(range(NT)), rhs_h, ev_h, evac_down, wdtype=BF16,
               k_off=j0, k_count=j1 - j0, extra_strip_wait=ev_ag)

    ev_rms = emit_ln(L, "rms", lambda i: ysb[:, i * TL:(i + 1) * TL],
                     lambda i: ev_y[i],
                     lambda i: ysb[:, i * TL:(i + 1) * TL],
                     rms=True, rms_eps=1e-6, alloc_tag="ln")
    xld = [0, 0]
    ev_res = {}
    for i in range(NT):
        par = i % 2
        p.op("scalar", lambda e, i=i, par=par: e.dma_start(
            out=x3str[:, par * TL:(par + 1) * TL],
            in_=x3_i.ap()[i * 128:(i + 1) * 128, :]),
            waits=[ev_res.get(i - 2)], inc=f"x3l{par}", inc_by=16)
        xld[par] += 1
        p.op("vector", lambda e, i=i, par=par: e.tensor_add(
            ysb[:, i * TL:(i + 1) * TL], ysb[:, i * TL:(i + 1) * TL],
            x3str[:, par * TL:(par + 1) * TL]),
            waits=[(f"x3l{par}", 16 * xld[par]), ev_rms[i]], inc="resv")
        ev_res[i] = p.now("resv")

    def post_tile(i, done_evt):
        par = i % 2
        p.op("gpsimd", lambda e, i=i, par=par: e.dma_start(
            out=y_o.ap()[i * 128:(i + 1) * 128, :],
            in_=dst[:, par * TL:(par + 1) * TL]),
            waits=[done_evt], inc=f"od{par}", inc_by=16)
        return p.now(f"od{par}")

    emit_ln(L, "ln", lambda i: ysb[:, i * TL:(i + 1) * TL],
            lambda i: ev_res[i],
            lambda i: dst[:, (i % 2) * TL:(i % 2) * TL + TL],
            post_tile=post_tile, alloc_tag="ln",
            prior_evt=("resv", NT))
    p.op("gpsimd", lambda e: e.engine_nop(),
         waits=[p.now("od0"), p.now("od1")], inc="fin")
    return L


def build_fused():
    L = Launch("mk")
    p = L.p
    io = {}
    io["xcat"] = L.din("xcat", [D, TL + 1], F32)
    io["WactT"] = L.din("WactT", [2 * D, D], F32)
    io["WrecT"] = L.din("WrecT", [2 * D, D], F32)
    io["WvalT"] = L.din("WvalT", [D, D], F32)
    io["pm"] = L.din("pm", [128, 1], F32)
    io["mbias"] = L.din("mbias", [128, 1], F32)
    io["w1sh"] = L.din("w1sh", [256, H0], F32)
    io["w3sh"] = L.din("w3sh", [256, H0], F32)
    io["w2sh"] = L.din("w2sh", [688, D], F32)
    io["WqT"] = L.din("WqT", [D, D], F32)
    io["WkT"] = L.din("WkT", [D, D], F32)
    io["WvT"] = L.din("WvT", [D, D], F32)
    io["WoT"] = L.din("WoT", [D, D], F32)
    io["tri01"] = L.din("tri01", [128, 128], BF16)
    io["negsib"] = L.din("negsib", [128, 1], F32)
    io["gupT"] = L.din("gupT", [256, GUP], I32)
    io["gusE"] = L.din("gusE", [256, GUP], F16)
    io["dnpT"] = L.din("dnpT", [704, D], I32)
    io["dnsE"] = L.din("dnsE", [704, D], F16)
    io["nibmask"] = L.din("nibmask", [128, 1], I32)
    io["nibinv"] = L.din("nibinv", [128, 1], F32)
    io["y"] = L.dout("y", [D, TL], F32)
    io["xv"] = L.dint("xv_d", [D, TL], F32)
    io["r4"] = L.dint("r4_d", [D, TL], BF16)
    io["kv"] = L.dint("kv_d", [D, TL], BF16)
    io["x1"] = L.dint("x1_d", [D, TL], F32)
    io["x2"] = L.dint("x2_d", [D, TL], F32)
    io["x3"] = L.dint("x3_d", [D, TL], F32)

    def phase(ns, barrier):
        L.ns = ns
        p.barrier_evts = barrier
        return p.snapshot(), L.mark()

    s, m_pre0 = phase("F0p_", [])
    pre0 = {}
    build_ffn0(L, io, pre_out=pre0)
    ev_pre0 = p.events_since(s)
    names_pre0 = p.active_names()

    s, m0 = phase("P0_", [])
    part2 = []
    build_l0a(L, io, part2=part2)
    ev_p0a = p.events_since(s)

    s1, m1 = phase("P1_", ev_p0a)
    build_l0s(L, io)
    ev_p1 = p.events_since(s1)

    L.ns = "P0_"
    p.barrier_evts = []
    s0b = p.snapshot()
    part2[0]()
    ev_p0b = p.events_since(s0b)

    L.free_to_mark(m1)
    L.free_to_mark(m0)

    s2, m2 = phase("P2_", ev_p1 + ev_p0b)
    build_l0v(L, io)
    ev_p2 = p.events_since(s2)
    names_early = p.active_names() - names_pre0
    L.free_to_mark(m2)

    s, m_pre1 = phase("F1p_", ev_p2 + ev_pre0)
    pre0["free"]()
    pre1 = {}
    build_ffn1(L, io, pre_out=pre1)
    ev_pre1 = p.events_since(s)
    names_pre1 = p.active_names() - names_pre0 - names_early

    s3, m3 = phase("P3_", ev_p2 + ev_pre0)
    p.retire_names(names_pre0 | names_early)
    build_ffn0(L, io, pre=pre0)
    ev_p3 = p.events_since(s3)
    L.free_to_mark(m3)

    s4, m4 = phase("P4_", ev_p3)
    build_l1attn(L, io)
    ev_p4 = p.events_since(s4)
    names_mid = (p.active_names() - names_pre1)
    L.free_to_mark(m4)

    s5, m5 = phase("P5_", ev_p4 + ev_pre1)
    p.retire_names(names_mid | names_pre1)
    build_ffn1(L, io, pre=pre1)
    L.ns = ""
    return L


_FUSED = [None]


def kernel(**inputs):
    common, per_core = stage_all(inputs)
    R = range(NCORES)
    w1sh = np.split(common["w1T"], NCORES, axis=0)
    w3sh = np.split(common["w3T"], NCORES, axis=0)
    w2sh = np.split(common["w2T"], NCORES, axis=0)

    if _FUSED[0] is None:
        _FUSED[0] = build_fused()
        _FUSED[0].p.emit()
    L = _FUSED[0]
    in_maps = []
    for c in R:
        pc = per_core[c]
        in_maps.append(dict(
            xcat=pc["xcat"], WactT=common["WactT"], WrecT=common["WrecT"],
            WvalT=common["WvalT"], pm=pc["pm"], mbias=pc["mbias"],
            w1sh=w1sh[c], w3sh=w3sh[c], w2sh=w2sh[c],
            WqT=common["WqT"], WkT=common["WkT"], WvT=common["WvT"],
            WoT=common["WoT"], tri01=common["tri01"], negsib=pc["negsib"],
            gupT=common["gupT"], gusE=common["gusE"], dnpT=common["dnpT"],
            dnsE=common["dnsE"], nibmask=pc["nibmask"], nibinv=pc["nibinv"],
        ))
    res = run_bass_kernel_spmd(L.nc, in_maps, core_ids=list(range(NCORES)))
    out = np.empty((B, T, D), np.float32)
    for c in R:
        b, h = c // 2, c % 2
        out[b, h * TL:(h + 1) * TL, :] = res.results[c]["y"].T[:, PI_INV]
    return out


# version 04
# speedup vs baseline: 1.1528x; 1.1528x over previous
import numpy as np
import ml_dtypes
import concourse.bass as bass
import concourse.mybir as mybir
from concourse.alu_op_type import AluOpType as OP
from concourse.bass_utils import run_bass_kernel_spmd

F32 = mybir.dt.float32
F16 = mybir.dt.float16
I32 = mybir.dt.int32
BF16 = mybir.dt.bfloat16
AF = mybir.ActivationFunctionType
AX = mybir.AxisListType.X

NCORES = 8
D = 2048
B = 4
T = 2048
TL = 1024
NT = D // 128
H0 = 5461
H0P = 5504
NH = 16
HD = 128
I1 = 5632
GUP = 2 * I1
NEG = -1e30
BIGNEG = -3.0e38
INV2PI = float(1.0 / (2 * np.pi))
TWOPI = float(2 * np.pi)

PI_P = np.arange(D).reshape(256, 8).T.reshape(-1).copy()
PI_INV = np.argsort(PI_P)
XVP = np.concatenate([np.arange(0, D, 2), np.arange(1, D, 2)])
PII = np.arange(I1).reshape(704, 8).T.reshape(-1).copy()


def ceil_div(a, b):
    return (a + b - 1) // b


class Prog:
    def __init__(self, nc):
        self.nc = nc
        self.ops = []
        self.sems = {}
        self.counts = {}
        self.ns = ""
        self.barrier_evts = []
        self._active = {}
        self._pool = []
        self._dma_only = {}
        self.base = {}

    def sem(self, name, kind=None):
        if name in self.sems:
            if kind is not None and self._dma_only.get(name) is None:
                self._dma_only[name] = kind
            return name
        name2 = self.ns + name
        if name2 not in self.sems:
            h = None
            base = 0
            want = kind or "c"
            for i, (ph, pc, pk) in enumerate(self._pool):
                if pk == want and (want == "c" or pc % 16 == 0):
                    h, base, _ = self._pool.pop(i)
                    break
            if h is None:
                h = self.nc.semaphore(name2).__enter__()
                base = 0
            self.sems[name2] = h
            self.base[name2] = base
            self.counts[name2] = 0
            self._active[name2] = h
            self._dma_only[name2] = kind
        else:
            if kind is not None and self._dma_only.get(name2) is None:
                self._dma_only[name2] = kind
        return name2

    def retire_all(self):
        for name, h in self._active.items():
            self._pool.append((h, self.base[name] + self.counts[name],
                               self._dma_only.get(name) or "c"))
        self._active = {}

    def retire_names(self, names):
        for name in list(names):
            if name in self._active:
                h = self._active.pop(name)
                self._pool.append((h, self.base[name] + self.counts[name],
                                   self._dma_only.get(name) or "c"))

    def active_names(self):
        return set(self._active.keys())

    def events_since(self, snap):
        return [(s, v) for s, v in self.counts.items() if v > snap.get(s, 0)]

    def cnt(self, name):
        return self.counts.get(self.ns + name, self.counts.get(name, 0))

    def has(self, name):
        return (self.ns + name) in self.counts

    def op(self, engine, fn, waits=(), inc=None, inc_by=1):
        wl = [(self.sem(w[0]), w[1]) for w in waits
              if w is not None and w[1] > 0]
        wl = wl + self.barrier_evts
        incs = []
        if inc is not None:
            if inc_by == 16:
                kind = "swdge" if engine == "gpsimd" else "hwdge"
            else:
                kind = "c"
            inc = self.sem(inc, kind=kind)
            self.counts[inc] += inc_by
            incs = [(inc, inc_by)]
        self.ops.append((engine, fn, wl, incs))
        return (inc, self.counts[inc]) if inc else None

    def now(self, semname):
        semname = self.sem(semname)
        return (semname, self.counts[semname])

    def snapshot(self):
        return dict(self.counts)

    def merge_even(self, lo, mid, hi):
        X = self.ops[lo:mid]
        Y = self.ops[mid:hi]
        if not X or not Y:
            return
        big, small = (X, Y) if len(X) >= len(Y) else (Y, X)
        ratio = len(big) / len(small)
        merged = []
        si = 0
        for bi in range(len(big)):
            merged.append(big[bi])
            while si < len(small) and si < int((bi + 1) / ratio):
                merged.append(small[si]); si += 1
        merged.extend(small[si:])
        self.ops[lo:hi] = merged

    def set_barrier_from(self, snap):
        evs = []
        for s, v in self.counts.items():
            if v > snap.get(s, 0):
                evs.append((s, v))
        self.barrier_evts = evs
        self.retire_all()

    def emit(self):
        nc = self.nc
        order = ("sync", "vector", "scalar", "tensor", "gpsimd")
        with nc.Block() as block:
            for en in order:
                def body(eng, en=en):
                    last = {}
                    for (e, fn, waits, incs) in self.ops:
                        if e != en:
                            continue
                        for (s, v) in waits:
                            va = v + self.base.get(s, 0)
                            if last.get(s, 0) >= va:
                                continue
                            eng.wait_ge(self.sems[s], va)
                            last[s] = va
                        instr = fn(eng)
                        for (s, by) in incs:
                            instr.then_inc(self.sems[s], by)
                getattr(block, en)(body)


class Launch:
    def __init__(self, name):
        self.nc = bass.Bass("TRN2", target_bir_lowering=False, debug=False)
        self.p = Prog(self.nc)
        self.ins = {}
        self.outs = {}
        self._sb = {}
        self._ps = {}
        self._ctx = {}
        self._sb_stack = []
        self._ps_stack = []

    @property
    def ns(self):
        return self.p.ns

    @ns.setter
    def ns(self, v):
        self.p.ns = v

    def mark(self):
        return (len(self._sb_stack), len(self._ps_stack))

    def free_to_mark(self, mark):
        sbn, psn = mark
        while len(self._ps_stack) > psn:
            self.psum_free(self._ps_stack[-1], prefixed=True)
        while len(self._sb_stack) > sbn:
            self.sbuf_free(self._sb_stack[-1], prefixed=True)

    def din(self, name, shape, dtype):
        t = self.nc.dram_tensor(name, list(shape), dtype, kind="ExternalInput")
        self.ins[name] = t
        return t

    def dout(self, name, shape, dtype):
        t = self.nc.dram_tensor(name, list(shape), dtype, kind="ExternalOutput")
        self.outs[name] = t
        return t

    def io(self, iod, name, shape, dtype, out=False):
        if iod and name in iod:
            return iod[name]
        return (self.dout if out else self.din)(name, shape, dtype)

    def dint(self, name, shape, dtype):
        return self.nc.dram_tensor(self.ns + name, list(shape), dtype)

    def sbuf(self, name, shape, dtype):
        name = self.ns + name
        cm = self.nc.sbuf_tensor(name, list(shape), dtype)
        t = cm.__enter__()
        self._ctx[name] = cm
        self._sb[name] = t
        self._sb_stack.append(name)
        return t

    def sbuf_free_abs(self, name):
        self.sbuf_free(name, prefixed=True)

    def psum_free_abs(self, name):
        self.psum_free(name, prefixed=True)

    def sbuf_free(self, name, prefixed=False):
        if not prefixed:
            name = self.ns + name
        self._ctx.pop(name).__exit__(None, None, None)
        self._sb.pop(name)
        self._sb_stack.remove(name)

    def psum_free(self, name, prefixed=False):
        if not prefixed:
            name = self.ns + name
        self._ctx.pop("psum_" + name).__exit__(None, None, None)
        self._ps.pop(name)
        self._ps_stack.remove(name)

    def psum(self, name, shape, dtype=F32):
        name = self.ns + name
        cm = self.nc.psum_tensor(name, list(shape), dtype)
        t = cm.__enter__()
        self._ctx["psum_" + name] = cm
        self._ps[name] = t
        self._ps_stack.append(name)
        return t

    def run(self, in_maps):
        self.p.emit()
        res = run_bass_kernel_spmd(self.nc, in_maps, core_ids=list(range(NCORES)))
        return res.results


def stage_all(inputs):
    f32 = np.float32
    x = np.asarray(inputs["L0_attn_x"], f32)

    W_act = np.asarray(inputs["L0_attn_W_act"], f32)
    W_rec = np.asarray(inputs["L0_attn_W_rec"], f32)
    W_val = np.asarray(inputs["L0_attn_W_val"], f32)
    in_perm = np.concatenate([PI_P, D + PI_P])
    WactT = np.ascontiguousarray(W_act[XVP][:, in_perm].T)
    WrecT = np.ascontiguousarray(W_rec[PI_P][:, in_perm].T)
    kvmap = np.concatenate([np.arange(0, D, 2), np.arange(1, D, 2)])
    WvalT = np.ascontiguousarray(W_val[PI_P][:, kvmap].T)

    w1T = np.ascontiguousarray(np.asarray(inputs["L0_ffn_w1"], f32)[:, PI_P].T)
    w3T = np.ascontiguousarray(np.asarray(inputs["L0_ffn_w3"], f32)[:, PI_P].T)
    w2T = np.zeros((H0P, D), f32)
    w2T[:H0] = np.asarray(inputs["L0_ffn_w2"], f32)[PI_P].T

    WqT = np.ascontiguousarray(np.asarray(inputs["L1_attn_Wq"], f32)[:, PI_P].T)
    WkT = np.ascontiguousarray(np.asarray(inputs["L1_attn_Wk"], f32)[:, PI_P].T)
    WvT = np.ascontiguousarray(np.asarray(inputs["L1_attn_Wv"], f32)[:, PI_P].T)
    WoT = np.ascontiguousarray(np.asarray(inputs["L1_attn_Wo"], f32)[PI_P].T)

    gup = np.asarray(inputs["L1_ffn_gate_up_packed"], np.int32)
    gus = np.asarray(inputs["L1_ffn_gate_up_scales"])
    dnp = np.asarray(inputs["L1_ffn_down_packed"], np.int32)
    dns = np.asarray(inputs["L1_ffn_down_scales"])
    row_perm_gu = np.concatenate([PII, I1 + PII])
    gupT = np.ascontiguousarray(gup[row_perm_gu].T)
    gusE = np.ascontiguousarray(np.repeat(gus[:, row_perm_gu], 16, axis=0))
    dnpT = np.ascontiguousarray(dnp[PI_P].T)
    dnsE = np.ascontiguousarray(np.repeat(dns[:, PI_P], 16, axis=0))

    common = dict(WactT=WactT, WrecT=WrecT, WvalT=WvalT,
                  w1T=w1T, w3T=w3T, w2T=w2T,
                  WqT=WqT, WkT=WkT, WvT=WvT, WoT=WoT,
                  gupT=gupT, gusE=gusE, dnpT=dnpT, dnsE=dnsE,
                  tri01=np.triu(np.ones((128, 128), np.float32)).astype(ml_dtypes.bfloat16))

    per_core = []
    for c in range(NCORES):
        b, h = c // 2, c % 2
        t0 = h * TL
        xc = np.empty((D, TL + 1), f32)
        xc[:, 1:] = x[b, t0:t0 + TL, :][:, PI_P].T
        xc[:, 0] = x[b, t0 - 1, PI_P] if t0 > 0 else 0.0
        second = (h == 1)
        per_core.append(dict(
            xcat=xc,
            pm=np.full((128, 1), 1.0 if second else 0.0, f32),
            mbias=np.full((128, 1), 0.0 if second else BIGNEG, f32),
            negsib=np.full((128, 1), 0.0 if second else NEG, f32),
            nibmask=np.full((128, 1), np.int32(15) << np.int32(4 * c), np.int32),
            nibinv=np.full((128, 1), float(2.0 ** (-4 * c)), np.float32),
        ))
    return common, per_core


def emit_ln(L, tag, src_ap, src_evt, dst_ap, ntok=TL, rms=False, rms_eps=None,
            post_tile=None, alloc_tag=None, prior_evt=None, dst_lag=2):
    p = L.p
    at = alloc_tag or tag
    ns = L.ns
    nch = ntok // 512
    sq = L._sb.get(ns + at + "_sqbf") or L.sbuf(at + "_sqbf", [128, ntok], BF16)
    if ("psum_" + ns + at + "_pst0") in L._ctx:
        ps_st = [L._ps[ns + at + f"_pst{j}"] for j in range(nch)]
        bc_ps = [L._ps[ns + at + "_bcp0"]] * nch
    else:
        ps_st = [L.psum(at + f"_pst{j}", [64, 512]) for j in range(nch)]
        bc_ps = [L.psum(at + "_bcp0", [128, 512])] * nch
    ps_y = [t[0:1, :] for t in ps_st]
    ps_q = [t[32:33, :] for t in ps_st]
    st = L._sb.get(ns + at + "_st") or L.sbuf(at + "_st", [1, 2 * ntok], F32)
    ab = L._sb.get(ns + at + "_ab") or L.sbuf(at + "_ab", [1, 2 * ntok], F32)
    bca = L._sb.get(ns + at + "_bca") or L.sbuf(at + "_bca", [128, ntok], F32)
    bcb = None
    if not rms:
        bcb = L._sb.get(ns + at + "_bcb") or L.sbuf(at + "_bcb", [128, ntok], F32)
    ones = L._sb.get(ns + "ones_bf") or L.sbuf("ones_bf", [128, 2], BF16)
    onesr = L._sb.get(ns + "onesr_f32") or L.sbuf("onesr_f32", [2, 128], F32)
    epsc = L._sb.get(ns + tag + "_eps") or L.sbuf(tag + "_eps", [1, 1], F32)
    eps0 = rms_eps if rms else 1e-5
    if not L.p.has("ones_init"):
        p.op("vector", lambda e: e.memset(ones[:], 1.0), inc="ones_init")
        p.op("vector", lambda e: e.memset(onesr[:], 1.0), inc="ones_init")
    p.op("vector", lambda e: e.memset(epsc[:], eps0), inc="ones_init")
    ev_ones = p.now("ones_init")

    sv, ss, stm = tag + "_v", tag + "_s", tag + "_t"
    stm_base = p.cnt(stm)
    per_tile_mm = (1 if rms else 2) * nch
    for i in range(NT):
        buf = 0
        war = (stm, stm_base + i * per_tile_mm) if i >= 1 else None
        p.op("scalar", lambda e, i=i, buf=buf: e.activation(
            sq[:, buf:buf + ntok], src_ap(i), AF.Square),
            waits=[src_evt(i), war, prior_evt], inc=ss)
        ev_sq = p.now(ss)
        for j in range(nch):
            p.op("tensor", lambda e, i=i, j=j, buf=buf: e.matmul(
                ps_q[j], ones[:, 0:1], sq[:, buf + j * 512:buf + (j + 1) * 512],
                start=(i == 0), stop=(i == NT - 1)),
                waits=[ev_sq, ev_ones, prior_evt], inc=stm)
        if not rms:
            p.op("scalar", lambda e, i=i, buf=buf: e.activation(
                sq[:, buf:buf + ntok], src_ap(i), AF.Identity),
                waits=[p.now(stm), src_evt(i)], inc=ss)
            ev_y = p.now(ss)
            for j in range(nch):
                p.op("tensor", lambda e, i=i, j=j, buf=buf: e.matmul(
                    ps_y[j], ones[:, 0:1],
                    sq[:, buf + j * 512:buf + (j + 1) * 512],
                    start=(i == 0), stop=(i == NT - 1)), waits=[ev_y],
                    inc=stm)
    ev_stats = p.now(stm)
    for j in range(nch):
        p.op("vector", lambda e, j=j: e.tensor_copy(
            st[0:1, ntok + j * 512:ntok + (j + 1) * 512], ps_q[j]),
            waits=[ev_stats], inc=sv)
        if not rms:
            p.op("vector", lambda e, j=j: e.tensor_copy(
                st[0:1, j * 512:(j + 1) * 512], ps_y[j]),
                waits=[ev_stats], inc=sv)
    ev_st = p.now(sv)
    n = ntok
    if rms:
        p.op("scalar", lambda e: e.activation(ab[0:1, 0:n], st[0:1, n:2 * n],
                                              AF.Sqrt, scale=1.0 / D,
                                              bias=epsc[0:1, 0:1]),
             waits=[ev_st, ev_ones], inc=ss)
        p.op("vector", lambda e: e.reciprocal(ab[0:1, 0:n], ab[0:1, 0:n]),
             waits=[p.now(ss)], inc=sv)
    else:
        mu = bca[0:1, 0:n]
        tmp = bcb[0:1, 0:n]
        p.op("vector", lambda e: e.tensor_scalar(mu, st[0:1, 0:n], 1.0 / D,
                                                 None, OP.mult),
             waits=[ev_st], inc=sv)
        p.op("scalar", lambda e: e.activation(tmp, mu, AF.Square),
             waits=[p.now(sv)], inc=ss)
        p.op("vector", lambda e: e.scalar_tensor_tensor(
            tmp, st[0:1, n:2 * n], 1.0 / D, tmp, OP.mult, OP.subtract),
            waits=[p.now(ss), ev_st], inc=sv)
        p.op("scalar", lambda e: e.activation(tmp, tmp, AF.Sqrt,
                                              bias=epsc[0:1, 0:1]),
             waits=[p.now(sv), ev_ones], inc=ss)
        p.op("vector", lambda e: e.reciprocal(ab[0:1, 0:n], tmp),
             waits=[p.now(ss)], inc=sv)
        p.op("vector", lambda e: e.scalar_tensor_tensor(
            ab[0:1, n:2 * n], mu, -1.0, ab[0:1, 0:n],
            OP.mult, OP.mult),
            waits=[p.now(sv)], inc=sv)
    ev_ab = [p.now(sv), p.now(ss)]
    for j in range(nch):
        p.op("tensor", lambda e, j=j: e.matmul(
            bc_ps[j][:, :], onesr[0:1, :], ab[0:1, j * 512:(j + 1) * 512],
            start=True, stop=True),
            waits=ev_ab + [ev_stats, prior_evt, p.now(sv)], inc=stm)
        p.op("vector", lambda e, j=j: e.tensor_copy(bca[:, j * 512:(j + 1) * 512],
                                                    bc_ps[j][:, :]),
             waits=[p.now(stm)], inc=sv)
        if not rms:
            p.op("tensor", lambda e, j=j: e.matmul(
                bc_ps[j][:, :], onesr[0:1, :], ab[0:1, n + j * 512:n + (j + 1) * 512],
                start=True, stop=True), waits=ev_ab + [p.now(sv)], inc=stm)
            p.op("vector", lambda e, j=j: e.tensor_copy(bcb[:, j * 512:(j + 1) * 512],
                                                        bc_ps[j][:, :]),
                 waits=[p.now(stm)], inc=sv)
    ev_bc = p.now(sv)
    done = []
    post_evts = {}
    for i in range(NT):
        war = post_evts.get(i - dst_lag)
        p.op("vector", lambda e, i=i: e.tensor_mul(dst_ap(i), src_ap(i), bca[:]),
             waits=[ev_bc, src_evt(i), war], inc=sv)
        if not rms:
            p.op("vector", lambda e, i=i: e.tensor_add(dst_ap(i), dst_ap(i), bcb[:]),
                 waits=[p.now(sv)], inc=sv)
        done.append(p.now(sv))
        if post_tile is not None:
            post_evts[i] = post_tile(i, done[-1])
    return done


class WStream:

    def __init__(self, L, tag, max_k, strip_cols=128, need_f32=True,
                 f32_bufs=2):
        self.L, self.p, self.tag = L, L.p, tag
        self.maxw = max_k * strip_cols
        self.f32_bufs = f32_bufs
        self.wf = (L.sbuf(tag + "_wf", [128, f32_bufs * self.maxw], F32)
                   if need_f32 else None)
        self.wb = L.sbuf(tag + "_wb", [128, 2 * self.maxw], BF16)
        self.pss = [L.psum(tag + f"_ps{i}", [128, 512]) for i in range(4)]
        self.sd = [tag + "_d0", tag + "_d1"]
        self.sv, self.stm = tag + "_v", tag + "_m"
        self.mi = 0
        self.dmas = [0, 0]
        self.casts = 0
        self.mms = {}
        self.evac_evts = {}

    def job(self, wdram, Ktiles, m_list, rhs_ap, rhs_evt, evac_fn,
            wdtype=F32, strip_cols=128, queue="sync", extra_strip_wait=None,
            k_off=0, k_count=None):
        p, L = self.p, self.L
        nK = Ktiles if k_count is None else k_count
        needs_cast = (wdtype == F32)
        nch = TL // 512
        for m in m_list:
            mi = self.mi
            self.mi += 1
            sl = (mi % 2) * self.maxw
            mw = min(strip_cols, wdram.shape[1] - m * strip_cols)
            src = wdram.ap()[k_off * 128:(k_off + nK) * 128,
                             m * strip_cols:m * strip_cols + mw]
            src = src.rearrange("(k p) f -> p k f", p=128)
            par = mi % 2
            if needs_cast:
                fpar = mi % self.f32_bufs
                slf = fpar * self.maxw
                nb = self.f32_bufs
                war = [(self.sv, self.casts - (nb - 1))] if mi >= nb else []
                p.op(queue, lambda e, src=src, dst=self.wf[:, slf:slf + nK * mw]
                     .rearrange("p (k f) -> p k f", f=mw):
                     e.dma_start(out=dst, in_=src),
                     waits=war + ([extra_strip_wait] if extra_strip_wait else []),
                     inc=self.sd[par], inc_by=16)
                self.dmas[par] += 1
                p.op("vector", lambda e, sl=sl, slf=slf, mw=mw, nK=nK:
                     e.tensor_copy(self.wb[:, sl:sl + nK * mw],
                                   self.wf[:, slf:slf + nK * mw]),
                     waits=[(self.sd[par], 16 * self.dmas[par]),
                            (self.stm, self.mms.get(mi - 2, 0))],
                     inc=self.sv)
                self.casts += 1
                ev_w = (self.sv, self.casts)
            else:
                war = [(self.stm, self.mms.get(mi - 2, 0))]
                p.op(queue, lambda e, src=src, dst=self.wb[:, sl:sl + nK * mw]
                     .rearrange("p (k f) -> p k f", f=mw):
                     e.dma_start(out=dst, in_=src),
                     waits=war + ([extra_strip_wait] if extra_strip_wait else []),
                     inc=self.sd[par], inc_by=16)
                self.dmas[par] += 1
                ev_w = (self.sd[par], 16 * self.dmas[par])
            ps_pair = (self.pss[0], self.pss[1]) if mi % 2 == 0 else \
                      (self.pss[2], self.pss[3])
            ev_prev_evac = self.evac_evts.get(mi - 2)
            for j in range(nch):
                for k in range(nK):
                    p.op("tensor", lambda e, sl=sl, mw=mw, j=j, k=k, ps=ps_pair[j]:
                         e.matmul(ps[0:mw, :],
                                  self.wb[:, sl + k * mw:sl + (k + 1) * mw],
                                  rhs_ap(k, j * 512, (j + 1) * 512),
                                  start=(k == 0), stop=(k == nK - 1)),
                         waits=[ev_w, rhs_evt, ev_prev_evac] if k == 0
                         else [ev_w, rhs_evt],
                         inc=self.stm)
            self.mms[mi] = p.cnt(self.stm)
            ev_mm = (self.stm, self.mms[mi])
            self.evac_evts[mi] = evac_fn(m, mi, ps_pair, ev_mm, mw)
        return self.evac_evts

    def free(self):
        for i in (3, 2, 1, 0):
            self.L.psum_free(self.tag + f"_ps{i}")
        self.L.sbuf_free(self.tag + "_wb")
        if self.wf is not None:
            self.L.sbuf_free(self.tag + "_wf")


def build_l0a(L=None, io=None, part2=None):
    L = L or Launch("l0a")
    p = L.p
    xcat = L.io(io, "xcat", [D, TL + 1], F32)
    WactT = L.io(io, "WactT", [2 * D, D], F32)
    WrecT = L.io(io, "WrecT", [2 * D, D], F32)
    xv_o = L.io(io, "xv", [D, TL], F32, out=True)
    r4_o = L.io(io, "r4", [D, TL], BF16, out=True)

    xs = L.sbuf("xs", [128, 2 * (TL + 1)], F32)
    xb = L.sbuf("xb", [128, NT * (TL + 1)], BF16)
    ev1 = L.sbuf("ev1", [128, 1024], F32)
    ev2 = L.sbuf("ev2", [128, 1024], F32)
    dst = L.sbuf("dst", [128, 2 * TL], F32)
    dstr = L.sbuf("dstr", [128, 2 * TL], BF16)

    for i in range(NT):
        par = i % 2
        p.op("sync", lambda e, i=i, par=par: e.dma_start(
            out=xs[:, par * (TL + 1):(par + 1) * (TL + 1)],
            in_=xcat.ap()[i * 128:(i + 1) * 128, :]),
            waits=([("xc", i - 1)] if i >= 2 else []),
            inc=f"xd{par}", inc_by=16)
        p.op("vector", lambda e, i=i, par=par: e.tensor_copy(
            xb[:, i * (TL + 1):(i + 1) * (TL + 1)],
            xs[:, par * (TL + 1):(par + 1) * (TL + 1)]),
            waits=[p.now(f"xd{par}")], inc="xc")
    ev_x = p.now("xc")

    def rhs_ap(k, c0, c1):
        base = (k % NT) * (TL + 1) + (0 if k < NT else 1)
        return xb[:, base + c0:base + c1]

    def mk_evac(which):
        def evac(m, mi, ps_pair, ev_mm, mw):
            evs = []
            for j in range(2):
                sl = (m % 2) * TL + j * 512
                buf = ev1 if j == 0 else ev2
                par = m % 2
                if which == "xv":
                    p.op("vector", lambda e, ps=ps_pair[j], buf=buf:
                         e.tensor_scalar(buf[:, 0:512], ps[:], 0.0, None, OP.max),
                         waits=[ev_mm, p.now("xvs")], inc="xvev")
                    p.op("scalar", lambda e, sl=sl, buf=buf: e.activation(
                        dst[:, sl:sl + 512], buf[:, 0:512], AF.Sigmoid),
                        waits=[p.now("xvev"),
                               (f"xvod{par}", 16 * (m // 2)) if m >= 2 else None],
                        inc="xvs")
                    evs.append(p.now("xvs"))
                else:
                    p.op("vector", lambda e, ps=ps_pair[j], buf=buf:
                         e.tensor_scalar(buf[:, 0:512], ps[:], 0.0, None, OP.max),
                         waits=[ev_mm, p.now("r4ev")], inc="r4ev")
                    p.op("scalar", lambda e, buf=buf: e.activation(
                        buf[:, 512:1024], buf[:, 0:512], AF.Sigmoid),
                        waits=[p.now("r4ev")], inc="r4s")
                    p.op("scalar", lambda e, buf=buf: e.activation(
                        buf[:, 0:512], buf[:, 512:1024], AF.Square),
                        waits=[p.now("r4s")], inc="r4s")
                    p.op("vector", lambda e, sl=sl, buf=buf: e.tensor_mul(
                        dstr[:, sl:sl + 512], buf[:, 0:512], buf[:, 0:512]),
                        waits=[p.now("r4s"),
                               (f"r4od{par}", 16 * (m // 2)) if m >= 2 else None],
                        inc="r4ev")
                    evs.append(p.now("r4ev"))
            if which == "xv":
                p.op("gpsimd", lambda e, m=m: e.dma_start(
                    out=xv_o.ap()[m * 128:(m + 1) * 128, :],
                    in_=dst[:, (m % 2) * TL:(m % 2) * TL + TL]),
                    waits=evs, inc=f"xvod{par}", inc_by=16)
                return p.now(f"xvod{par}")
            else:
                p.op("gpsimd", lambda e, m=m: e.dma_start(
                    out=r4_o.ap()[m * 128:(m + 1) * 128, :],
                    in_=dstr[:, (m % 2) * TL:(m % 2) * TL + TL]),
                    waits=evs, inc=f"r4od{par}", inc_by=16)
                return p.now(f"r4od{par}")
        return evac

    ws = WStream(L, "w", 2 * NT, f32_bufs=1)
    ws.job(WactT, 2 * NT, list(range(NT)), rhs_ap, ev_x, mk_evac("xv"))
    if part2 is None:
        ws.job(WrecT, 2 * NT, list(range(NT)), rhs_ap, ev_x, mk_evac("r4"))
        p.op("gpsimd", lambda e: e.engine_nop(),
             waits=[p.now("xvod0"), p.now("xvod1"), p.now("r4od0"),
                    p.now("r4od1")], inc="fin")
        return L
    def do_part2():
        ws.job(WrecT, 2 * NT, list(range(NT)), rhs_ap, ev_x, mk_evac("r4"))
        p.op("gpsimd", lambda e: e.engine_nop(),
             waits=[p.now("xvod0"), p.now("xvod1"), p.now("r4od0"),
                    p.now("r4od1")], inc="fin")
    part2.append(do_part2)
    return L


def build_l0s(L=None, io=None):
    L = L or Launch("l0s")
    p = L.p
    xv_i = L.io(io, "xv", [D, TL], F32)
    pm_i = L.io(io, "pm", [128, 1], F32)
    mb_i = L.io(io, "mbias", [128, 1], F32)
    kv_o = L.io(io, "kv", [D, TL], BF16, out=True)

    snd = L.dint("snd", [128, 24], F32)
    agg = L.dint("agg", [256, 24], F32)

    sb = L.sbuf("sb", [128, NT * TL], F32)
    A = L.sbuf("A", [128, TL], F32)
    Bu = L.sbuf("Bu", [128, TL], F32)
    C = L.sbuf("C", [128, TL], F32)
    Dd = L.sbuf("Dd", [128, TL], F32)
    Ii = L.sbuf("Ii", [128, TL], I32)
    kv = L.sbuf("kv_sb", [128, 2 * TL], BF16)
    sendb = L.sbuf("sendb", [128, 24], F32)
    sibb = L.sbuf("sibb", [128, 24], F32)
    inib = L.sbuf("inib", [128, 24], F32)
    pm = L.sbuf("pm_sb", [128, 1], F32)
    mb = L.sbuf("mb_sb", [128, 1], F32)

    p.op("sync", lambda e: e.dma_start(
        out=sb[:].rearrange("p (i f) -> p i f", f=TL),
        in_=xv_i.ap().rearrange("(i p) f -> p i f", p=128)), inc="xd", inc_by=16)
    p.op("sync", lambda e: e.dma_start(out=pm[:], in_=pm_i.ap()), inc="xd", inc_by=16)
    p.op("sync", lambda e: e.dma_start(out=mb[:], in_=mb_i.ap()), inc="xd", inc_by=16)
    ev_x = p.now("xd")

    def sl(i):
        return sb[:, i * TL:(i + 1) * TL]

    sv, ss = "sv", "ss"
    Ee = L.sbuf("Ee", [128, TL], F32)
    for t in range(8):
        re, im = sl(t), sl(8 + t)
        p.op("scalar", lambda e, re=re: e.activation(A[:], re, AF.Square),
             waits=[ev_x, p.now(sv)], inc=ss)
        p.op("scalar", lambda e, im=im: e.activation(Bu[:], im, AF.Square),
             waits=[p.now(ss)], inc=ss)
        p.op("vector", lambda e: e.tensor_add(A[:], A[:], Bu[:]),
             waits=[p.now(ss)], inc=sv)
        p.op("scalar", lambda e: e.activation(Ee[:], A[:], AF.Sqrt),
             waits=[p.now(sv)], inc=ss)
        p.op("vector", lambda e, re=re: e.tensor_add(Ee[:], Ee[:], re),
             waits=[p.now(ss)], inc=sv)
        p.op("scalar", lambda e: e.activation(C[:], Ee[:], AF.Ln),
             waits=[p.now(sv)], inc=ss)
        p.op("scalar", lambda e: e.activation(C[:], C[:], AF.Exp, scale=-1.0),
             waits=[p.now(ss)], inc=ss)
        p.op("scalar", lambda e, re=re: e.activation(re, A[:], AF.Ln),
             waits=[p.now(ss), p.now(sv)], inc=ss)
        ev_u2 = p.now(ss)
        p.op("vector", lambda e, im=im: e.tensor_mul(C[:], im, C[:]),
             waits=[ev_u2], inc=sv)
        p.op("scalar", lambda e: e.activation(C[:], C[:], AF.Arctan),
             waits=[p.now(sv)], inc=ss)
        p.op("vector", lambda e, im=im: e.tensor_tensor_scan(
            im, C[:], C[:], 0.0, OP.add, OP.bypass),
            waits=[p.now(ss)], inc=sv)
        p.op("vector", lambda e, re=re: e.tensor_tensor_scan(
            Bu[:], re, re, BIGNEG, OP.max, OP.max),
            waits=[p.now(sv), ev_u2], inc=sv)
        p.op("vector", lambda e, t=t: e.tensor_copy(sendb[:, t:t + 1],
                                                    Bu[:, TL - 1:TL]),
             waits=[p.now(sv)], inc=sv)
        p.op("vector", lambda e, re=re: e.tensor_sub(A[:], re, Bu[:]),
             waits=[p.now(sv)], inc=sv)
        p.op("vector", lambda e, t=t: e.reduce_sum(sendb[:, 8 + t:9 + t], A[:], axis=AX),
             waits=[p.now(sv)], inc=sv)
        p.op("vector", lambda e, t=t, im=im: e.tensor_copy(sendb[:, 16 + t:17 + t],
                                                           im[:, TL - 1:TL]),
             waits=[p.now(sv)], inc=sv)
    ev_p1 = p.now(sv)
    p.op("gpsimd", lambda e: e.dma_start(out=snd.ap(), in_=sendb[:]),
         waits=[ev_p1], inc="cd", inc_by=16)
    p.op("gpsimd", lambda e: e.collective_compute(
        "AllGather", OP.bypass,
        replica_groups=[[2 * i, 2 * i + 1] for i in range(4)],
        ins=[snd.ap().opt()], outs=[agg.ap().opt()]),
        waits=[p.now("cd")], inc="cc")
    p.op("gpsimd", lambda e: e.dma_start(out=sibb[:], in_=agg.ap()[0:128, :]),
         waits=[p.now("cc")], inc="cd", inc_by=16)
    ev_sib = p.now("cd")
    p.op("vector", lambda e: e.tensor_scalar(inib[:, 0:8], sibb[:, 0:8],
                                             pm[:, 0:1], None, OP.mult),
         waits=[ev_sib, ev_x], inc=sv)
    p.op("vector", lambda e: e.tensor_scalar(inib[:, 0:8], inib[:, 0:8],
                                             mb[:, 0:1], None, OP.add),
         waits=[p.now(sv)], inc=sv)
    p.op("vector", lambda e: e.tensor_scalar(inib[:, 8:24], sibb[:, 8:24],
                                             pm[:, 0:1], None, OP.mult),
         waits=[p.now(sv), ev_sib], inc=sv)
    ev_ini = p.now(sv)
    for t in range(8):
        u2, Th = sl(t), sl(8 + t)
        p.op("vector", lambda e, u2=u2, t=t: e.tensor_tensor_scan(
            A[:], u2, u2, inib[:, t:t + 1], OP.max, OP.max),
            waits=[ev_ini, p.now(ss)], inc=sv)
        p.op("vector", lambda e, u2=u2: e.tensor_sub(Bu[:], u2, A[:]),
             waits=[p.now(sv)], inc=sv)
        p.op("vector", lambda e, t=t: e.tensor_tensor_scan(
            A[:], Bu[:], Bu[:], inib[:, 8 + t:9 + t], OP.add, OP.bypass),
            waits=[p.now(sv)], inc=sv)
        p.op("scalar", lambda e: e.activation(Bu[:], A[:], AF.Exp, scale=0.5),
             waits=[p.now(sv)], inc=ss)
        p.op("vector", lambda e, Th=Th, t=t: e.tensor_scalar(
            C[:], Th, inib[:, 16 + t:17 + t], 2.0, OP.add, OP.mult),
            waits=[p.now(ss), ev_ini], inc=sv)
        def rr_sin(theta_ap, out_tile_off):
            p.op("vector", lambda e: e.tensor_scalar(A[:], theta_ap, INV2PI, None,
                                                     OP.mult),
                 waits=[p.now(sv)], inc=sv)
            p.op("vector", lambda e: e.tensor_copy(Ii[:], A[:]),
                 waits=[p.now(sv)], inc=sv)
            p.op("vector", lambda e: e.tensor_copy(A[:], Ii[:]),
                 waits=[p.now(sv)], inc=sv)
            p.op("vector", lambda e: e.scalar_tensor_tensor(
                A[:], A[:], -TWOPI, theta_ap, OP.mult, OP.add),
                waits=[p.now(sv)], inc=sv)
            p.op("vector", lambda e: e.tensor_scalar(Dd[:], A[:], float(np.pi),
                                                     None, OP.is_gt),
                 waits=[p.now(sv)], inc=sv)
            p.op("vector", lambda e: e.tensor_scalar(Ee[:], A[:], float(-np.pi),
                                                     None, OP.is_lt),
                 waits=[p.now(sv)], inc=sv)
            p.op("vector", lambda e: e.tensor_sub(Dd[:], Dd[:], Ee[:]),
                 waits=[p.now(sv)], inc=sv)
            p.op("vector", lambda e: e.scalar_tensor_tensor(
                A[:], Dd[:], -TWOPI, A[:], OP.mult, OP.add),
                waits=[p.now(sv)], inc=sv)
            p.op("scalar", lambda e: e.activation(Dd[:], A[:], AF.Sin),
                 waits=[p.now(sv)], inc=ss)
            p.op("vector", lambda e, o=out_tile_off: e.tensor_mul(
                kv[:, o:o + TL], Bu[:], Dd[:]), waits=[p.now(ss)], inc=sv)
        kwar = ("kod", 16 * 2 * t) if t >= 1 else None
        p.op("vector", lambda e: e.engine_nop(), waits=[kwar], inc=sv)
        rr_sin(C[:], TL)
        p.op("vector", lambda e: e.tensor_scalar(C[:], C[:], float(np.pi / 2), None,
                                                 OP.add),
             waits=[p.now(sv)], inc=sv)
        rr_sin(C[:], 0)
        ev_t = p.now(sv)
        p.op("gpsimd", lambda e, t=t: e.dma_start(
            out=kv_o.ap()[t * 128:(t + 1) * 128, :],
            in_=kv[:, 0:TL]),
            waits=[ev_t], inc="kod", inc_by=16)
        p.op("gpsimd", lambda e, t=t: e.dma_start(
            out=kv_o.ap()[(8 + t) * 128:(9 + t) * 128, :],
            in_=kv[:, TL:2 * TL]),
            waits=[ev_t], inc="kod", inc_by=16)
    p.op("gpsimd", lambda e: e.engine_nop(), waits=[p.now("kod")], inc="fin")
    return L


def build_l0v(L=None, io=None):
    L = L or Launch("l0v")
    p = L.p
    kv_i = L.io(io, "kv", [D, TL], BF16)
    r4_i = L.io(io, "r4", [D, TL], BF16)
    x_i = L.io(io, "xcat", [D, TL + 1], F32)
    WvalT = L.io(io, "WvalT", [D, D], F32)
    x1_o = L.io(io, "x1", [D, TL], F32, out=True)

    kv = L.sbuf("kv_sb", [128, NT * TL], BF16)
    r4 = L.sbuf("r4_sb", [128, 2 * TL], BF16)
    xstr = L.sbuf("xstr", [128, 2 * TL], F32)
    ysb = L.sbuf("ysb", [128, NT * TL], F32)
    dst = L.sbuf("dst", [128, 2 * TL], F32)

    p.op("sync", lambda e: e.dma_start(
        out=kv[:].rearrange("p (i f) -> p i f", f=TL),
        in_=kv_i.ap().rearrange("(i p) f -> p i f", p=128)), inc="xd", inc_by=16)
    ev_x = p.now("xd")
    xld = [0, 0]
    ev_xt = {}

    def load_x(m, war_evt):
        par = m % 2
        p.op("scalar", lambda e, m=m: e.dma_start(
            out=xstr[:, par * TL:(par + 1) * TL],
            in_=x_i.ap()[m * 128:(m + 1) * 128, 1:]),
            waits=[war_evt], inc=f"xld{par}", inc_by=16)
        p.op("scalar", lambda e, m=m: e.dma_start(
            out=r4[:, par * TL:(par + 1) * TL],
            in_=r4_i.ap()[m * 128:(m + 1) * 128, :]),
            waits=[war_evt], inc=f"xld{par}", inc_by=16)
        xld[par] += 2
        ev_xt[m] = (f"xld{par}", 16 * xld[par])

    def rhs_ap(k, c0, c1):
        return kv[:, k * TL + c0:k * TL + c1]

    ev_y = {}

    def evac(m, mi, ps_pair, ev_mm, mw):
        if m not in ev_xt:
            load_x(m, ev_y.get(m - 2))
        evs = []
        par = m % 2
        for j in range(2):
            p.op("vector", lambda e, m=m, j=j, par=par, ps=ps_pair[j]:
                 e.scalar_tensor_tensor(
                     ysb[:, m * TL + j * 512:m * TL + (j + 1) * 512],
                     ps[:], 1.0, r4[:, par * TL + j * 512:par * TL + (j + 1) * 512],
                     OP.mult, OP.mult),
                 waits=[ev_mm, ev_x, ev_xt[m]], inc="yev")
            p.op("vector", lambda e, m=m, j=j, par=par: e.tensor_add(
                ysb[:, m * TL + j * 512:m * TL + (j + 1) * 512],
                ysb[:, m * TL + j * 512:m * TL + (j + 1) * 512],
                xstr[:, par * TL + j * 512:par * TL + (j + 1) * 512]),
                waits=[p.now("yev"), ev_xt[m]], inc="yev")
            evs.append(p.now("yev"))
        ev_y[m] = evs[-1]
        return evs[-1]

    ws = WStream(L, "wv", NT)
    ws.job(WvalT, NT, list(range(NT)), rhs_ap, ev_x, evac)

    def post_tile(i, done_evt):
        par = i % 2
        p.op("gpsimd", lambda e, i=i: e.dma_start(
            out=x1_o.ap()[i * 128:(i + 1) * 128, :],
            in_=dst[:, (i % 2) * TL:(i % 2) * TL + TL]),
            waits=[done_evt], inc=f"od{par}", inc_by=16)
        return p.now(f"od{par}")

    emit_ln(L, "ln", lambda i: ysb[:, i * TL:(i + 1) * TL],
            lambda i: ev_y[i],
            lambda i: dst[:, (i % 2) * TL:(i % 2) * TL + TL],
            post_tile=post_tile)
    p.op("gpsimd", lambda e: e.engine_nop(),
         waits=[p.now("od0"), p.now("od1")], inc="fin")
    return L


def build_ffn0(L=None, io=None, pre_out=None, pre=None):
    L = L or Launch("ffn0")
    p = L.p
    x1_i = L.io(io, "x1", [D, TL], F32)
    w1sh = L.io(io, "w1sh", [256, H0], F32)
    w3sh = L.io(io, "w3sh", [256, H0], F32)
    w2sh = L.io(io, "w2sh", [688, D], F32)
    x2_o = L.io(io, "x2", [D, TL], F32, out=True)

    q1sh_d = L.dint("q1sh_d", [256, H0], BF16)
    q3sh_d = L.dint("q3sh_d", [256, H0], BF16)
    q2sh_d = L.dint("q2sh_d", [688, D], BF16)
    q1_d = L.dint("q1_d", [D, H0], BF16)
    q3_d = L.dint("q3_d", [D, H0], BF16)
    q2_d = L.dint("q2_d", [H0P, D], BF16)
    pp_d = L.dint("pp_d", [128, 4], F32)
    ssum_d = L.dint("ssum_d", [1, 4], F32)
    sred_d = L.dint("sred_d", [1, 4], F32)

    if pre is not None:
        xbf = L.sbuf("xbf", [128, NT * TL], BF16)
        ev_ag = pre["ev_ag"]
        q1_d, q3_d, q2_d = pre["q1_d"], pre["q3_d"], pre["q2_d"]
        ev_q_all = list(pre["ev_q"])
        pf = L.sbuf("pfm", [128, 2 * TL], F32)
        for i in range(NT):
            par = i % 2
            p.op("scalar", lambda e, i=i, par=par: e.dma_start(
                out=pf[:, par * TL:(par + 1) * TL],
                in_=x1_i.ap()[i * 128:(i + 1) * 128, :]),
                waits=([("xc", i - 1)] if i >= 2 else []),
                inc=f"xd{par}", inc_by=16)
            p.op("vector", lambda e, i=i, par=par: e.tensor_copy(
                xbf[:, i * TL:(i + 1) * TL], pf[:, par * TL:(par + 1) * TL]),
                waits=[p.now(f"xd{par}")], inc="xc")
        ev_xbf = p.now("xc")
        L.sbuf_free("pfm")
        return _ffn0_main(L, io, x1_i, x2_o, xbf, ev_xbf, ev_ag,
                          q1_d, q3_d, q2_d)
    if pre_out is None:
        xbf = L.sbuf("xbf", [128, NT * TL], BF16)
    acc = L.sbuf("acc", [128, 8], F32)
    red1 = L.sbuf("red1", [128, 64], F32)
    srow = L.sbuf("srow", [1, 1024], F32)
    scb = L.sbuf("scb", [128, 16], F32)
    onesq = L.sbuf("onesq", [2, 128], F32)
    PFW = (H0 // 4) + 3
    pf = L.sbuf("pf", [128, 2 * PFW], F32)
    qb = L.sbuf("qb", [128, 2 * PFW], BF16)
    ps_b = L.psum("ps_b", [128, 16])
    pf_name, qb_name, psb_name = L.ns + "pf", L.ns + "qb", L.ns + "ps_b"

    p.op("vector", lambda e: e.memset(onesq[:], 1.0), inc="ini")
    p.op("vector", lambda e: e.memset(acc[:], 0.0), inc="ini")
    p.op("vector", lambda e: e.memset(srow[0:1, 528:544], 0.0), inc="ini")
    ev_ini = p.now("ini")


    jobs = []
    for (wsh_, qsh_, mat_, rows_, colw_) in ((w1sh, q1sh_d, 0, 256, H0),
                                             (w3sh, q3sh_d, 1, 256, H0),
                                             (w2sh, q2sh_d, 2, 688, D)):
        q4 = colw_ // 4
        csp = [(0, q4), (q4, q4), (2 * q4, q4), (3 * q4, colw_ - 3 * q4)]
        for r0_ in range(0, rows_, 128):
            nr_ = min(128, rows_ - r0_)
            for (c0_, cw_) in csp:
                jobs.append((wsh_, qsh_, r0_, nr_, c0_, cw_, mat_))
    pa = 0
    for (wsh_, qsh_, r0, nr, c0, cw, mat) in jobs:
        par = pa % 2
        sl = par * PFW
        p.op("sync", lambda e, wsh=wsh_, r0=r0, nr=nr, c0=c0, cw=cw, sl=sl:
             e.dma_start(out=pf[0:nr, sl:sl + cw],
                         in_=wsh.ap()[r0:r0 + nr, c0:c0 + cw]),
             waits=[("pav", 2 * (pa - 1)) if pa >= 2 else None],
             inc=f"pad{par}", inc_by=16)
        nd = p.cnt(f"pad{par}")
        p.op("vector", lambda e, nr=nr, sl=sl, cw=cw:
             e.reduce_sum(red1[0:nr, 63:64], pf[0:nr, sl:sl + cw],
                          axis=AX, apply_absolute_value=True),
             waits=[(f"pad{par}", nd), p.now("pav"), ev_ini], inc="pav")
        p.op("vector", lambda e, nr=nr, mat=mat: e.tensor_add(
            acc[0:nr, mat:mat + 1], acc[0:nr, mat:mat + 1], red1[0:nr, 63:64]),
            waits=[p.now("pav")], inc="pav")
        pa += 1
    ev_acc = p.now("pav")
    p.op("gpsimd", lambda e: e.dma_start(out=pp_d.ap()[:, 0:4], in_=acc[:, 0:4]),
         waits=[ev_acc], inc="ppd", inc_by=16)
    p.op("gpsimd", lambda e: e.dma_start(
        out=srow[0:1, 0:512].rearrange("q (c p) -> q c p", p=128),
        in_=pp_d.ap().rearrange("p c -> () c p"),
        allow_slow_non_contiguous=True),
        waits=[p.now("ppd")], inc="ppe", inc_by=16)
    p.op("vector", lambda e: e.tensor_reduce(
        srow[0:1, 512:516], srow[0:1, 0:512].rearrange("q (c p) -> q c p", p=128),
        axis=AX, op=OP.add),
        waits=[p.now("ppe")], inc="pav")
    p.op("gpsimd", lambda e: e.dma_start(out=ssum_d.ap(), in_=srow[0:1, 512:516]),
         waits=[p.now("pav")], inc="ppf", inc_by=16)
    p.op("gpsimd", lambda e: e.collective_compute(
        "AllReduce", OP.add, replica_groups=[list(range(NCORES))],
        ins=[ssum_d.ap().opt()], outs=[sred_d.ap().opt()]),
        waits=[p.now("ppf")], inc="cc")
    p.op("gpsimd", lambda e: e.dma_start(out=srow[0:1, 520:524], in_=sred_d.ap()),
         waits=[p.now("cc")], inc="ppg", inc_by=16)
    ev_tot = p.now("ppg")
    CN1 = float(H0 * D)
    for mat, cnt in ((0, CN1), (1, CN1), (2, CN1)):
        p.op("vector", lambda e, mat=mat, cnt=cnt: e.tensor_scalar(
            srow[0:1, 528 + mat:529 + mat], srow[0:1, 520 + mat:521 + mat],
            0.5 / cnt, None, OP.mult), waits=[ev_tot, ev_ini, p.now("pav")], inc="pav")
        p.op("vector", lambda e, mat=mat, cnt=cnt: e.tensor_scalar(
            srow[0:1, 531 + mat:532 + mat], srow[0:1, 520 + mat:521 + mat],
            1.0 / cnt, None, OP.mult), waits=[ev_tot, ev_ini, p.now("pav")], inc="pav")
        p.op("vector", lambda e, mat=mat, cnt=cnt: e.tensor_scalar(
            srow[0:1, 534 + mat:535 + mat], srow[0:1, 520 + mat:521 + mat],
            -0.5 / cnt, None, OP.mult), waits=[ev_tot, ev_ini, p.now("pav")], inc="pav")
    ev_svals = p.now("pav")
    p.op("tensor", lambda e: e.matmul(ps_b[:, 0:16], onesq[0:1, :],
                                      srow[0:1, 528:544], start=True, stop=True),
         waits=[ev_svals, ev_ini], inc="pbm")
    p.op("vector", lambda e: e.tensor_copy(scb[:], ps_b[:, 0:16]),
         waits=[p.now("pbm")], inc="pav")
    ev_scb = p.now("pav")

    pb = 0
    qd_evts = {}
    for (wsh_, qsh_, r0, nr, c0, cw, mat) in jobs:
        par = pb % 2
        sl = par * PFW
        p.op("scalar", lambda e, wsh=wsh_, r0=r0, nr=nr, c0=c0, cw=cw, sl=sl:
             e.dma_start(out=pf[0:nr, sl:sl + cw],
                         in_=wsh.ap()[r0:r0 + nr, c0:c0 + cw]),
             waits=[("qav", 3 * (pb - 1)) if pb >= 2 else None,
                    ev_acc],
             inc=f"qbd{par}", inc_by=16)
        nd = p.cnt(f"qbd{par}")
        p.op("vector", lambda e, nr=nr, sl=sl, cw=cw, mat=mat:
             e.tensor_scalar(qb[0:nr, sl:sl + cw], pf[0:nr, sl:sl + cw],
                             scb[0:nr, 0 + mat:1 + mat],
                             scb[0:nr, 3 + mat:4 + mat],
                             OP.is_gt, OP.mult),
             waits=[(f"qbd{par}", nd), ev_scb, p.now("qav"),
                    qd_evts.get(pb - 2)], inc="qav")
        p.op("vector", lambda e, nr=nr, sl=sl, cw=cw, mat=mat:
             e.tensor_scalar(pf[0:nr, sl:sl + cw], pf[0:nr, sl:sl + cw],
                             scb[0:nr, 6 + mat:7 + mat],
                             scb[0:nr, 3 + mat:4 + mat],
                             OP.is_lt, OP.mult),
             waits=[p.now("qav")], inc="qav")
        p.op("vector", lambda e, nr=nr, sl=sl, cw=cw:
             e.tensor_sub(qb[0:nr, sl:sl + cw], qb[0:nr, sl:sl + cw],
                          pf[0:nr, sl:sl + cw]),
             waits=[p.now("qav")], inc="qav")
        p.op("gpsimd", lambda e, qsh=qsh_, r0=r0, nr=nr, c0=c0, cw=cw, sl=sl:
             e.dma_start(out=qsh.ap()[r0:r0 + nr, c0:c0 + cw],
                         in_=qb[0:nr, sl:sl + cw]),
             waits=[p.now("qav")], inc=f"qod{par}", inc_by=16)
        qd_evts[pb] = p.now(f"qod{par}")
        pb += 1
    ev_qout = [p.now("qod0"), p.now("qod1")]
    p.op("gpsimd", lambda e: e.collective_compute(
        "AllGather", OP.bypass, replica_groups=[list(range(NCORES))],
        ins=[q1sh_d.ap().opt()], outs=[q1_d.ap().opt()]),
        waits=ev_qout, inc="cc")
    p.op("gpsimd", lambda e: e.collective_compute(
        "AllGather", OP.bypass, replica_groups=[list(range(NCORES))],
        ins=[q3sh_d.ap().opt()], outs=[q3_d.ap().opt()]),
        waits=[p.now("cc")], inc="cc")
    p.op("gpsimd", lambda e: e.collective_compute(
        "AllGather", OP.bypass, replica_groups=[list(range(NCORES))],
        ins=[q2sh_d.ap().opt()], outs=[q2_d.ap().opt()]),
        waits=[p.now("cc")], inc="cc")
    ev_ag = p.now("cc")
    if pre_out is not None:
        pre_out.update(dict(ev_ag=ev_ag, q1_d=q1_d, q3_d=q3_d, q2_d=q2_d,
                            ev_q=ev_qout + [p.now("qav")],
                            free=lambda: (L.sbuf_free_abs(qb_name),
                                          L.sbuf_free_abs(pf_name),
                                          L.psum_free_abs(psb_name))))
        return L

    ev_q_all = ev_qout + [p.now("qav")]
    for i in range(NT):
        par = i % 2
        p.op("scalar", lambda e, i=i, par=par: e.dma_start(
            out=pf[:, par * TL:(par + 1) * TL],
            in_=x1_i.ap()[i * 128:(i + 1) * 128, :]),
            waits=ev_q_all + ([("xc", i - 1)] if i >= 2 else []),
            inc=f"xd{par}", inc_by=16)
        p.op("vector", lambda e, i=i, par=par: e.tensor_copy(
            xbf[:, i * TL:(i + 1) * TL], pf[:, par * TL:(par + 1) * TL]),
            waits=[p.now(f"xd{par}")], inc="xc")
    ev_xbf = p.now("xc")

    L.sbuf_free("qb")
    L.sbuf_free("pf")
    L.psum_free("ps_b")
    return _ffn0_main(L, io, x1_i, x2_o, xbf, ev_xbf, ev_ag, q1_d, q3_d, q2_d)


def _ffn0_main(L, io, x1_i, x2_o, xbf, ev_xbf, ev_ag, q1_d, q3_d, q2_d):
    p = L.p
    hsb = L.sbuf("hsb", [128, 16 * TL], BF16)
    ysb = L.sbuf("ysb", [128, NT * TL], F32)
    hA = L.sbuf("hA", [128, TL], F32)
    x1str = L.sbuf("x1str", [128, TL], F32)
    dst = L.sbuf("dst", [128, TL], F32)

    p.op("vector", lambda e: e.memset(hsb[:, 10 * TL:11 * TL], 0.0),
         waits=[ev_xbf], inc="hi")
    ev_hz = [p.now("hi")]

    ws = WStream(L, "f0", 16, need_f32=False)

    def rhs_x(k, c0, c1):
        return xbf[:, k * TL + c0:k * TL + c1]

    rounds = [(0, 16), (16, 32), (32, 43)]
    ev_y = {}
    xld = [0, 0]
    ev_xt = {}

    def load_x1(m, war_evt):
        p.op("scalar", lambda e, m=m: e.dma_start(
            out=x1str[:, 0:TL],
            in_=x1_i.ap()[m * 128:(m + 1) * 128, :]),
            waits=[war_evt], inc="x1ld0", inc_by=16)
        xld[0] += 1
        ev_xt[m] = ("x1ld0", 16 * xld[0])

    for ri, (j0, j1) in enumerate(rounds):
        hevts = {}
        for j in range(j0, j1):
            slot = j - j0
            def evac_gate(m, mi, ps_pair, ev_mm, mw, slot=slot):
                evs = []
                for jj in range(2):
                    p.op("scalar", lambda e, jj=jj, ps=ps_pair[jj], mw=mw:
                         e.activation(hA[0:mw, jj * 512:(jj + 1) * 512],
                                      ps[0:mw, :], AF.Silu),
                         waits=[ev_mm, ("hi", p.cnt("hi"))], inc="hs")
                    evs.append(p.now("hs"))
                return evs[-1]

            ws.job(q1_d, NT, [j], rhs_x, ev_xbf, evac_gate, wdtype=BF16,
                   extra_strip_wait=ev_ag)
            ev_g = p.now("hs")

            def evac_up(m, mi, ps_pair, ev_mm, mw, slot=slot, ev_g=ev_g):
                evs = []
                for jj in range(2):
                    p.op("vector", lambda e, jj=jj, ps=ps_pair[jj], slot=slot, mw=mw:
                         e.tensor_mul(hsb[0:mw, slot * TL + jj * 512:
                                          slot * TL + (jj + 1) * 512],
                                      hA[0:mw, jj * 512:(jj + 1) * 512],
                                      ps[0:mw, :]),
                         waits=[ev_mm, ev_g, ev_hz[0]], inc="hi")
                    evs.append(p.now("hi"))
                return evs[-1]

            ws.job(q3_d, NT, [j], rhs_x, ev_xbf, evac_up, wdtype=BF16,
                   extra_strip_wait=ev_ag)
            hevts[j] = p.now("hi")
        ev_h = p.now("hi")

        def rhs_h(k, c0, c1):
            return hsb[:, k * TL + c0:k * TL + c1]

        def evac_down(m, mi, ps_pair, ev_mm, mw, ri=ri):
            if ri == 0 and m not in ev_xt:
                load_x1(m, ev_y.get(m - 1))
            evs = []
            for jj in range(2):
                if ri == 0:
                    p.op("vector", lambda e, m=m, jj=jj, ps=ps_pair[jj]:
                         e.tensor_add(ysb[:, m * TL + jj * 512:m * TL + (jj + 1) * 512],
                                      ps[0:128, :],
                                      x1str[:, jj * 512:(jj + 1) * 512]),
                         waits=[ev_mm, ev_xt[m]], inc="yev")
                else:
                    p.op("vector", lambda e, m=m, jj=jj, ps=ps_pair[jj]:
                         e.tensor_add(ysb[:, m * TL + jj * 512:m * TL + (jj + 1) * 512],
                                      ysb[:, m * TL + jj * 512:m * TL + (jj + 1) * 512],
                                      ps[0:128, :]),
                         waits=[ev_mm, ev_y[m]], inc="yev")
                evs.append(p.now("yev"))
            ev_y[m] = evs[-1]
            return evs[-1]

        ws.job(q2_d, 43, list(range(NT)), rhs_h, ev_h, evac_down, wdtype=BF16,
               k_off=j0, k_count=j1 - j0, extra_strip_wait=ev_ag)
        if ri < len(rounds) - 1:
            p.op("vector", lambda e: e.memset(hsb[:, 10 * TL:11 * TL], 0.0),
                 waits=[p.now(ws.stm)], inc="hi")
            ev_hz[0] = p.now("hi")

    def post_tile(i, done_evt):
        p.op("gpsimd", lambda e, i=i: e.dma_start(
            out=x2_o.ap()[i * 128:(i + 1) * 128, :],
            in_=dst[:, 0:TL]),
            waits=[done_evt], inc="od0", inc_by=16)
        return p.now("od0")

    emit_ln(L, "ln", lambda i: ysb[:, i * TL:(i + 1) * TL],
            lambda i: ev_y[i],
            lambda i: dst[:, 0:TL],
            post_tile=post_tile, dst_lag=1)
    p.op("gpsimd", lambda e: e.engine_nop(),
         waits=[p.now("od0")], inc="fin")
    return L


def build_l1attn(L=None, io=None):
    L = L or Launch("l1attn")
    p = L.p
    x2_i = L.io(io, "x2", [D, TL], F32)
    WqT_i = L.io(io, "WqT", [D, D], F32)
    WkT_i = L.io(io, "WkT", [D, D], F32)
    WvT_i = L.io(io, "WvT", [D, D], F32)
    WoT_i = L.io(io, "WoT", [D, D], F32)
    tri_i = L.io(io, "tri01", [128, 128], BF16)
    negsib_i = L.io(io, "negsib", [128, 1], F32)
    x3_o = L.io(io, "x3", [D, TL], F32, out=True)

    q_d = L.dint("q_d", [D, TL], BF16)
    ksnd_d = L.dint("ksnd_d", [D, TL], BF16)
    vsnd_d = L.dint("vsnd_d", [TL, D], BF16)
    kag_d = L.dint("kag_d", [2 * D, TL], BF16)
    vag_d = L.dint("vag_d", [2 * TL, D], BF16)

    RSC = float(1.0 / np.sqrt(128.0))
    ISQ = float(1.0 / np.sqrt(128.0))

    tri = L.sbuf("tri", [128, 128], BF16)
    negsib = L.sbuf("negsib_sb", [128, 1], F32)
    ones = L.sbuf("ones_bf", [128, 2], BF16)
    onesr = L.sbuf("onesr_f32", [2, 128], F32)
    zsb = L.sbuf("zsb", [128, NT * TL], BF16)
    xbf = L.sbuf("xbf", [128, NT * TL], BF16)
    xf = L.sbuf("xf", [128, 2 * TL], F32)
    stg = L.sbuf("stg", [128, 2 * TL], BF16)

    p.op("vector", lambda e: e.memset(ones[:], 1.0), inc="ones_init")
    p.op("vector", lambda e: e.memset(onesr[:], 1.0), inc="ones_init")
    ev_ones = p.now("ones_init")
    p.op("sync", lambda e: e.dma_start(out=tri[:], in_=tri_i.ap()), inc="cd0",
         inc_by=16)
    p.op("sync", lambda e: e.dma_start(out=negsib[:], in_=negsib_i.ap()),
         inc="cd0", inc_by=16)
    ev_cst = p.now("cd0")

    for i in range(NT):
        par = i % 2
        p.op("scalar", lambda e, i=i, par=par: e.dma_start(
            out=xf[:, par * TL:(par + 1) * TL],
            in_=x2_i.ap()[i * 128:(i + 1) * 128, :]),
            waits=[("xc", i - 1)] if i >= 2 else [],
            inc=f"xd{par}", inc_by=16)
        p.op("vector", lambda e, i=i, par=par: e.tensor_copy(
            xbf[:, i * TL:(i + 1) * TL], xf[:, par * TL:(par + 1) * TL]),
            waits=[(f"xd{par}", 16 * ((i // 2) + 1))], inc="xc")
    ev_xbf = p.now("xc")

    def rhs_x(k, c0, c1):
        return xbf[:, k * TL + c0:k * TL + c1]

    ws = WStream(L, "qk", NT)
    sd_cnt = [0, 0]

    def mk_evac(dram):
        def evac(m, mi, ps_pair, ev_mm, mw):
            par = m % 2
            evs = []
            for j in range(2):
                p.op("vector", lambda e, ps=ps_pair[j], par=par, j=j:
                     e.tensor_copy(stg[:, par * TL + j * 512:par * TL + (j + 1) * 512],
                                   ps[:]),
                     waits=[ev_mm,
                            (f"qko{par}", 16 * sd_cnt[par]) if sd_cnt[par] else None],
                     inc="qkev")
                evs.append(p.now("qkev"))
            p.op("gpsimd", lambda e, m=m, par=par: e.dma_start(
                out=dram.ap()[m * 128:(m + 1) * 128, :],
                in_=stg[:, par * TL:(par + 1) * TL]),
                waits=evs, inc=f"qko{par}", inc_by=16)
            sd_cnt[par] += 1
            return p.now(f"qko{par}")
        return evac

    ws.job(WqT_i, NT, list(range(NT)), rhs_x, ev_xbf, mk_evac(q_d))
    ws.job(WkT_i, NT, list(range(NT)), rhs_x, ev_xbf, mk_evac(ksnd_d))
    ev_k_done = [p.now("qko0"), p.now("qko1")]

    p.op("gpsimd", lambda e: e.collective_compute(
        "AllGather", OP.bypass, replica_groups=[[2 * i, 2 * i + 1] for i in range(4)],
        ins=[ksnd_d.ap().opt()], outs=[kag_d.ap().opt()]),
        waits=ev_k_done, inc="cc")

    ev_qk_ev = p.now("qkev")
    ev_qk_mm = p.now("qk_m")
    ws.free()
    L.sbuf_free("stg")
    L.sbuf_free("xf")
    free_xbf = [False]
    wvf = L.sbuf("wvf", [128, 2 * NT * 512], F32)
    wvb = L.sbuf("wvb", [128, 2 * NT * 512], BF16)
    ps_v = [L.psum("ps_v0", [128, 512]), L.psum("ps_v1", [128, 512])]
    vstg = L.sbuf("vstg", [128, 2 * 512], BF16)
    vmm = 0
    vod = [0, 0]
    for ch in range(4):
        par = ch % 2
        sl = par * NT * 512
        p.op("sync", lambda e, ch=ch, sl=sl: e.dma_start(
            out=wvf[:, sl:sl + NT * 512].rearrange("p (k f) -> p k f", f=512),
            in_=WvT_i.ap()[:, ch * 512:(ch + 1) * 512]
            .rearrange("(k p) f -> p k f", p=128)),
            waits=([("wvc", ch - 1)] if ch >= 2 else []) + [ev_qk_mm] + ev_k_done,
            inc=f"wvd{par}", inc_by=16)
        p.op("vector", lambda e, sl=sl: e.tensor_copy(wvb[:, sl:sl + NT * 512],
                                                      wvf[:, sl:sl + NT * 512]),
             waits=[(f"wvd{par}", 16 * (ch // 2 + 1)),
                    ("vmm", max(0, vmm - 8 * NT))], inc="wvc")
        ev_w = p.now("wvc")
        for tt in range(8):
            pp = ps_v[tt % 2]
            for kc in range(NT):
                p.op("tensor", lambda e, sl=sl, tt=tt, kc=kc, pp=pp: e.matmul(
                    pp[:], xbf[:, kc * TL + tt * 128:kc * TL + (tt + 1) * 128],
                    wvb[:, sl + kc * 512:sl + (kc + 1) * 512],
                    start=(kc == 0), stop=(kc == NT - 1)),
                    waits=[ev_w, ev_xbf, ev_qk_ev,
                           ("vev", max(0, (ch * 8 + tt - 2) + 1)) if (ch * 8 + tt) >= 2 else None]
                    if kc == 0 else [ev_w],
                    inc="vmm")
                vmm += 1
            ev_vmm = p.now("vmm")
            vpar = tt % 2
            p.op("vector", lambda e, pp=pp, vpar=vpar: e.tensor_copy(
                vstg[:, vpar * 512:(vpar + 1) * 512], pp[:]),
                waits=[ev_vmm,
                       (f"vod{vpar}", 16 * vod[vpar]) if vod[vpar] else None],
                inc="vev")
            p.op("gpsimd", lambda e, ch=ch, tt=tt, vpar=vpar: e.dma_start(
                out=vsnd_d.ap()[tt * 128:(tt + 1) * 128, ch * 512:(ch + 1) * 512],
                in_=vstg[:, vpar * 512:(vpar + 1) * 512]),
                waits=[p.now("vev")], inc=f"vod{vpar}", inc_by=16)
            vod[vpar] += 1
    ev_v_done = [p.now("vod0"), p.now("vod1")]
    p.op("gpsimd", lambda e: e.collective_compute(
        "AllGather", OP.bypass, replica_groups=[[2 * i, 2 * i + 1] for i in range(4)],
        ins=[vsnd_d.ap().opt()], outs=[vag_d.ap().opt()]),
        waits=ev_v_done + [p.now("cc")], inc="cc")
    ev_ag = p.now("cc")

    L.psum_free("ps_v1")
    L.psum_free("ps_v0")
    L.sbuf_free("vstg")
    L.sbuf_free("wvb")
    L.sbuf_free("wvf")
    L.sbuf_free("xbf")

    ksb = L.sbuf("ksb", [128, 2 * 2048], BF16)
    vsb = L.sbuf("vsb", [128, 2 * 2048], BF16)
    qh = L.sbuf("qh", [128, 2 * TL], BF16)
    q2b = L.sbuf("q2b", [128, TL], BF16)
    k2b = L.sbuf("k2b", [128, 2048], BF16)
    mrow = L.sbuf("mrow", [1, 4 * TL], F32)
    mrbf = L.sbuf("mrbf", [1, 2 * TL], BF16)
    mhb = L.sbuf("mhb", [128, 2 * TL], F32)
    scf = L.sbuf("scf", [128, TL], F32)
    pb = L.sbuf("pb", [128, 3 * TL], BF16)
    linvb = L.sbuf("linvb", [128, TL], F32)
    ps_s = [L.psum("ps_s0", [128, 512]), L.psum("ps_s1", [128, 512])]
    ps_z = L.psum("ps_z", [128, TL])
    ps_l = L.psum("ps_l", [64, 512])
    ps_l2 = L.psum("ps_l2", [64, 512])
    ps_bh = L.psum("ps_bh", [128, TL])

    kld = [0, 0]
    z_evt = {}
    for h in range(NH):
        hp = h % 2
        ko, vo, qo = hp * 2048, hp * 2048, hp * TL
        war = z_evt.get(h - 2)
        p.op("sync", lambda e, h=h, ko=ko: e.dma_start(
            out=ksb[:, ko:ko + 1024], in_=kag_d.ap()[h * 128:(h + 1) * 128, :]),
            waits=[ev_ag, war], inc=f"kld{hp}", inc_by=16)
        p.op("sync", lambda e, h=h, ko=ko: e.dma_start(
            out=ksb[:, ko + 1024:ko + 2048],
            in_=ksnd_d.ap()[h * 128:(h + 1) * 128, :]),
            waits=[ev_ag, war], inc=f"kld{hp}", inc_by=16)
        p.op("sync", lambda e, h=h, vo=vo: e.dma_start(
            out=vsb[:, vo:vo + 1024].rearrange("p (s f) -> p s f", f=128),
            in_=vag_d.ap()[0:1024, h * 128:(h + 1) * 128]
            .rearrange("(s p) f -> p s f", p=128)),
            waits=[ev_ag, war], inc=f"kld{hp}", inc_by=16)
        p.op("sync", lambda e, h=h, vo=vo: e.dma_start(
            out=vsb[:, vo + 1024:vo + 2048].rearrange("p (s f) -> p s f", f=128),
            in_=vsnd_d.ap()[:, h * 128:(h + 1) * 128]
            .rearrange("(s p) f -> p s f", p=128)),
            waits=[ev_ag, war], inc=f"kld{hp}", inc_by=16)
        p.op("sync", lambda e, h=h, qo=qo: e.dma_start(
            out=qh[:, qo:qo + TL], in_=q_d.ap()[h * 128:(h + 1) * 128, :]),
            waits=[ev_ag, war], inc=f"kld{hp}", inc_by=16)
        kld[hp] += 5
        ev_ld = (f"kld{hp}", 16 * kld[hp])

        p.op("scalar", lambda e, qo=qo: e.activation(q2b[:], qh[:, qo:qo + TL],
                                                     AF.Square),
             waits=[ev_ld, ("mh", p.cnt("mh"))], inc="mh")
        p.op("scalar", lambda e, ko=ko: e.activation(k2b[:], ksb[:, ko:ko + 2048],
                                                     AF.Square),
             waits=[ev_ld, p.now("mh")], inc="mh")
        ev_sq = p.now("mh")
        for j in range(2):
            p.op("tensor", lambda e, j=j: e.matmul(
                ps_l[0:1, :], ones[:, 0:1], q2b[:, j * 512:(j + 1) * 512],
                start=True, stop=True), waits=[ev_sq, ev_ones, p.now("mhv")],
                inc="mhm")
            p.op("vector", lambda e, j=j: e.tensor_copy(
                mrow[0:1, j * 512:(j + 1) * 512], ps_l[0:1, :]),
                waits=[p.now("mhm")], inc="mhv")
        for j in range(4):
            p.op("tensor", lambda e, j=j: e.matmul(
                ps_l2[0:1, :], ones[:, 0:1], k2b[:, j * 512:(j + 1) * 512],
                start=True, stop=True), waits=[ev_sq, ev_ones, p.now("mhv")],
                inc="mhm")
            p.op("vector", lambda e, j=j: e.tensor_copy(
                mrow[0:1, TL + j * 512:TL + (j + 1) * 512], ps_l2[0:1, :]),
                waits=[p.now("mhm")], inc="mhv")
        p.op("vector", lambda e: e.reduce_max(mrow[0:1, 2 * TL:2 * TL + 1],
                                              mrow[0:1, TL:2 * TL], axis=AX),
             waits=[p.now("mhv")], inc="mhv")
        p.op("scalar", lambda e: e.activation(mrow[0:1, 2 * TL + 1:2 * TL + 2],
                                              mrow[0:1, 2 * TL:2 * TL + 1],
                                              AF.Sqrt),
             waits=[p.now("mhv")], inc="mh")
        p.op("scalar", lambda e: e.activation(mrow[0:1, 0:TL], mrow[0:1, 0:TL],
                                              AF.Sqrt),
             waits=[p.now("mh"), p.now("mhv")], inc="mh")
        p.op("vector", lambda e: e.tensor_scalar(
            mrow[0:1, 0:TL], mrow[0:1, 0:TL], mrow[0:1, 2 * TL + 1:2 * TL + 2],
            ISQ, OP.mult, OP.mult),
            waits=[p.now("mh")], inc="mhv")
        p.op("vector", lambda e: e.tensor_copy(mrbf[0:1, 0:TL], mrow[0:1, 0:TL]),
             waits=[p.now("mhv")], inc="mhv")
        ev_mh = p.now("mhv")
        for j in range(2):
            p.op("tensor", lambda e, j=j: e.matmul(
                ps_bh[:, j * 512:(j + 1) * 512], ones[0:1, 0:1] if False else
                onesr[0:1, :].bitcast(F32) if False else onesr[0:1, :],
                mrow[0:1, j * 512:(j + 1) * 512], start=True, stop=True),
                waits=[ev_mh, ("sev", p.cnt("sev"))], inc="mhm")
        p.op("vector", lambda e: e.tensor_copy(mhb[:, 0:TL], ps_bh[:, 0:TL]),
             waits=[p.now("mhm")], inc="mhv")
        p.op("vector", lambda e: e.tensor_scalar(
            mhb[:, TL:2 * TL], mhb[:, 0:TL], negsib[:, 0:1], None, OP.subtract),
            waits=[p.now("mhv"), ev_cst], inc="mhv")
        ev_mhb = p.now("mhv")

        first = True
        pb_evt = {}
        for st in range(16):
            sib = (st < 8)
            q0 = 0 if sib else (st - 8) * 128
            nq = TL - q0
            pslot = (st % 3) * TL
            nchq = ceil_div(nq, 512)
            ev_sc = []
            for j in range(nchq):
                c0 = q0 + j * 512
                c1 = min(TL, c0 + 512)
                p.op("tensor", lambda e, st=st, ko=ko, qo=qo, c0=c0, c1=c1, j=j:
                     e.matmul(ps_s[j][0:128, 0:c1 - c0],
                              ksb[:, ko + st * 128:ko + (st + 1) * 128],
                              qh[:, qo + c0:qo + c1], start=True, stop=True),
                     waits=[ev_ld, ("sev", p.cnt("sev") - 2)
                            if p.cnt("sev") >= 2 else None],
                     inc="smm")
                ev_mm = p.now("smm")
                p.op("vector", lambda e, st=st, c0=c0, c1=c1, j=j, sib=sib:
                     e.scalar_tensor_tensor(
                         scf[:, c0:c1], ps_s[j][0:128, 0:c1 - c0], RSC,
                         mhb[:, (TL if sib else 0) + c0:(TL if sib else 0) + c1],
                         OP.mult, OP.subtract),
                     waits=[ev_mm, ev_mhb], inc="sev")
                ev_sc.append(p.now("sev"))
            if q0 > 0:
                p.op("vector", lambda e, q0=q0, pslot=pslot: e.memset(
                    pb[:, pslot:pslot + q0], 0.0),
                    waits=[pb_evt.get(st - 3)], inc="pex")
            p.op("scalar", lambda e, q0=q0, nq=nq, pslot=pslot: e.activation(
                pb[:, pslot + q0:pslot + TL], scf[:, q0:TL], AF.Exp),
                waits=ev_sc + [pb_evt.get(st - 3), p.now("pex")], inc="pex")
            ev_p = p.now("pex")
            if not sib:
                p.op("vector", lambda e, pslot=pslot, q0=q0: e.tensor_mul(
                    pb[:, pslot + q0:pslot + q0 + 128],
                    pb[:, pslot + q0:pslot + q0 + 128], tri[:]),
                    waits=[ev_p, ev_cst], inc="pex")
                ev_p = p.now("pex")
            segs = [(0, 512), (512, TL)]
            for (c0, c1) in segs:
                psl = ps_l if c0 < 512 else ps_l2
                cb = c0 % 512
                p.op("tensor", lambda e, c0=c0, c1=c1, pslot=pslot, psl=psl, cb=cb,
                     s0=first, s1=(st == 15):
                     e.matmul(psl[0:1, cb:cb + c1 - c0], ones[:, 0:1],
                              pb[:, pslot + c0:pslot + c1],
                              start=s0, stop=s1),
                     waits=[ev_p], inc="smm")
            for (c0, c1) in segs:
                p.op("tensor", lambda e, st=st, vo=vo, c0=c0, c1=c1, pslot=pslot,
                     s0=first, s1=(st == 15):
                     e.matmul(ps_z[:, c0:c1],
                              vsb[:, vo + st * 128:vo + (st + 1) * 128],
                              pb[:, pslot + c0:pslot + c1],
                              start=s0, stop=s1),
                     waits=[ev_p], inc="smm")
            pb_evt[st] = p.now("smm")
            first = False
        ev_att = p.now("smm")
        for j in range(2):
            psl = ps_l if j == 0 else ps_l2
            p.op("vector", lambda e, j=j, psl=psl: e.tensor_copy(
                mrow[0:1, TL + j * 512:TL + (j + 1) * 512], psl[0:1, :]),
                waits=[ev_att], inc="mhv")
        p.op("vector", lambda e: e.reciprocal(mrow[0:1, 3 * TL:4 * TL],
                                              mrow[0:1, TL:2 * TL]),
             waits=[p.now("mhv")], inc="mhv")
        ev_li = p.now("mhv")
        for j in range(2):
            p.op("tensor", lambda e, j=j: e.matmul(
                ps_bh[:, j * 512:(j + 1) * 512], onesr[0:1, :],
                mrow[0:1, 3 * TL + j * 512:3 * TL + (j + 1) * 512],
                start=True, stop=True),
                waits=[ev_li], inc="mhm")
        p.op("vector", lambda e: e.tensor_copy(linvb[:, 0:TL], ps_bh[:, 0:TL]),
             waits=[p.now("mhm"), z_evt.get(h - 1)], inc="mhv")
        ev_lb = p.now("mhv")
        for j in range(2):
            p.op("vector", lambda e, h=h, j=j: e.scalar_tensor_tensor(
                zsb[:, h * TL + j * 512:h * TL + (j + 1) * 512],
                ps_z[:, j * 512:(j + 1) * 512], 1.0,
                linvb[:, j * 512:(j + 1) * 512],
                OP.mult, OP.mult),
                waits=[ev_att, ev_lb], inc="zev")
        z_evt[h] = p.now("zev")
    ev_z = p.now("zev")

    for nm in ["ps_bh", "ps_l2", "ps_l", "ps_z", "ps_s1", "ps_s0"]:
        L.psum_free(nm)
    for nm in ["linvb", "pb", "scf", "mhb", "mrbf", "mrow", "k2b", "q2b",
               "qh", "vsb", "ksb"]:
        L.sbuf_free(nm)

    ysb = L.sbuf("ysb", [128, NT * TL], F32)
    x2str = L.sbuf("x2str", [128, TL], F32)
    dst = L.sbuf("dst", [128, TL], F32)

    def rhs_z(k, c0, c1):
        return zsb[:, k * TL + c0:k * TL + c1]

    ev_y = {}
    xld2 = [0, 0]
    ev_xt = {}

    def load_x2(m, war_evt):
        p.op("scalar", lambda e, m=m: e.dma_start(
            out=x2str[:, 0:TL],
            in_=x2_i.ap()[m * 128:(m + 1) * 128, :]),
            waits=[war_evt or ev_z], inc="x2l0", inc_by=16)
        xld2[0] += 1
        ev_xt[m] = ("x2l0", 16 * xld2[0])

    def evac_o(m, mi, ps_pair, ev_mm, mw):
        if m not in ev_xt:
            load_x2(m, ev_y.get(m - 1))
        evs = []
        for j in range(2):
            p.op("vector", lambda e, m=m, j=j, ps=ps_pair[j]: e.tensor_add(
                ysb[:, m * TL + j * 512:m * TL + (j + 1) * 512], ps[:],
                x2str[:, j * 512:(j + 1) * 512]),
                waits=[ev_mm, ev_xt[m]], inc="yev")
            evs.append(p.now("yev"))
        ev_y[m] = evs[-1]
        return evs[-1]

    ws2 = WStream(L, "wo", NT)
    ws2.job(WoT_i, NT, list(range(NT)), rhs_z, ev_z, evac_o)

    def post_tile(i, done_evt):
        p.op("gpsimd", lambda e, i=i: e.dma_start(
            out=x3_o.ap()[i * 128:(i + 1) * 128, :],
            in_=dst[:, 0:TL]),
            waits=[done_evt], inc="od0", inc_by=16)
        return p.now("od0")

    emit_ln(L, "ln", lambda i: ysb[:, i * TL:(i + 1) * TL],
            lambda i: ev_y[i],
            lambda i: dst[:, 0:TL],
            post_tile=post_tile, dst_lag=1)
    p.op("gpsimd", lambda e: e.engine_nop(),
         waits=[p.now("od0")], inc="fin")
    return L


def build_ffn1(L=None, io=None, pre_out=None, pre=None):
    L = L or Launch("ffn1")
    p = L.p
    x3_i = L.io(io, "x3", [D, TL], F32)
    gup_i = L.io(io, "gupT", [256, GUP], I32)
    gus_i = L.io(io, "gusE", [256, GUP], F16)
    dnp_i = L.io(io, "dnpT", [704, D], I32)
    dns_i = L.io(io, "dnsE", [704, D], F16)
    nmk_i = L.io(io, "nibmask", [128, 1], I32)
    niv_i = L.io(io, "nibinv", [128, 1], F32)
    y_o = L.io(io, "y", [D, TL], F32, out=True)

    qgush_d = L.dint("qgush_d", [256, GUP], BF16)
    qdnsh_d = L.dint("qdnsh_d", [704, D], BF16)
    qgu_d = L.dint("qgu_d", [D, GUP], BF16)
    qdn_d = L.dint("qdn_d", [I1, D], BF16)

    if pre is not None:
        pre["free"]()
        xbf = L.sbuf("xbf", [128, NT * TL], BF16)
        pf = L.sbuf("pfm", [128, 2 * TL], F32)
        for i in range(NT):
            par = i % 2
            p.op("scalar", lambda e, i=i, par=par: e.dma_start(
                out=pf[:, par * TL:(par + 1) * TL],
                in_=x3_i.ap()[i * 128:(i + 1) * 128, :]),
                waits=([("xc", i - 1)] if i >= 2 else []),
                inc=f"xd{par}", inc_by=16)
            p.op("vector", lambda e, i=i, par=par: e.tensor_copy(
                xbf[:, i * TL:(i + 1) * TL], pf[:, par * TL:(par + 1) * TL]),
                waits=[p.now(f"xd{par}")], inc="xc")
        ev_xbf = p.now("xc")
        L.sbuf_free("pfm")
        return _ffn1_main(L, io, x3_i, y_o, xbf, ev_xbf, pre["ev_ag"],
                          pre["qgu_d"], pre["qdn_d"])
    if pre_out is None:
        xbf = L.sbuf("xbf", [128, NT * TL], BF16)
    nmk = L.sbuf("nmk", [128, 1], I32)
    niv = L.sbuf("niv", [128, 1], F32)
    p.op("sync", lambda e: e.dma_start(out=nmk[:], in_=nmk_i.ap()), inc="nd",
         inc_by=16)
    p.op("sync", lambda e: e.dma_start(out=niv[:], in_=niv_i.ap()), inc="nd",
         inc_by=16)
    ev_nsh = p.now("nd")

    PW = 704
    pw = L.sbuf("pw", [128, 2 * PW], I32)
    psc = L.sbuf("psc", [128, 2 * PW], F16)
    ib1 = L.sbuf("ib1", [128, PW], I32)
    ib2 = L.sbuf("ib2", [128, PW], I32)
    fA = L.sbuf("fA", [128, PW], F32)
    fB = L.sbuf("fB", [128, PW], F32)
    fC = L.sbuf("fC", [128, PW], F32)
    qo = L.sbuf("qo", [128, 2 * PW], BF16)

    pieces = []
    for t in range(2):
        for cp in range(16):
            pieces.append((gup_i, gus_i, qgush_d, t * 128, 128, cp * PW, PW))
    for t in range(6):
        nr = min(128, 704 - t * 128)
        for cp in range(4):
            pieces.append((dnp_i, dns_i, qdnsh_d, t * 128, nr, cp * 512, 512))
    qdo = {}
    for pi, (wd, sd, qd, r0, nr, c0, cols) in enumerate(pieces):
        par = pi % 2
        sl = par * PW
        war = [("dqv", 14 * (pi - 1)) if pi >= 2 else None, qdo.get(pi - 2)]
        p.op("sync", lambda e, wd=wd, r0=r0, nr=nr, c0=c0, cols=cols, sl=sl:
             e.dma_start(out=pw[0:nr, sl:sl + cols],
                         in_=wd.ap()[r0:r0 + nr, c0:c0 + cols]),
             waits=war, inc=f"dqd{par}", inc_by=16)
        p.op("sync", lambda e, sd=sd, r0=r0, nr=nr, c0=c0, cols=cols, sl=sl:
             e.dma_start(out=psc[0:nr, sl:sl + cols],
                         in_=sd.ap()[r0:r0 + nr, c0:c0 + cols]),
             waits=war, inc=f"dqd{par}", inc_by=16)
        nd = p.cnt(f"dqd{par}")
        ev_in = (f"dqd{par}", nd)
        W = [ev_in, ev_nsh]
        p.op("vector", lambda e, nr=nr, sl=sl, cols=cols: e.tensor_scalar(
            ib1[0:nr, 0:cols], pw[0:nr, sl:sl + cols], nmk[0:nr, 0:1], None,
            OP.bitwise_and),
            waits=W + [("dqv", 14 * pi)], inc="dqv")
        p.op("vector", lambda e, nr=nr, cols=cols: e.tensor_copy(
            fA[0:nr, 0:cols], ib1[0:nr, 0:cols]),
            waits=[p.now("dqv")], inc="dqv")
        p.op("vector", lambda e, nr=nr, cols=cols: e.tensor_scalar(
            fA[0:nr, 0:cols], fA[0:nr, 0:cols], niv[0:nr, 0:1], None, OP.mult),
            waits=[p.now("dqv")], inc="dqv")
        p.op("vector", lambda e, nr=nr, cols=cols: e.tensor_scalar(
            fB[0:nr, 0:cols], fA[0:nr, 0:cols], 8.0, None, OP.is_ge),
            waits=[p.now("dqv")], inc="dqv")
        p.op("vector", lambda e, nr=nr, cols=cols: e.scalar_tensor_tensor(
            fA[0:nr, 0:cols], fB[0:nr, 0:cols], -8.0, fA[0:nr, 0:cols],
            OP.mult, OP.add),
            waits=[p.now("dqv")], inc="dqv")
        p.op("vector", lambda e, nr=nr, cols=cols: e.tensor_scalar(
            fA[0:nr, 0:cols], fA[0:nr, 0:cols], 4194304.0, 1056964608.0,
            OP.mult, OP.add),
            waits=[p.now("dqv")], inc="dqv")
        p.op("vector", lambda e, nr=nr, cols=cols: e.tensor_copy(
            ib2[0:nr, 0:cols], fA[0:nr, 0:cols]),
            waits=[p.now("dqv")], inc="dqv")
        p.op("vector", lambda e, nr=nr, cols=cols: e.tensor_scalar(
            fC[0:nr, 0:cols], ib2[0:nr, 0:cols].bitcast(F32), 1.0, None, OP.is_lt),
            waits=[p.now("dqv")], inc="dqv")
        p.op("vector", lambda e, nr=nr, cols=cols: e.scalar_tensor_tensor(
            fA[0:nr, 0:cols], fC[0:nr, 0:cols], 1.0, ib2[0:nr, 0:cols].bitcast(F32),
            OP.add, OP.mult),
            waits=[p.now("dqv")], inc="dqv")
        p.op("vector", lambda e, nr=nr, cols=cols: e.tensor_sub(
            fA[0:nr, 0:cols], fA[0:nr, 0:cols], fC[0:nr, 0:cols]),
            waits=[p.now("dqv")], inc="dqv")
        p.op("vector", lambda e, nr=nr, cols=cols: e.tensor_scalar(
            fB[0:nr, 0:cols], fB[0:nr, 0:cols], -2.0, 1.0, OP.mult, OP.add),
            waits=[p.now("dqv")], inc="dqv")
        p.op("vector", lambda e, nr=nr, cols=cols: e.tensor_mul(
            fA[0:nr, 0:cols], fA[0:nr, 0:cols], fB[0:nr, 0:cols]),
            waits=[p.now("dqv")], inc="dqv")
        p.op("vector", lambda e, nr=nr, cols=cols, sl=sl: e.tensor_copy(
            fC[0:nr, 0:cols], psc[0:nr, sl:sl + cols]),
            waits=[p.now("dqv")], inc="dqv")
        p.op("vector", lambda e, nr=nr, cols=cols, sl=sl: e.tensor_mul(
            qo[0:nr, sl:sl + cols], fA[0:nr, 0:cols], fC[0:nr, 0:cols]),
            waits=[p.now("dqv")], inc="dqv")
        p.op("gpsimd", lambda e, qd=qd, r0=r0, nr=nr, c0=c0, cols=cols, sl=sl:
             e.dma_start(out=qd.ap()[r0:r0 + nr, c0:c0 + cols],
                         in_=qo[0:nr, sl:sl + cols]),
             waits=[p.now("dqv")], inc=f"dqo{par}", inc_by=16)
        qdo[pi] = p.now(f"dqo{par}")
    ev_dq = [p.now("dqo0"), p.now("dqo1")]
    p.op("gpsimd", lambda e: e.collective_compute(
        "AllGather", OP.bypass, replica_groups=[list(range(NCORES))],
        ins=[qgush_d.ap().opt()], outs=[qgu_d.ap().opt()]),
        waits=ev_dq, inc="cc")
    p.op("gpsimd", lambda e: e.collective_compute(
        "AllGather", OP.bypass, replica_groups=[list(range(NCORES))],
        ins=[qdnsh_d.ap().opt()], outs=[qdn_d.ap().opt()]),
        waits=[p.now("cc")], inc="cc")
    ev_ag = p.now("cc")
    if pre_out is not None:
        pnames = [L.ns + nm for nm in
                  ["qo", "fC", "fB", "fA", "ib2", "ib1", "psc", "pw"]]
        pre_out.update(dict(ev_ag=ev_ag, qgu_d=qgu_d, qdn_d=qdn_d,
                            ev_q=ev_dq + [p.now("dqv")],
                            free=lambda: [L.sbuf_free_abs(nm) for nm in pnames]))
        return L

    ev_q_all = ev_dq + [p.now("dqv")]
    for i in range(NT):
        par = i % 2
        p.op("scalar", lambda e, i=i, par=par: e.dma_start(
            out=fA[:, 0:TL] if par == 0 else fC[:, 0:TL],
            in_=x3_i.ap()[i * 128:(i + 1) * 128, :]),
            waits=ev_q_all + ([("xc", i - 1)] if i >= 2 else []),
            inc=f"xd{par}", inc_by=16)
        p.op("vector", lambda e, i=i, par=par: e.tensor_copy(
            xbf[:, i * TL:(i + 1) * TL], fA[:, 0:TL] if par == 0 else fC[:, 0:TL]),
            waits=[(f"xd{par}", 16 * (i // 2 + 1))], inc="xc")
    ev_xbf = p.now("xc")

    for nm in ["qo", "fC", "fB", "fA", "ib2", "ib1", "psc", "pw"]:
        L.sbuf_free(nm)

    return _ffn1_main(L, io, x3_i, y_o, xbf, ev_xbf, ev_ag, qgu_d, qdn_d)


def _ffn1_main(L, io, x3_i, y_o, xbf, ev_xbf, ev_ag, qgu_d, qdn_d):
    p = L.p
    hsb = L.sbuf("hsb", [128, 22 * TL], BF16)
    ysb = L.sbuf("ysb", [128, NT * TL], F32)
    hA = L.sbuf("hA", [128, TL], F32)
    x3str = L.sbuf("x3str", [128, 2 * TL], F32)
    dst = L.sbuf("dst", [128, 2 * TL], F32)

    ws = WStream(L, "f1", 22, need_f32=False)

    def rhs_x(k, c0, c1):
        return xbf[:, k * TL + c0:k * TL + c1]

    rounds = [(0, 22), (22, 44)]
    ev_y = {}

    for ri, (j0, j1) in enumerate(rounds):
        for j in range(j0, j1):
            slot = j - j0

            def evac_gate(m, mi, ps_pair, ev_mm, mw):
                evs = []
                for jj in range(2):
                    p.op("scalar", lambda e, jj=jj, ps=ps_pair[jj]:
                         e.activation(hA[:, jj * 512:(jj + 1) * 512],
                                      ps[:], AF.Silu),
                         waits=[ev_mm, p.now("hi")], inc="hs")
                    evs.append(p.now("hs"))
                return evs[-1]

            ws.job(qgu_d, NT, [j], rhs_x, ev_xbf, evac_gate, wdtype=BF16,
                   extra_strip_wait=ev_ag)
            ev_g = p.now("hs")

            def evac_up(m, mi, ps_pair, ev_mm, mw, slot=slot, ev_g=ev_g):
                evs = []
                for jj in range(2):
                    p.op("vector", lambda e, jj=jj, ps=ps_pair[jj], slot=slot:
                         e.tensor_mul(hsb[:, slot * TL + jj * 512:
                                          slot * TL + (jj + 1) * 512],
                                      hA[:, jj * 512:(jj + 1) * 512],
                                      ps[:]),
                         waits=[ev_mm, ev_g], inc="hi")
                    evs.append(p.now("hi"))
                return evs[-1]

            ws.job(qgu_d, NT, [44 + j], rhs_x, ev_xbf, evac_up, wdtype=BF16,
                   extra_strip_wait=ev_ag)
        ev_h = p.now("hi")

        def rhs_h(k, c0, c1):
            return hsb[:, k * TL + c0:k * TL + c1]

        def evac_down(m, mi, ps_pair, ev_mm, mw, ri=ri):
            evs = []
            for jj in range(2):
                if ri == 0:
                    p.op("vector", lambda e, m=m, jj=jj, ps=ps_pair[jj]:
                         e.tensor_copy(ysb[:, m * TL + jj * 512:m * TL + (jj + 1) * 512],
                                       ps[:]),
                         waits=[ev_mm], inc="yev")
                else:
                    p.op("vector", lambda e, m=m, jj=jj, ps=ps_pair[jj]:
                         e.tensor_add(ysb[:, m * TL + jj * 512:m * TL + (jj + 1) * 512],
                                      ysb[:, m * TL + jj * 512:m * TL + (jj + 1) * 512],
                                      ps[:]),
                         waits=[ev_mm, ev_y[m]], inc="yev")
                evs.append(p.now("yev"))
            ev_y[m] = evs[-1]
            return evs[-1]

        ws.job(qdn_d, 44, list(range(NT)), rhs_h, ev_h, evac_down, wdtype=BF16,
               k_off=j0, k_count=j1 - j0, extra_strip_wait=ev_ag)

    ev_rms = emit_ln(L, "rms", lambda i: ysb[:, i * TL:(i + 1) * TL],
                     lambda i: ev_y[i],
                     lambda i: ysb[:, i * TL:(i + 1) * TL],
                     rms=True, rms_eps=1e-6, alloc_tag="ln")
    xld = [0, 0]
    ev_res = {}
    for i in range(NT):
        par = i % 2
        p.op("scalar", lambda e, i=i, par=par: e.dma_start(
            out=x3str[:, par * TL:(par + 1) * TL],
            in_=x3_i.ap()[i * 128:(i + 1) * 128, :]),
            waits=[ev_res.get(i - 2)], inc=f"x3l{par}", inc_by=16)
        xld[par] += 1
        p.op("vector", lambda e, i=i, par=par: e.tensor_add(
            ysb[:, i * TL:(i + 1) * TL], ysb[:, i * TL:(i + 1) * TL],
            x3str[:, par * TL:(par + 1) * TL]),
            waits=[(f"x3l{par}", 16 * xld[par]), ev_rms[i]], inc="resv")
        ev_res[i] = p.now("resv")

    def post_tile(i, done_evt):
        par = i % 2
        p.op("gpsimd", lambda e, i=i, par=par: e.dma_start(
            out=y_o.ap()[i * 128:(i + 1) * 128, :],
            in_=dst[:, par * TL:(par + 1) * TL]),
            waits=[done_evt], inc=f"od{par}", inc_by=16)
        return p.now(f"od{par}")

    emit_ln(L, "ln", lambda i: ysb[:, i * TL:(i + 1) * TL],
            lambda i: ev_res[i],
            lambda i: dst[:, (i % 2) * TL:(i % 2) * TL + TL],
            post_tile=post_tile, alloc_tag="ln",
            prior_evt=("resv", NT))
    p.op("gpsimd", lambda e: e.engine_nop(),
         waits=[p.now("od0"), p.now("od1")], inc="fin")
    return L


def build_fused():
    L = Launch("mk")
    p = L.p
    io = {}
    io["xcat"] = L.din("xcat", [D, TL + 1], F32)
    io["WactT"] = L.din("WactT", [2 * D, D], F32)
    io["WrecT"] = L.din("WrecT", [2 * D, D], F32)
    io["WvalT"] = L.din("WvalT", [D, D], F32)
    io["pm"] = L.din("pm", [128, 1], F32)
    io["mbias"] = L.din("mbias", [128, 1], F32)
    io["w1sh"] = L.din("w1sh", [256, H0], F32)
    io["w3sh"] = L.din("w3sh", [256, H0], F32)
    io["w2sh"] = L.din("w2sh", [688, D], F32)
    io["WqT"] = L.din("WqT", [D, D], F32)
    io["WkT"] = L.din("WkT", [D, D], F32)
    io["WvT"] = L.din("WvT", [D, D], F32)
    io["WoT"] = L.din("WoT", [D, D], F32)
    io["tri01"] = L.din("tri01", [128, 128], BF16)
    io["negsib"] = L.din("negsib", [128, 1], F32)
    io["gupT"] = L.din("gupT", [256, GUP], I32)
    io["gusE"] = L.din("gusE", [256, GUP], F16)
    io["dnpT"] = L.din("dnpT", [704, D], I32)
    io["dnsE"] = L.din("dnsE", [704, D], F16)
    io["nibmask"] = L.din("nibmask", [128, 1], I32)
    io["nibinv"] = L.din("nibinv", [128, 1], F32)
    io["y"] = L.dout("y", [D, TL], F32)
    io["xv"] = L.dint("xv_d", [D, TL], F32)
    io["r4"] = L.dint("r4_d", [D, TL], BF16)
    io["kv"] = L.dint("kv_d", [D, TL], BF16)
    io["x1"] = L.dint("x1_d", [D, TL], F32)
    io["x2"] = L.dint("x2_d", [D, TL], F32)
    io["x3"] = L.dint("x3_d", [D, TL], F32)

    def phase(ns, barrier):
        L.ns = ns
        p.barrier_evts = barrier
        return p.snapshot(), L.mark()

    i0 = len(p.ops)
    s, m_pre0 = phase("F0p_", [])
    pre0 = {}
    build_ffn0(L, io, pre_out=pre0)
    ev_pre0 = p.events_since(s)
    names_pre0 = p.active_names()

    i1 = len(p.ops)
    s, m0 = phase("P0_", [])
    part2 = []
    build_l0a(L, io, part2=part2)
    ev_p0a = p.events_since(s)
    p.merge_even(i0, i1, len(p.ops))

    i2 = len(p.ops)
    s1, m1 = phase("P1_", ev_p0a)
    build_l0s(L, io)
    ev_p1 = p.events_since(s1)
    i3 = len(p.ops)
    L.ns = "P0_"
    p.barrier_evts = []
    s0b = p.snapshot()
    part2[0]()
    ev_p0b = p.events_since(s0b)
    p.merge_even(i2, i3, len(p.ops))

    L.free_to_mark(m1)
    L.free_to_mark(m0)

    s2, m2 = phase("P2_", ev_p1 + ev_p0b)
    build_l0v(L, io)
    ev_p2 = p.events_since(s2)
    names_early = p.active_names() - names_pre0
    L.free_to_mark(m2)

    i4 = len(p.ops)
    s, m_pre1 = phase("F1p_", ev_p2 + ev_pre0)
    pre0["free"]()
    pre1 = {}
    build_ffn1(L, io, pre_out=pre1)
    ev_pre1 = p.events_since(s)
    names_pre1 = p.active_names() - names_pre0 - names_early

    i5 = len(p.ops)
    s3, m3 = phase("P3_", ev_p2 + ev_pre0)
    p.retire_names(names_pre0 | names_early)
    build_ffn0(L, io, pre=pre0)
    ev_p3 = p.events_since(s3)
    L.free_to_mark(m3)
    p.merge_even(i4, i5, len(p.ops))

    s4, m4 = phase("P4_", ev_p3)
    build_l1attn(L, io)
    ev_p4 = p.events_since(s4)
    names_mid = (p.active_names() - names_pre1)
    L.free_to_mark(m4)

    s5, m5 = phase("P5_", ev_p4 + ev_pre1)
    p.retire_names(names_mid | names_pre1)
    build_ffn1(L, io, pre=pre1)
    L.ns = ""
    return L


_FUSED = [None]


def kernel(**inputs):
    common, per_core = stage_all(inputs)
    R = range(NCORES)
    w1sh = np.split(common["w1T"], NCORES, axis=0)
    w3sh = np.split(common["w3T"], NCORES, axis=0)
    w2sh = np.split(common["w2T"], NCORES, axis=0)

    if _FUSED[0] is None:
        _FUSED[0] = build_fused()
        _FUSED[0].p.emit()
    L = _FUSED[0]
    in_maps = []
    for c in R:
        pc = per_core[c]
        in_maps.append(dict(
            xcat=pc["xcat"], WactT=common["WactT"], WrecT=common["WrecT"],
            WvalT=common["WvalT"], pm=pc["pm"], mbias=pc["mbias"],
            w1sh=w1sh[c], w3sh=w3sh[c], w2sh=w2sh[c],
            WqT=common["WqT"], WkT=common["WkT"], WvT=common["WvT"],
            WoT=common["WoT"], tri01=common["tri01"], negsib=pc["negsib"],
            gupT=common["gupT"], gusE=common["gusE"], dnpT=common["dnpT"],
            dnsE=common["dnsE"], nibmask=pc["nibmask"], nibinv=pc["nibinv"],
        ))
    res = run_bass_kernel_spmd(L.nc, in_maps, core_ids=list(range(NCORES)))
    out = np.empty((B, T, D), np.float32)
    for c in R:
        b, h = c // 2, c % 2
        out[b, h * TL:(h + 1) * TL, :] = res.results[c]["y"].T[:, PI_INV]
    return out
